# version 22
# baseline (speedup 1.0000x reference)
import math
import numpy as np

import concourse.bass as bass
import concourse.mybir as mybir
from concourse.alu_op_type import AluOpType as aop
from concourse.bass_utils import run_bass_kernel_spmd

F32 = mybir.dt.float32
BF16 = mybir.dt.bfloat16
AF = mybir.ActivationFunctionType
AX = mybir.AxisListType

D = 1024
H = 16
DH = 64
DCH = D // 128
KI = 2816
NKB = KI // 128
WIN = 128
SEGL = 16
SPW = WIN // SEGL
C_SOFT = 12.0





QFOLD = True
SPSTORE = True
PREFETCH = True
TPAR = True
TDEPTH = 1


def build_program(B_CORE=2, N=4096, T=512, taps=()):
    NW = N // WIN
    WPM = T // WIN
    MPB = NW // WPM
    NSEG = N // SEGL
    LMT = (NSEG + 127) // 128
    TC = T // 128
    taps = set(taps)

    nc = bass.Bass("TRN2", target_bir_lowering=False, detect_race_conditions=False)

    last = {}
    drain_on = [True]

    class nodrain:
        def __enter__(self):
            drain_on.append(False)

        def __exit__(self, *a):
            drain_on.pop()

    class EngProxy:
        _SKIP = {"wait_ge", "sem_inc", "wait_op", "nop", "engine_nop", "drain",
                 "sem_clear"}
        _NODRAIN = {"dma_start", "dma_start_transpose"}

        def __init__(self, eng, name):
            self._eng = eng
            self._name = name

        def __getattr__(self, m):
            f = getattr(self._eng, m)
            if not callable(f) or m in self._SKIP:
                return f
            name = self._name
            eng = self._eng
            drainable = name in ("dve", "act") and m not in self._NODRAIN

            def wrap(*a, **k):
                r = f(*a, **k)
                if hasattr(r, "then_inc"):
                    last[name] = r
                    if drainable and a and isinstance(a[0], bass.AP):
                        out = a[0]
                        try:
                            free = 1
                            for s in out.shape[1:]:
                                free *= s
                            nbytes = free * mybir.dt.size(out.dtype)
                        except Exception:
                            nbytes = 0
                        if nbytes < 2048 and drain_on[-1]:
                            eng.drain()
                return r
            return wrap

    te = EngProxy(nc.tensor, "pe")
    ac = EngProxy(nc.scalar, "act")
    ve = EngProxy(nc.vector, "dve")
    po = EngProxy(nc.gpsimd, "pool")
    sp = EngProxy(nc.sync, "sp")

    x_in = nc.dram_tensor("x", [B_CORE, N, D], F32, kind="ExternalInput").ap()
    wq_in = nc.dram_tensor("wq", [D, D], F32, kind="ExternalInput").ap()
    wkv_in = nc.dram_tensor("wkv", [D, D], F32, kind="ExternalInput").ap()
    wproj_in = nc.dram_tensor("wproj", [DH], F32, kind="ExternalInput").ap()
    wout_in = nc.dram_tensor("wout", [D, D], F32, kind="ExternalInput").ap()
    bout_in = nc.dram_tensor("bout", [D], F32, kind="ExternalInput").ap()
    lng_in = nc.dram_tensor("lng", [4, DH], F32, kind="ExternalInput").ap()
    wg_in = nc.dram_tensor("wgate", [KI, D], F32, kind="ExternalInput").ap()
    wu_in = nc.dram_tensor("wup", [KI, D], F32, kind="ExternalInput").ap()
    wd_in = nc.dram_tensor("wdown", [D, KI], F32, kind="ExternalInput").ap()
    out_d = nc.dram_tensor("out", [B_CORE, N, D], F32, kind="ExternalOutput").ap()
    wqkv_d = nc.dram_tensor("wqkv_bf", [D, 2 * D], BF16).ap()
    woutb_d = nc.dram_tensor("wout_bf", [D, D], BF16).ap()
    wgq_d = nc.dram_tensor("wgq_t", [D, KI], BF16).ap()
    costab_d = nc.dram_tensor("costab_d", [128, N], BF16).ap()
    sintab_d = nc.dram_tensor("sintab_d", [128, N], BF16).ap()
    wuq_d = nc.dram_tensor("wuq_t", [D, KI], BF16).ap()
    wdq_d = nc.dram_tensor("wdq_t", [KI, D], BF16).ap()

    tap_d = {}
    def def_tap(name, shape, dt=F32):
        if name in taps:
            tap_d[name] = nc.dram_tensor("tap_" + name, list(shape), dt,
                                         kind="ExternalOutput").ap()
    def_tap("xtm", [B_CORE, MPB, 128, TC, D], dt=BF16)
    def_tap("xf", [B_CORE, MPB, 128, DCH, T], dt=BF16)
    def_tap("qpre", [B_CORE, MPB, 128, DCH, T], dt=BF16)
    def_tap("qrot", [B_CORE, MPB, 128, DCH, T], dt=BF16)
    def_tap("kvkm", [B_CORE, MPB, 128, WPM, H * DH], dt=BF16)
    def_tap("vhkm", [B_CORE, MPB, 128, WPM, H * DH], dt=BF16)
    def_tap("gkm", [B_CORE, MPB, 128, LMT, H * DH], dt=BF16)
    def_tap("vgkm", [B_CORE, MPB, 128, LMT, H * DH], dt=BF16)
    def_tap("pscore", [B_CORE, MPB, H, T], dt=F32)
    def_tap("attnf", [B_CORE, MPB, 128, DCH, T], dt=BF16)
    def_tap("x1", [B_CORE, MPB, 128, TC, D], dt=F32)
    def_tap("x1q", [B_CORE, MPB, TC, 128, D], dt=BF16)
    def_tap("htm", [B_CORE, MPB, 128, TC, KI], dt=BF16)
    def_tap("cost", [B_CORE, MPB, 128, T], dt=BF16)

    ctxs = []
    def mk(cm):
        h = cm.__enter__()
        ctxs.append(cm)
        return h

    done = {e: mk(nc.semaphore("dn_" + e)) for e in ("pe", "act", "dve", "pool", "sp")}
    cnt = {e: 0 for e in done}
    dsem = {s: [mk(nc.semaphore("dm_" + s)), 0] for s in ("din", "dw", "dout", "dm", "dmp")}

    def R(eng, instr):
        return instr

    ENG = {"pe": te, "act": ac, "dve": ve, "pool": po, "sp": sp}

    def fence(producers, consumers):
        for pe_ in producers:
            ins = last.get(pe_)
            assert ins is not None, f"fence: nothing emitted on {pe_}"
            ins.then_inc(done[pe_], 1)
            cnt[pe_] += 1
            last[pe_] = None
        for ce in consumers:
            for pe_ in producers:
                ENG[ce].wait_ge(done[pe_], cnt[pe_])

    def mark(eng):
        ins = last.get(eng)
        assert ins is not None, f"mark: nothing emitted on {eng}"
        ins.then_inc(done[eng], 1)
        cnt[eng] += 1
        last[eng] = None
        return cnt[eng]

    def waiton(consumer, prod, count):
        if count > 0:
            ENG[consumer].wait_ge(done[prod], count)

    def dma(eng, s, out, in_):
        ENG[eng].dma_start(out=out, in_=in_).then_inc(dsem[s][0], 16)
        dsem[s][1] += 16

    def dma_wait(eng, s):
        ENG[eng].wait_ge(dsem[s][0], dsem[s][1])

    def tap_wait():
        for e in ("dve", "act", "pe", "sp"):
            po.wait_ge(done[e], cnt[e])

    def sp_mark():
        sp.sem_inc(done["sp"], 1)
        cnt["sp"] += 1

    def pool_mark():
        po.sem_inc(done["pool"], 1)
        cnt["pool"] += 1

    CONSTS = [0.0, 1.0, 0.125, -float(C_SOFT), 1e-5, -0.5]
    cpool = mk(nc.sbuf_tensor("cpool", [128, len(CONSTS)], F32))
    for ci, cv in enumerate(CONSTS):
        nc.const_aps.register(F32, float(cv), cpool[:, ci:ci + 1])

    ident = mk(nc.sbuf_tensor("ident", [128, 128], BF16))
    p32 = mk(nc.sbuf_tensor("p32", [128, 128], BF16))
    negi = mk(nc.sbuf_tensor("negi", [128, 128], BF16))
    ucur = mk(nc.sbuf_tensor("ucur", [128, 128], BF16))
    useg = mk(nc.sbuf_tensor("useg", [128, SPW], BF16))
    segf = mk(nc.sbuf_tensor("segf", [128, 256], BF16))
    onesr = mk(nc.sbuf_tensor("onesr", [1, 128], BF16))
    onesf = mk(nc.sbuf_tensor("onesf", [1, 128], F32))
    boutr = mk(nc.sbuf_tensor("boutr", [1, D], BF16))
    wpsh = mk(nc.sbuf_tensor("wpsh", [128, 32], BF16))
    lngt = mk(nc.sbuf_tensor("lngt", [128, 4 * DH], BF16))
    iotap = mk(nc.sbuf_tensor("iotap", [128, 1], F32))
    iotaf = mk(nc.sbuf_tensor("iotaf", [128, 128], F32))
    invf = mk(nc.sbuf_tensor("invf", [1, 32], F32))
    invfs = mk(nc.sbuf_tensor("invfs", [1, 32], F32))
    tabsc = mk(nc.sbuf_tensor("tabsc", [128, 128], F32))
    costab = mk(nc.sbuf_tensor("costab", [128, T], BF16))
    sintab = mk(nc.sbuf_tensor("sintab", [128, T], BF16))

    wq_sb = mk(nc.sbuf_tensor("wq_sb", [128, 2, DCH * 128], BF16))
    wo_sb = mk(nc.sbuf_tensor("wo_sb", [128, 2, D], BF16))
    wg_sb = mk(nc.sbuf_tensor("wg_sb", [128, 2, DCH * 128], BF16))
    wu_sb = mk(nc.sbuf_tensor("wu_sb", [128, 2, DCH * 128], BF16))
    sg2 = mk(nc.sbuf_tensor("sg2", [128, 2, 512], BF16))
    hf2 = mk(nc.sbuf_tensor("hf2", [128, 2, 512], BF16))
    wd_sb = mk(nc.sbuf_tensor("wd_sb", [128, 3, D], BF16))

    x_tm = mk(nc.sbuf_tensor("x_tm", [128, TC, D], BF16))
    xfw = mk(nc.sbuf_tensor("xfw", [128, DCH, T], BF16))
    q_rot = mk(nc.sbuf_tensor("q_rot", [128, DCH, T], BF16))
    kvro = mk(nc.sbuf_tensor("kvro", [128, DCH, T], BF16))
    kv_km = mk(nc.sbuf_tensor("kv_km", [128, WPM, H * DH], BF16))
    vh_km = mk(nc.sbuf_tensor("vh_km", [128, WPM + 1, H * DH], BF16))
    vh_f = mk(nc.sbuf_tensor("vh_f", [128, DCH, (WPM + 1) * WIN], BF16))
    gkm = mk(nc.sbuf_tensor("gkm", [128, LMT, H * DH], BF16))
    vg_km = mk(nc.sbuf_tensor("vg_km", [128, LMT, H * DH], BF16))
    vg_f = mk(nc.sbuf_tensor("vg_f", [128, DCH, LMT * 128], BF16))
    pscore = mk(nc.sbuf_tensor("pscore", [H, T], F32))
    pkv = mk(nc.sbuf_tensor("pkv", [H, T], BF16))
    pkv_km = mk(nc.sbuf_tensor("pkv_km", [128, TC, H], BF16))
    pt_loc = mk(nc.sbuf_tensor("pt_loc", [128, WPM + 1, 256], BF16))
    pt_g = mk(nc.sbuf_tensor("pt_g", [128, LMT, T], BF16))
    p_sb = mk(nc.sbuf_tensor("p_sb", [128, 1024], BF16))
    denom = mk(nc.sbuf_tensor("denom", [128, 6], F32))

    x1 = mk(nc.sbuf_tensor("x1", [128, TC, D], F32))
    f32a = mk(nc.sbuf_tensor("f32a", [128, 3072], F32))
    qa = mk(nc.sbuf_tensor("qa", [128, KI], BF16))
    qs = mk(nc.sbuf_tensor("qs", [128, KI], BF16))
    qc1 = mk(nc.sbuf_tensor("qc1", [128, KI], BF16))
    qc2 = mk(nc.sbuf_tensor("qc2", [128, KI], BF16))
    qbm = mk(nc.sbuf_tensor("qbm", [128, KI // 16], F32))
    qsc = mk(nc.sbuf_tensor("qsc", [128, KI // 16], F32))
    qrs = mk(nc.sbuf_tensor("qrs", [128, KI // 16], F32))
    qscb = mk(nc.sbuf_tensor("qscb", [128, KI // 16], BF16))
    qrsb = mk(nc.sbuf_tensor("qrsb", [128, KI // 16], BF16))
    h_tm = mk(nc.sbuf_tensor("h_tm", [128, TC, KI], BF16))
    hq_tm = mk(nc.sbuf_tensor("hq_tm", [128, 1024], BF16))
    hq_f = mk(nc.sbuf_tensor("hq_f", [128, NKB, 256], BF16))
    stats = mk(nc.sbuf_tensor("stats", [128, 8 * 4 * H], F32))
    lnsc = mk(nc.sbuf_tensor("lnsc", [128, 12], F32))
    rowsum = mk(nc.sbuf_tensor("rowsum", [128, 4], F32))

    ps_big = mk(nc.psum_tensor("ps_big", [128, 2048], F32))
    ps_s = mk(nc.psum_tensor("ps_s", [128, 1024], F32))
    ps_t = mk(nc.psum_tensor("ps_t", [128, 1024], BF16))
    ps_p = mk(nc.psum_tensor("ps_p", [128, 512], F32))

    for ci, cv in enumerate(CONSTS):
        ve.memset(cpool[:, ci:ci + 1], float(cv))
    ve.memset(gkm[:, :, :], 0.0)
    ve.memset(vg_km[:, :, :], 0.0)
    ve.memset(pt_g[:, :, :], 0.0)
    ve.memset(onesr[0:1, :], 1.0)
    R("dve", ve.memset(onesf[0:1, :], 1.0))
    fence(["dve"], ["pool", "sp", "act", "pe"])

    po.iota(iotaf[:, :], [[1, 128]], channel_multiplier=0,
            allow_small_or_imprecise_dtypes=True)
    R("pool", po.iota(iotap[:, :], [[0, 1]], channel_multiplier=1,
                      allow_small_or_imprecise_dtypes=True))
    fence(["pool"], ["dve"])

    ipb = bass.AP(iotap, 0, [[1, 128], [0, 128]])
    ve.tensor_tensor(ident[:, :], iotaf[:, :], ipb, aop.is_equal)
    ve.tensor_scalar(negi[:, :], ident[:, :], -30000.0, None, aop.mult)
    ve.tensor_scalar(tabsc[:, 0:1], iotap[:, :], 63.5, 64.0, aop.is_gt, aop.mult)
    ve.tensor_tensor(tabsc[:, 1:2], iotap[:, :], tabsc[:, 0:1], aop.subtract)
    ve.tensor_scalar(tabsc[:, 2:3], tabsc[:, 1:2], 31.5, 64.0, aop.is_gt, aop.mult)
    ve.tensor_tensor(tabsc[:, 3:4], iotap[:, :], tabsc[:, 2:3], aop.subtract)
    ve.tensor_scalar(tabsc[:, 3:4], tabsc[:, 3:4], 32.0, None, aop.add)
    ve.tensor_tensor(p32[:, :], iotaf[:, :],
                     bass.AP(tabsc, 3, [[128, 128], [0, 128]]), aop.is_equal)
    ve.tensor_tensor(ucur[:, :], iotaf[:, :], ipb, aop.is_gt)
    ve.tensor_scalar(tabsc[:, 8:8 + SPW], iotaf[:, 0:SPW], 16.0, 15.0, aop.mult, aop.add)
    ve.tensor_tensor(useg[:, :], tabsc[:, 8:8 + SPW],
                     bass.AP(iotap, 0, [[1, 128], [0, SPW]]), aop.is_ge)
    ve.tensor_scalar(tabsc[:, 4:5], iotap[:, :], 15.5, None, aop.is_gt)
    for k in range(2, 8):
        ve.scalar_tensor_tensor(tabsc[:, 4:5], iotap[:, :], 16.0 * k - 0.5,
                                tabsc[:, 4:5], aop.is_gt, aop.add)
    ve.tensor_scalar(tabsc[:, 5:6], tabsc[:, 4:5], 128.0, None, aop.add)
    fence(["dve"], ["pool"])
    po.iota(f32a[:, 0:256], [[1, 256]], channel_multiplier=0,
            allow_small_or_imprecise_dtypes=True).then_inc(done["pool"], 1)
    cnt["pool"] += 1
    ve.wait_ge(done["pool"], cnt["pool"])
    R("dve", ve.tensor_tensor(segf[:, :], f32a[:, 0:256],
                              bass.AP(tabsc, 5, [[128, 128], [0, 256]]), aop.is_equal))
    fence(["dve"], ["act", "pe", "sp", "pool"])

    po.iota(invf[0:1, :], [[1, 32]], channel_multiplier=0,
            allow_small_or_imprecise_dtypes=True).then_inc(done["pool"], 1)
    cnt["pool"] += 1
    ac.wait_ge(done["pool"], cnt["pool"])
    R("act", ac.activation(invf[0:1, :], invf[0:1, :], AF.Exp,
                           bias=0.0, scale=-math.log(10000.0) / 32.0))
    fence(["act"], ["dve", "pe"])

    dma("sp", "dm", f32a[0:1, 0:D], bout_in[None, :])
    dma("sp", "dm", f32a[0:64, 1024:1025], wproj_in[:, None])
    dma("sp", "dm", f32a[64:128, 1025:1026], wproj_in[:, None])
    dma("sp", "dm", f32a[:, 1026:1282],
        bass.AP(lng_in.tensor, 0, [[0, 128], [1, 4 * DH]]))
    dma_wait("sp", "dm")
    sp_mark()
    ve.wait_ge(done["sp"], cnt["sp"])
    ve.tensor_copy(boutr[0:1, :], f32a[0:1, 0:D])
    R("dve", ve.tensor_copy(lngt[:, :], f32a[:, 1026:1282]))
    ve.tensor_scalar(tabsc[:, 6:7], iotap[:, :], 63.5, 16.0, aop.is_gt, aop.add)
    ve.tensor_tensor(tabsc[:, 7:8], iotaf[:, 0:1], iotaf[:, 0:1], aop.subtract)
    ve.tensor_tensor(qa[:, 0:32], iotaf[:, 0:32],
                     bass.AP(tabsc, 6, [[128, 128], [0, 32]]), aop.is_equal)
    ve.tensor_copy(tabsc[0:64, 7:8], f32a[0:64, 1024:1025])
    ve.tensor_copy(tabsc[64:128, 7:8], f32a[64:128, 1025:1026])
    R("dve", ve.tensor_tensor(wpsh[:, :], qa[:, 0:32],
                              bass.AP(tabsc, 7, [[128, 128], [0, 32]]), aop.mult))
    fence(["dve"], ["pe", "sp"])

    for src, dst, cols in ((wq_in, wqkv_d, 0), (wkv_in, wqkv_d, D), (wout_in, woutb_d, 0)):
        for cc in range(DCH):
            dma("sp", "din", f32a[:, 0:D], src[cc * 128:(cc + 1) * 128, :])
            dma_wait("sp", "din")
            sp_mark()
            ve.wait_ge(done["sp"], cnt["sp"])
            R("dve", ve.tensor_copy(qa[:, 0:D], f32a[:, 0:D]))
            fence(["dve"], ["pool"])
            if dst is wqkv_d:
                dma("pool", "dw", bass.AP(dst.tensor, cc * 128 * 2 * D + cols,
                                          [[2 * D, 128], [1, D]]), qa[:, 0:D])
            else:
                dma("pool", "dw", bass.AP(dst.tensor, cc * 128 * D,
                                          [[D, 128], [1, D]]), qa[:, 0:D])
            dma_wait("pool", "dw")
            pool_mark()
            sp.wait_ge(done["pool"], cnt["pool"])

    for wg0 in range(0, N // WIN, 8):
        for wi in range(8):
            ve.tensor_scalar(f32a[0:1, 1024 + wi * 32:1024 + (wi + 1) * 32],
                             invf[0:1, :], float((wg0 + wi) * WIN), None, aop.mult)
        for wi in range(8):
            ve.tensor_copy(f32a[0:1, 1280 + wi * 32:1280 + (wi + 1) * 32],
                           invf[0:1, :])
        R("dve", ve.tensor_copy(f32a[0:1, 1280:1281], f32a[0:1, 1280:1281]))
        fence(["dve"], ["pe"])
        te.matmul(ps_big[:, 0:256], iotaf[0:1, :], f32a[0:1, 1280:1536],
                  start=True, stop=False)
        R("pe", te.matmul(ps_big[:, 0:256], onesf[0:1, :], f32a[0:1, 1024:1280],
                          start=False, stop=True))
        fence(["pe"], ["dve"])
        ve.tensor_copy(f32a[:, 0:256], ps_big[:, 0:256])
        for j in range(9, -1, -1):
            c = float(2.0 ** j * 2.0 * math.pi)
            ve.tensor_scalar(f32a[:, 256:512], f32a[:, 0:256], c, c,
                             aop.is_ge, aop.mult)
            ve.tensor_tensor(f32a[:, 0:256], f32a[:, 0:256], f32a[:, 256:512],
                             aop.subtract)
        ve.tensor_scalar(f32a[:, 256:512], f32a[:, 0:256], float(math.pi),
                         float(2 * math.pi), aop.is_gt, aop.mult)
        ve.tensor_tensor(f32a[:, 512:768], f32a[:, 0:256], f32a[:, 256:512],
                         aop.subtract)
        ve.tensor_scalar(f32a[:, 256:512], f32a[:, 0:256], float(math.pi / 2),
                         None, aop.add)
        ve.tensor_scalar(f32a[:, 768:1024], f32a[:, 256:512], float(math.pi),
                         float(2 * math.pi), aop.is_gt, aop.mult)
        R("dve", ve.tensor_tensor(f32a[:, 768:1024], f32a[:, 256:512],
                                  f32a[:, 768:1024], aop.subtract))
        fence(["dve"], ["act"])
        ac.activation(f32a[:, 0:256], f32a[:, 512:768], AF.Sin)
        R("act", ac.activation(f32a[:, 256:512], f32a[:, 768:1024], AF.Sin))
        fence(["act"], ["dve"])
        R("dve", ve.tensor_copy(qa[:, 0:512], f32a[:, 0:512]))
        fence(["dve"], ["pe"])
        for tt in range(4):
            R("pe", te.transpose(ps_t[:, tt * 128:(tt + 1) * 128],
                                 qa[:, tt * 128:(tt + 1) * 128], ident[:, :]))
        fence(["pe"], ["dve"])
        with nodrain():
            for wi in range(8):
                blk, rowo = divmod(wi * 32, 128)
                wc = (wg0 + wi) * WIN
                ve.tensor_scalar(sintab[0:32, 0:WIN] if False else qc1[0:32, wc % 512:wc % 512 + WIN], ps_t[0:1, 0:1], 0.0, None, aop.mult) if False else None
                ve.tensor_scalar(qc1[0:32, 0:WIN], ps_t[rowo:rowo + 32, blk * 128:(blk + 1) * 128],
                                 -1.0, None, aop.mult)
                ve.tensor_copy(qc1[32:64, 0:WIN], ps_t[rowo:rowo + 32, blk * 128:(blk + 1) * 128])
                ve.tensor_copy(qc1[64:96, 0:WIN], qc1[0:32, 0:WIN]) if False else None
                ve.tensor_copy(qc2[0:32, 0:WIN], ps_t[rowo:rowo + 32, (2 + blk) * 128:(3 + blk) * 128])
                R("dve", ve.tensor_copy(qc2[32:64, 0:WIN], ps_t[rowo:rowo + 32, (2 + blk) * 128:(3 + blk) * 128]))
                fence(["dve"], ["pool"])
                dma("pool", "dw", bass.AP(sintab_d.tensor, wc, [[N, 64], [1, WIN]]),
                    qc1[0:64, 0:WIN])
                dma("pool", "dw", bass.AP(sintab_d.tensor, 64 * N + wc, [[N, 64], [1, WIN]]),
                    qc1[0:64, 0:WIN])
                dma("pool", "dw", bass.AP(costab_d.tensor, wc, [[N, 64], [1, WIN]]),
                    qc2[0:64, 0:WIN])
                dma("pool", "dw", bass.AP(costab_d.tensor, 64 * N + wc, [[N, 64], [1, WIN]]),
                    qc2[0:64, 0:WIN])
                dma_wait("pool", "dw")
                pool_mark()
                ve.wait_ge(done["pool"], cnt["pool"])

    def stair(K, rsb_t, scb_t, ysrc, out_t, afp32=False, out_flat=None, final3=None):
        if afp32:
            ac.activation(qs[:, 0:K], ysrc, AF.Sign)
            R("act", ac.activation(ysrc, ysrc, AF.Abs))
            fence(["act"], ["dve"])
            a = ysrc
        else:
            ac.activation(qa[:, 0:K], ysrc, AF.Abs)
            R("act", ac.activation(qs[:, 0:K], ysrc, AF.Sign))
            fence(["act"], ["dve"])
            a = qa[:, 0:K]
        ve.tensor_scalar(qc1[:, 0:K], a, 0.25, None, aop.is_gt)
        ve.scalar_tensor_tensor(qc2[:, 0:K], a, 0.75, qc1[:, 0:K], aop.is_gt, aop.add)
        ve.scalar_tensor_tensor(qc1[:, 0:K], a, 1.25, qc2[:, 0:K], aop.is_gt, aop.add)
        ve.scalar_tensor_tensor(qc2[:, 0:K], a, 1.75, qc1[:, 0:K],
                                 aop.is_gt, aop.add)
        ve.tensor_scalar(qc1[:, 0:K], qc2[:, 0:K], 0.5, None, aop.mult)
        ve.scalar_tensor_tensor(qc2[:, 0:K], a, 2.5, qc1[:, 0:K], aop.is_gt, aop.add)
        ve.scalar_tensor_tensor(qc1[:, 0:K], a, 3.5, qc2[:, 0:K], aop.is_gt, aop.add)
        ve.tensor_scalar(qc2[:, 0:K], a, 5.0, 2.0, aop.is_gt, aop.mult)
        ve.tensor_tensor(qc1[:, 0:K], qc1[:, 0:K], qc2[:, 0:K], aop.add)
        ve.tensor_tensor(qc2[:, 0:K], qc1[:, 0:K], qs[:, 0:K], aop.mult)
        if final3 is not None:
            dst, srcv, scb = final3
            return R("dve", ve.tensor_tensor(dst, srcv, scb, aop.mult))
        scb = bass.AP(scb_t, 0, [[KI // 16, 128], [1, K // 16], [0, 16]])
        dst = out_flat if out_flat is not None else out_t.rearrange(
            "p (a b) -> p a b", b=16)
        return R("dve", ve.tensor_tensor(
            dst, qc2[:, 0:K].rearrange("p (a b) -> p a b", b=16),
            scb, aop.mult))

    def wquant(src_dram, dst_dram, nrows, kcols):
        nrt = nrows // 128
        kct = kcols // 128
        dst_rowlen = nrows
        for nr in range(nrt):
            sp.wait_ge(done["act"], cnt["act"])
            sp.wait_ge(done["dve"], cnt["dve"])
            dma("sp", "din", f32a[:, 0:kcols], src_dram[nr * 128:(nr + 1) * 128, :])
            dma_wait("sp", "din")
            sp_mark()
            ve.wait_ge(done["sp"], cnt["sp"])
            ve.wait_ge(done["pool"], cnt["pool"])
            nb = kcols // 16
            ve.tensor_reduce(qbm[:, 0:nb],
                             f32a[:, 0:kcols].rearrange("p (a b) -> p a b", b=16),
                             AX.X, aop.max, apply_absolute_value=True)
            ve.tensor_scalar(qsc[:, 0:nb], qbm[:, 0:nb], 1.0 / 6.0, 1e-12,
                             aop.mult, aop.max)
            ve.reciprocal(qrs[:, 0:nb], qsc[:, 0:nb])
            ve.tensor_copy(qscb[:, 0:nb], qsc[:, 0:nb])
            rsb = bass.AP(qrs, 0, [[KI // 16, 128], [1, nb], [0, 16]])
            R("dve", ve.tensor_tensor(
                f32a[:, 0:kcols].rearrange("p (a b) -> p a b", b=16),
                f32a[:, 0:kcols].rearrange("p (a b) -> p a b", b=16),
                rsb, aop.mult))
            fence(["dve"], ["act"])
            stair(kcols, qrs, qscb, f32a[:, 0:kcols], qc2[:, 0:kcols], afp32=True)
            fence(["dve"], ["pe"])
            for k0 in range(0, kct, 8):
                kn = min(8, kct - k0)
                for kk in range(kn):
                    R("pe", te.transpose(ps_t[:, kk * 128:(kk + 1) * 128],
                                         qc2[:, (k0 + kk) * 128:(k0 + kk + 1) * 128],
                                         ident[:, :]))
                fence(["pe"], ["dve"])
                R("dve", ve.tensor_copy(qa[:, 0:kn * 128],
                                        bass.AP(ps_t, 0, [[1024, 128], [1, kn * 128]])))
                fence(["dve"], ["pool", "pe"])
                dst = bass.AP(dst_dram.tensor, (k0 * 128) * dst_rowlen + nr * 128,
                              [[dst_rowlen, 128], [128 * dst_rowlen, kn], [1, 128]])
                dma("pool", "dw", dst,
                    bass.AP(qa, 0, [[KI, 128], [128, kn], [1, 128]]))
                dma_wait("pool", "dw")
                pool_mark()
                ve.wait_ge(done["pool"], cnt["pool"])
            last["pe"] = None

    wquant(wg_in, wgq_d, KI, D)
    wquant(wu_in, wuq_d, KI, D)
    wquant(wd_in, wdq_d, D, KI)
    po.sem_inc(done["pool"], 1); cnt["pool"] += 1
    for e in ("pe", "act", "dve", "sp"):
        ENG[e].wait_ge(done["pool"], cnt["pool"])

    for bb in range(B_CORE):
        for mt in range(MPB):
            t0 = mt * T
            w0 = mt * WPM
            sg0 = w0 * SPW
            nsg = WPM * SPW
            lt0, lr0 = divmod(sg0, 128)

            sp.wait_ge(done["dve"], cnt["dve"])
            sp.wait_ge(done["pe"], cnt["pe"])
            dma("sp", "dm", costab[:, :], costab_d[:, t0:t0 + T])
            dma("sp", "dm", sintab[:, :], sintab_d[:, t0:t0 + T])
            dma_wait("sp", "dm")
            sp_mark()
            for e in ("dve", "pe", "act"):
                ENG[e].wait_ge(done["sp"], cnt["sp"])
            if "cost" in tap_d:
                tap_wait()
                dma("pool", "dmp", tap_d["cost"][bb, mt], costab[:, :])

            for tcc in range(TC):
                sp.wait_ge(done["dve"], cnt["dve"])
                sp.wait_ge(done["act"], cnt["act"])
                sp.wait_ge(done["pool"], cnt["pool"])
                dma("sp", "din", f32a[:, 0:D],
                    x_in[bb, t0 + tcc * 128:t0 + (tcc + 1) * 128, :])
                dma_wait("sp", "din")
                sp_mark()
                ve.wait_ge(done["sp"], cnt["sp"])
                ve.wait_ge(done["pe"], cnt["pe"])
                R("dve", ve.tensor_copy(x_tm[:, tcc, :], f32a[:, 0:D]))
                fence(["dve"], ["pe"])
                for cc in range(DCH):
                    R("pe", te.transpose(ps_t[:, cc * 128:(cc + 1) * 128],
                                         x_tm[:, tcc, cc * 128:(cc + 1) * 128],
                                         ident[:, :]))
                fence(["pe"], ["dve"])
                R("dve", ve.tensor_copy(
                    bass.AP(xfw, tcc * 128, [[DCH * T, 128], [T, DCH], [1, 128]]),
                    bass.AP(ps_t, 0, [[1024, 128], [128, DCH], [1, 128]])))
                fence(["dve"], ["pe"])

            if "xtm" in tap_d:
                tap_wait()
                dma("pool", "dmp", tap_d["xtm"][bb, mt], x_tm[:, :, :])
            if "xf" in tap_d:
                tap_wait()
                dma("pool", "dmp", tap_d["xf"][bb, mt], xfw[:, :, :])
            ctile = bass.AP(costab, 0, [[T, 128], [0, 2], [1, T]])
            stile = bass.AP(sintab, 0, [[T, 128], [0, 2], [1, T]])
            qk_ev = [0] * (2 * DCH)
            rot_dve = {}
            wq_guard = {}
            for fc in range(2 * DCH):
                db = fc % 2
                if db in wq_guard:
                    waiton("pool", "pe", wq_guard[db])
                dma("pool", "dw", wq_sb[:, db, :],
                    bass.AP(wqkv_d.tensor, fc * 128,
                            [[2 * D, 128], [128 * 2 * D, DCH], [1, 128]]))
                te.wait_ge(dsem["dw"][0], dsem["dw"][1])
                if fc >= 2:
                    waiton("pe", "act", qk_ev[fc - 2])
                for cc in range(DCH):
                    R("pe", te.matmul(ps_big[:, db * 512:db * 512 + T],
                                      wq_sb[:, db, cc * 128:(cc + 1) * 128],
                                      xfw[:, cc, :], start=(cc == 0), stop=(cc == DCH - 1)))
                qmm = mark("pe")
                wq_guard[db] = qmm
                waiton("act", "pe", qmm)
                with nodrain():
                    if fc < DCH:
                        R("act", ac.activation(q_rot[:, fc, :], ps_big[:, db * 512:db * 512 + T],
                                               AF.Copy, bias=0.0, scale=0.125))
                    else:
                        R("act", ac.activation(kvro[:, fc - DCH, :],
                                               ps_big[:, db * 512:db * 512 + T],
                                               AF.Copy, bias=0.0, scale=1.0))
                qk_ev[fc] = mark("act")
                if fc % 2 == 1:
                    rp = fc // 2
                    buf = q_rot if fc < DCH else kvro
                    cc0 = (fc if fc < DCH else fc - DCH) - 1
                    rps = ps_big[:, 1024:2048] if rp % 2 == 0 else ps_s[:, 0:1024]
                    waiton("pe", "act", qk_ev[fc])
                    if rp >= 2:
                        waiton("pe", "dve", rot_dve[rp - 2])
                    te.matmul(rps[:, 0:T], p32[:, :], buf[:, cc0, :],
                              start=True, stop=True)
                    R("pe", te.matmul(rps[:, T:2 * T], p32[:, :], buf[:, cc0 + 1, :],
                                      start=True, stop=True))
                    rmm = mark("pe")
                    waiton("dve", "pe", rmm)
                    bb2 = buf[:, cc0:cc0 + 2, :]
                    ve.tensor_tensor(qa[:, 0:2 * T].rearrange("p (a t) -> p a t", t=T),
                                     bb2, ctile, aop.mult)
                    ve.tensor_tensor(qa[:, 2 * T:4 * T].rearrange("p (a t) -> p a t", t=T),
                                     rps[:, 0:2 * T].rearrange("p (a t) -> p a t", t=T),
                                     stile, aop.mult)
                    R("dve", ve.tensor_tensor(
                        bb2, qa[:, 0:2 * T].rearrange("p (a t) -> p a t", t=T),
                        qa[:, 2 * T:4 * T].rearrange("p (a t) -> p a t", t=T),
                        aop.add))
                    rot_dve[rp] = mark("dve")
            waiton("pe", "dve", rot_dve[DCH - 1])
            waiton("act", "dve", rot_dve[DCH - 1])

            if "qpre" in tap_d:
                tap_wait()
                dma("pool", "dmp", tap_d["qpre"][bb, mt], q_rot[:, :, :])
                dma_wait("pool", "dmp")
                pool_mark()
                for e in ("pe", "dve"):
                    ENG[e].wait_ge(done["pool"], cnt["pool"])
            if "qrot" in tap_d:
                tap_wait()
                dma("pool", "dmp", tap_d["qrot"][bb, mt], q_rot[:, :, :])

            for cc in range(DCH):
                R("pe", te.matmul(ps_p[0:H, 0:T], wpsh[:, 16 - 2 * cc:32 - 2 * cc],
                                  kvro[:, cc, :], start=(cc == 0),
                                  stop=(cc == DCH - 1)))
            fence(["pe"], ["act"])
            R("act", ac.activation(pscore[:, :], ps_p[0:H, 0:T], AF.Copy))
            fence(["act"], ["pe", "act"])
            if "pscore" in tap_d:
                tap_wait()
                dma("pool", "dmp", tap_d["pscore"][bb, mt], pscore[:, :])
                dma_wait("pool", "dmp")
                pool_mark()
                ac.wait_ge(done["pool"], cnt["pool"])
            R("act", ac.activation(pscore[:, :], pscore[:, :], AF.Exp))
            fence(["act"], ["dve"])
            ve.tensor_reduce(qbm[0:H, 0:T // 16],
                             pscore[:, :].rearrange("p (a b) -> p a b", b=16),
                             AX.X, aop.add)
            ve.reciprocal(qrs[0:H, 0:T // 16], qbm[0:H, 0:T // 16])
            rsb = bass.AP(qrs, 0, [[KI // 16, H], [1, T // 16], [0, 16]])
            R("dve", ve.tensor_tensor(
                pkv[:, :].rearrange("p (a b) -> p a b", b=16),
                pscore[:, :].rearrange("p (a b) -> p a b", b=16), rsb, aop.mult))
            fence(["dve"], ["pe"])
            for tcc in range(TC):
                R("pe", te.transpose(ps_t[:, tcc * H:(tcc + 1) * H],
                                     pkv[:, tcc * 128:(tcc + 1) * 128],
                                     ident[0:H, 0:H]))
            fence(["pe"], ["dve"])
            R("dve", ve.tensor_copy(
                bass.AP(pkv_km, 0, [[TC * H, 128], [H, TC], [1, H]]),
                bass.AP(ps_t, 0, [[1024, 128], [H, TC], [1, H]])))
            fence(["dve"], ["pe"])

            for w in range(WPM):
                te.wait_ge(done["dve"], cnt["dve"])
                for cc in range(DCH):
                    R("pe", te.transpose(ps_t[:, cc * 128:(cc + 1) * 128],
                                         kvro[:, cc, w * 128:(w + 1) * 128],
                                         ident[:, :]))
                fence(["pe"], ["dve"])
                R("dve", ve.tensor_copy(kv_km[:, w, :],
                                        bass.AP(ps_t, 0, [[1024, 128], [1, H * DH]])))
                fence(["dve"], ["pe"])
            if "kvkm" in tap_d:
                tap_wait()
                dma("pool", "dmp", tap_d["kvkm"][bb, mt], kv_km[:, :, :])

            for w in range(WPM):
                pk = bass.AP(pkv_km, w * H, [[TC * H, 128], [1, H], [0, DH]])
                R("dve", ve.tensor_tensor(
                    bass.AP(xfw, w * H * DH, [[DCH * T, 128], [DH, H], [1, DH]]),
                    kv_km[:, w, :].rearrange("p (h d) -> p h d", d=DH),
                    pk, aop.mult))
            fence(["dve"], ["pe"])
            for w in range(WPM):
                lr = w * SPW
                st = segf[:, 128 - lr:256 - lr]
                te.matmul(ps_p[:, 0:512], st,
                          bass.AP(xfw, w * H * DH, [[DCH * T, 128], [1, 512]]),
                          start=(w == 0), stop=(w == WPM - 1))
                R("pe", te.matmul(ps_big[:, 1536:2048], st,
                                  bass.AP(xfw, w * H * DH + 512,
                                          [[DCH * T, 128], [1, 512]]),
                                  start=(w == 0), stop=(w == WPM - 1)))
            fence(["pe"], ["act"])
            with nodrain():
                ac.activation(gkm[lr0:lr0 + nsg, lt0, 0:512], ps_p[0:nsg, 0:512],
                              AF.Copy)
                R("act", ac.activation(gkm[lr0:lr0 + nsg, lt0, 512:1024],
                                       ps_big[0:nsg, 1536:2048], AF.Copy))
            fence(["act"], ["pe", "dve"])
            if "gkm" in tap_d:
                tap_wait()
                dma("pool", "dmp", tap_d["gkm"][bb, mt], gkm[:, :, :])

            def ln64(src_ap, dst_ap, r0, nr, goff):
                R("dve", ve.tensor_reduce(
                    stats[r0:r0 + nr, 0:H],
                    src_ap.rearrange("p (h d) -> p h d", d=DH), AX.X, aop.add))
                fence(["dve"], ["act"])
                R("act", ac.activation(f32a[r0:r0 + nr, 0:H * DH], src_ap, AF.Square))
                fence(["act"], ["dve"])
                ve.tensor_reduce(stats[r0:r0 + nr, H:2 * H],
                                 f32a[r0:r0 + nr, 0:H * DH].rearrange(
                                     "p (h d) -> p h d", d=DH), AX.X, aop.add)
                ve.tensor_scalar(stats[r0:r0 + nr, 2 * H:3 * H],
                                 stats[r0:r0 + nr, 0:H], 1.0 / DH, None, aop.mult)
                ve.tensor_scalar(stats[r0:r0 + nr, 3 * H:4 * H],
                                 stats[r0:r0 + nr, H:2 * H], 1.0 / DH, None, aop.mult)
                ve.tensor_tensor(stats[r0:r0 + nr, 4 * H:5 * H],
                                 stats[r0:r0 + nr, 2 * H:3 * H],
                                 stats[r0:r0 + nr, 2 * H:3 * H], aop.mult)
                R("dve", ve.tensor_tensor(stats[r0:r0 + nr, 3 * H:4 * H],
                                          stats[r0:r0 + nr, 3 * H:4 * H],
                                          stats[r0:r0 + nr, 4 * H:5 * H], aop.subtract))
                fence(["dve"], ["act"])
                R("act", ac.activation(stats[r0:r0 + nr, 4 * H:5 * H],
                                       stats[r0:r0 + nr, 3 * H:4 * H], AF.Ln,
                                       bias=1e-5, scale=1.0))
                R("act", ac.activation(stats[r0:r0 + nr, 5 * H:6 * H],
                                       stats[r0:r0 + nr, 4 * H:5 * H], AF.Exp,
                                       bias=0.0, scale=-0.5))
                fence(["act"], ["dve"])
                srow = 8 * 4 * H
                mu_b = bass.AP(stats, r0 * srow + 2 * H, [[srow, nr], [1, H], [0, DH]])
                rs_b = bass.AP(stats, r0 * srow + 5 * H, [[srow, nr], [1, H], [0, DH]])
                g_b = bass.AP(lngt, r0 * 4 * DH + goff, [[4 * DH, nr], [0, H], [1, DH]])
                b_b = bass.AP(lngt, r0 * 4 * DH + goff + DH, [[4 * DH, nr], [0, H], [1, DH]])
                sq = f32a[r0:r0 + nr, 0:H * DH].rearrange("p (h d) -> p h d", d=DH)
                ve.tensor_tensor(sq, src_ap.rearrange("p (h d) -> p h d", d=DH),
                                 mu_b, aop.subtract)
                ve.tensor_tensor(sq, sq, rs_b, aop.mult)
                ve.tensor_tensor(sq, sq, g_b, aop.mult)
                R("dve", ve.tensor_tensor(dst_ap.rearrange("p (h d) -> p h d", d=DH),
                                          sq, b_b, aop.add))

            WH = WPM * H
            W2 = WPM // 2
            GH = W2 * H
            kvv = kv_km[:, :, :].rearrange("p w (h d) -> p (w h) d", d=DH)
            R("dve", ve.tensor_reduce(stats[:, 0:WH], kvv, AX.X, aop.add))
            fence(["dve"], ["act"])
            for hw in range(2):
                R("act", ac.activation(
                    f32a[:, 0:GH * DH],
                    kv_km[:, hw * W2:(hw + 1) * W2, :].rearrange(
                        "p w hd -> p (w hd)"), AF.Square))
                fence(["act"], ["dve"])
                R("dve", ve.tensor_reduce(
                    stats[:, WH + hw * GH:WH + (hw + 1) * GH],
                    f32a[:, 0:GH * DH].rearrange("p (g d) -> p g d", d=DH),
                    AX.X, aop.add))
                if hw == 0:
                    fence(["dve"], ["act"])
            ve.tensor_scalar(stats[:, 2 * WH:3 * WH], stats[:, 0:WH],
                             1.0 / DH, None, aop.mult)
            ve.tensor_scalar(stats[:, 3 * WH:4 * WH], stats[:, WH:2 * WH],
                             1.0 / DH, None, aop.mult)
            ve.tensor_tensor(stats[:, 4 * WH:5 * WH], stats[:, 2 * WH:3 * WH],
                             stats[:, 2 * WH:3 * WH], aop.mult)
            R("dve", ve.tensor_tensor(stats[:, 3 * WH:4 * WH],
                                      stats[:, 3 * WH:4 * WH],
                                      stats[:, 4 * WH:5 * WH], aop.subtract))
            fence(["dve"], ["act"])
            ac.activation(stats[:, 4 * WH:5 * WH], stats[:, 3 * WH:4 * WH],
                          AF.Ln, bias=1e-5, scale=1.0)
            R("act", ac.activation(stats[:, 5 * WH:6 * WH], stats[:, 4 * WH:5 * WH],
                                   AF.Exp, bias=0.0, scale=-0.5))
            fence(["act"], ["dve"])
            for hw in range(2):
                kvh = kv_km[:, hw * W2:(hw + 1) * W2, :].rearrange(
                    "p w (h d) -> p (w h) d", d=DH)
                mu_b = bass.AP(stats, 2 * WH + hw * GH,
                               [[8 * 4 * H, 128], [1, GH], [0, DH]])
                rs_b = bass.AP(stats, 5 * WH + hw * GH,
                               [[8 * 4 * H, 128], [1, GH], [0, DH]])
                g_b = bass.AP(lngt, 0, [[4 * DH, 128], [0, GH], [1, DH]])
                b_b = bass.AP(lngt, DH, [[4 * DH, 128], [0, GH], [1, DH]])
                sqv = f32a[:, 0:GH * DH].rearrange("p (g d) -> p g d", d=DH)
                ve.tensor_tensor(sqv, kvh, mu_b, aop.subtract)
                ve.tensor_tensor(sqv, sqv, rs_b, aop.mult)
                ve.tensor_tensor(sqv, sqv, g_b, aop.mult)
                R("dve", ve.tensor_tensor(
                    vh_km[:, 1 + hw * W2:1 + (hw + 1) * W2, :].rearrange(
                        "p w (h d) -> p (w h) d", d=DH),
                    sqv, b_b, aop.add))
            fence(["dve"], ["pe", "act"])
            if "vhkm" in tap_d:
                tap_wait()
                dma("pool", "dmp", tap_d["vhkm"][bb, mt], vh_km[:, 1:WPM + 1, :])
            ln64(gkm[lr0:lr0 + nsg, lt0, :], vg_km[lr0:lr0 + nsg, lt0, :],
                 lr0, nsg, 2 * DH)
            fence(["dve"], ["pe", "act"])
            if "vgkm" in tap_d:
                tap_wait()
                dma("pool", "dmp", tap_d["vgkm"][bb, mt], vg_km[:, :, :])

            for w in range(WPM):
                for cc in range(DCH):
                    R("pe", te.transpose(ps_t[:, cc * 128:(cc + 1) * 128],
                                         vh_km[:, w + 1, cc * 128:(cc + 1) * 128],
                                         ident[:, :]))
                fence(["pe"], ["dve"])
                R("dve", ve.tensor_copy(
                    bass.AP(vh_f, (w + 1) * WIN,
                            [[DCH * (WPM + 1) * WIN, 128], [(WPM + 1) * WIN, DCH],
                             [1, WIN]]),
                    bass.AP(ps_t, 0, [[1024, 128], [128, DCH], [1, 128]])))
                fence(["dve"], ["pe"])
            for cc in range(DCH):
                R("pe", te.transpose(ps_t[:, cc * 128:(cc + 1) * 128],
                                     vg_km[:, lt0, cc * 128:(cc + 1) * 128],
                                     ident[:, :]))
            fence(["pe"], ["dve"])
            R("dve", ve.tensor_copy(
                bass.AP(vg_f, lt0 * 128 + lr0,
                        [[DCH * LMT * 128, 128], [LMT * 128, DCH], [1, nsg]]),
                bass.AP(ps_t, lr0, [[1024, 128], [128, DCH], [1, nsg]])))
            fence(["dve"], ["pe"])

            av_pe = [0] * DCH
            av_ev = [0] * DCH
            NW = 2 * DCH * WPM
            pe_sg = [0] * NW
            act_eg = [0] * NW
            dve_ng = [0] * NW
            pe_tg = [0] * NW
            dve_vg = [0] * NW

            def geom(w):
                wg = w0 + w
                has_prev = wg > 0
                loc0 = 128 if has_prev else 0
                gn_old = wg * SPW
                gn = gn_old + SPW
                return wg, has_prev, loc0, gn_old, gn, loc0 + 128

            for g in range(NW + 1):
                if g < NW:
                    hidx = g // WPM
                    w = g % WPM
                    cc = hidx // 2
                    ro = (hidx % 2) * 64
                    wg, has_prev, loc0, gn_old, gn, off2 = geom(w)
                    par = g % 2
                    pso = par * 512
                    qs_ap = q_rot[ro:ro + 64, cc, w * 128:(w + 1) * 128]
                    if g >= 2:
                        waiton("pe", "act", act_eg[g - 2])
                    kst = (w if has_prev else w + 1) * WIN
                    te.matmul(ps_s[:, pso:pso + loc0 + 128], qs_ap,
                              vh_f[ro:ro + 64, cc, kst:kst + loc0 + 128],
                              start=True, stop=False)
                    te.matmul(ps_s[:, pso + off2:pso + off2 + gn], qs_ap,
                              vg_f[ro:ro + 64, cc, 0:gn], start=False,
                              stop=False)
                    te.matmul(ps_s[:, pso + loc0:pso + loc0 + 128],
                              negi[:, :], ucur[:, :], start=False, stop=False)
                    R("pe", te.matmul(
                        ps_s[:, pso + off2 + gn_old:pso + off2 + gn],
                        negi[:, :], useg[:, :], start=False, stop=True))
                    pe_sg[g] = mark("pe")
                if g >= 1:
                    gq = g - 1
                    hidx = gq // WPM
                    w = gq % WPM
                    cc = hidx // 2
                    hp = hidx % 2
                    hh = hidx
                    ro = hp * 64
                    wg, has_prev, loc0, gn_old, gn, off2 = geom(w)
                    par = gq % 2
                    pso = par * 512
                    ncols = off2 + gn
                    waiton("act", "pe", pe_sg[gq])
                    if gq >= 2:
                        waiton("act", "pe", pe_tg[gq - 2])
                    with nodrain():
                        R("act", ac.activation(
                            p_sb[:, pso:pso + ncols], ps_s[:, pso:pso + ncols],
                            AF.Exp, bias=-C_SOFT, scale=1.0,
                            accum_out=denom[:, par:par + 1]))
                    act_eg[gq] = mark("act")
                    waiton("dve", "act", act_eg[gq])
                    ve.reciprocal(denom[:, 2 + par:3 + par],
                                  denom[:, par:par + 1])
                    with nodrain():
                        R("dve", ve.tensor_scalar(
                            p_sb[:, pso:pso + ncols], p_sb[:, pso:pso + ncols],
                            denom[:, 2 + par:3 + par], None, aop.mult))
                    dve_ng[gq] = mark("dve")
                    waiton("pe", "dve", dve_ng[gq])
                    if gq >= 1:
                        waiton("pe", "dve", dve_vg[gq - 1])
                    nt = 0
                    if has_prev:
                        te.transpose(ps_t[:, 0:128],
                                     p_sb[:, pso:pso + 128], ident[:, :])
                        nt += 1
                    te.transpose(ps_t[:, nt * 128:nt * 128 + 128],
                                 p_sb[:, pso + loc0:pso + loc0 + 128],
                                 ident[:, :])
                    nt += 1
                    gdone = 0
                    while gdone < gn:
                        take = min(128, gn - gdone)
                        R("pe", te.transpose(
                            ps_t[0:take, nt * 128:nt * 128 + 128],
                            p_sb[:, pso + off2 + gdone:pso + off2 + gdone + take],
                            ident[:, :]))
                        nt += 1
                        gdone += take
                    pe_tg[gq] = mark("pe")
                    waiton("dve", "pe", pe_tg[gq])
                    ve.memset(pt_g[:, :, w * 128:(w + 1) * 128], 0.0)
                    ve.drain()
                    with nodrain():
                        nt = 0
                        if has_prev:
                            ve.tensor_copy(pt_loc[:, w, 128:256],
                                           ps_t[:, 0:128])
                            nt += 1
                        ve.tensor_copy(pt_loc[:, w + 1, 0:128],
                                       ps_t[:, nt * 128:nt * 128 + 128])
                        nt += 1
                        gdone = 0
                        while gdone < gn:
                            take = min(128, gn - gdone)
                            ve.tensor_copy(pt_g[0:take, gdone // 128,
                                                w * 128:(w + 1) * 128],
                                           ps_t[0:take, nt * 128:nt * 128 + 128])
                            nt += 1
                            gdone += take
                        R("dve", ve.tensor_copy(denom[:, 4 + par:5 + par],
                                                denom[:, 2 + par:3 + par]))
                    dve_vg[gq] = mark("dve")
                    if w == WPM - 1:
                        avo = (cc % 2) * 1024
                        waiton("pe", "dve", dve_vg[gq])
                        if cc >= 2 and hp == 0:
                            waiton("pe", "dve", av_ev[cc - 2])
                        for aw in range(WPM):
                            awg = w0 + aw
                            te.matmul(ps_big[ro:ro + 64, avo + aw * 128:avo + (aw + 1) * 128],
                                      vh_km[:, aw + 1, hh * DH:(hh + 1) * DH],
                                      pt_loc[:, aw + 1, 0:128], start=(aw == 0), stop=False)
                            if awg > 0:
                                te.matmul(ps_big[ro:ro + 64, avo + aw * 128:avo + (aw + 1) * 128],
                                          vh_km[:, aw, hh * DH:(hh + 1) * DH],
                                          pt_loc[:, aw, 128:256], start=False, stop=False)
                        ngl = (w0 + WPM) * SPW
                        gdone = 0
                        while gdone < ngl:
                            take = min(128, ngl - gdone)
                            R("pe", te.matmul(ps_big[ro:ro + 64, avo:avo + T],
                                              vg_km[0:take, gdone // 128,
                                                    hh * DH:(hh + 1) * DH],
                                              pt_g[0:take, gdone // 128, 0:T],
                                              start=False, stop=(gdone + take >= ngl)))
                            gdone += take
                        av_pe[cc] = mark("pe")
                        if hp == 1:
                            waiton("dve", "pe", av_pe[cc])
                            with nodrain():
                                R("dve", ve.tensor_copy(kvro[:, cc, :],
                                                        ps_big[:, avo:avo + T]))
                            av_ev[cc] = mark("dve")
            waiton("pe", "dve", av_ev[DCH - 1])
            waiton("act", "dve", av_ev[DCH - 1])
            if "attnf" in tap_d:
                tap_wait()
                dma("pool", "dmp", tap_d["attnf"][bb, mt], kvro[:, :, :])

            wo_guard = {}
            stt_m = [0] * TC
            xq_dve = [0] * TC
            for tcc in range(TC):
                half = (tcc % 2) * 1024
                sub = tcc % 2
                lo = sub * 6
                for cc in range(DCH):
                    db = cc % 2
                    if db in wo_guard:
                        waiton("pool", "pe", wo_guard[db])
                    dma("pool", "dw", wo_sb[:, db, :],
                        bass.AP(woutb_d.tensor, cc * 128 * D, [[D, 128], [1, D]]))
                    te.wait_ge(dsem["dw"][0], dsem["dw"][1])
                    if cc == 0:
                        if tcc == 0:
                            te.wait_ge(done["dve"], cnt["dve"])
                            te.wait_ge(done["act"], cnt["act"])
                        elif tcc >= 2:
                            waiton("pe", "dve", stt_m[tcc - 2])
                    for nh in range(2):
                        R("pe", te.matmul(
                            ps_big[:, half + nh * 512:half + (nh + 1) * 512],
                            kvro[:, cc, tcc * 128:(tcc + 1) * 128],
                            wo_sb[:, db, nh * 512:(nh + 1) * 512],
                            start=(cc == 0), stop=False))
                    last["pe"].then_inc(done["pe"], 1)
                    cnt["pe"] += 1
                    last["pe"] = None
                    wo_guard[db] = cnt["pe"]
                te.matmul(ps_big[:, half:half + 512], onesr[0:1, :],
                          boutr[0:1, 0:512], start=False, stop=True)
                R("pe", te.matmul(ps_big[:, half + 512:half + 1024],
                                  onesr[0:1, :], boutr[0:1, 512:1024],
                                  start=False, stop=True))
                wmm = mark("pe")
                waiton("dve", "pe", wmm)
                R("dve", ve.scalar_tensor_tensor(
                    x1[:, tcc, :], ps_big[:, half:half + D], 0.0,
                    x_tm[:, tcc, :], aop.add, aop.add,
                    accum_out=rowsum[:, sub * 2:sub * 2 + 1]))
                stt_m[tcc] = mark("dve")
                waiton("act", "dve", stt_m[tcc])
                R("act", ac.activation(f32a[:, 2 * D:3 * D], x1[:, tcc, :], AF.Square,
                                       accum_out=rowsum[:, sub * 2 + 1:sub * 2 + 2]))
                fence(["act"], ["dve"])
                ve.tensor_scalar(lnsc[:, lo:lo + 1], rowsum[:, sub * 2:sub * 2 + 1],
                                 1.0 / D, None, aop.mult)
                ve.tensor_scalar(lnsc[:, lo + 1:lo + 2],
                                 rowsum[:, sub * 2 + 1:sub * 2 + 2],
                                 1.0 / D, None, aop.mult)
                ve.tensor_tensor(lnsc[:, lo + 2:lo + 3], lnsc[:, lo:lo + 1],
                                 lnsc[:, lo:lo + 1], aop.mult)
                R("dve", ve.tensor_tensor(lnsc[:, lo + 1:lo + 2], lnsc[:, lo + 1:lo + 2],
                                          lnsc[:, lo + 2:lo + 3], aop.subtract))
                fence(["dve"], ["act"])
                ac.activation(lnsc[:, lo + 3:lo + 4], lnsc[:, lo + 1:lo + 2], AF.Ln,
                              bias=1e-5, scale=1.0)
                R("act", ac.activation(lnsc[:, lo + 4:lo + 5], lnsc[:, lo + 3:lo + 4],
                                       AF.Exp, bias=0.0, scale=-0.5))
                fence(["act"], ["dve"])
                ve.tensor_tensor(lnsc[:, lo + 5:lo + 6], lnsc[:, lo:lo + 1],
                                 lnsc[:, lo + 4:lo + 5], aop.mult)
                R("dve", ve.tensor_scalar(lnsc[:, lo + 5:lo + 6], lnsc[:, lo + 5:lo + 6],
                                          -1.0, None, aop.mult))
                fence(["dve"], ["act"])
                R("act", ac.activation(x1[:, tcc, :], x1[:, tcc, :], AF.Identity,
                                       bias=lnsc[:, lo + 5:lo + 6],
                                       scale=lnsc[:, lo + 4:lo + 5]))
                apm = mark("act")
                yst = sub * D
                nb = D // 16
                waiton("dve", "act", apm)
                ve.tensor_reduce(qbm[:, 0:nb],
                                 x1[:, tcc, :].rearrange("p (a b) -> p a b", b=16),
                                 AX.X, aop.max, apply_absolute_value=True)
                ve.tensor_scalar(qsc[:, 0:nb], qbm[:, 0:nb], 1.0 / 6.0, 1e-12,
                                 aop.mult, aop.max)
                ve.reciprocal(qrs[:, 0:nb], qsc[:, 0:nb])
                rsb = bass.AP(qrs, 0, [[KI // 16, 128], [1, nb], [0, 16]])
                R("dve", ve.tensor_tensor(
                    f32a[:, yst:yst + D].rearrange("p (a b) -> p a b", b=16),
                    x1[:, tcc, :].rearrange("p (a b) -> p a b", b=16),
                    rsb, aop.mult))
                fence(["dve"], ["act"])
                stair(D, qrs, qsc, f32a[:, yst:yst + D], None, afp32=True,
                      out_flat=bass.AP(xfw, tcc * D,
                                       [[DCH * T, 128], [16, D // 16], [1, 16]]))
                xq_dve[tcc] = mark("dve")
            if "x1" in tap_d:
                tap_wait()
                dma("pool", "dmp", tap_d["x1"][bb, mt], x1[:, :, :])

            for tcc in range(TC):
                waiton("pe", "dve", xq_dve[tcc])
                for cc in range(DCH):
                    R("pe", te.transpose(
                        ps_t[:, cc * 128:(cc + 1) * 128],
                        bass.AP(xfw, tcc * D + cc * 128, [[DCH * T, 128], [1, 128]]),
                        ident[:, :]))
                fence(["pe"], ["dve"])
                R("dve", ve.tensor_copy(
                    bass.AP(q_rot, tcc * 128, [[DCH * T, 128], [T, DCH], [1, 128]]),
                    bass.AP(ps_t, 0, [[1024, 128], [128, DCH], [1, 128]])))
                fence(["dve"], ["pe"])
            if "x1q" in tap_d:
                for tcc in range(TC):
                    dma("pool", "dmp", tap_d["x1q"][bb, mt, tcc],
                        bass.AP(q_rot, tcc * 128, [[DCH * T, 128], [T, DCH], [1, 128]]))

            pe_g = [0] * (NKB + 2)
            pe_u = [0] * (NKB + 2)
            act_s = [0] * (NKB + 2)
            dve_m = [0] * (NKB + 2)
            pe_h = [0] * (NKB + 2)
            dve_e = [0] * (NKB + 2)
            pool_w = [0] * (NKB + 2)
            for ncb in range(NKB + 2):
                par = ncb % 2
                if ncb < NKB:
                    if ncb >= 2:
                        waiton("pool", "pe", pe_g[ncb - 2])
                        waiton("pool", "pe", pe_u[ncb - 2])
                    dma("pool", "dw", wg_sb[:, par, :],
                        bass.AP(wgq_d.tensor, ncb * 128,
                                [[KI, 128], [128 * KI, DCH], [1, 128]]))
                    dma("pool", "dw", wu_sb[:, par, :],
                        bass.AP(wuq_d.tensor, ncb * 128,
                                [[KI, 128], [128 * KI, DCH], [1, 128]]))
                    pool_w[ncb] = dsem["dw"][1]
                    te.wait_ge(dsem["dw"][0], pool_w[ncb])
                    if ncb >= 2:
                        waiton("pe", "act", act_s[ncb - 2])
                    for cc in range(DCH):
                        R("pe", te.matmul(ps_big[:, par * 512:par * 512 + 512],
                                          wg_sb[:, par, cc * 128:(cc + 1) * 128],
                                          q_rot[:, cc, :],
                                          start=(cc == 0), stop=(cc == DCH - 1)))
                    pe_g[ncb] = mark("pe")
                    waiton("act", "pe", pe_g[ncb])
                    if ncb >= 2:
                        waiton("act", "dve", dve_m[ncb - 2])
                    with nodrain():
                        R("act", ac.activation(sg2[:, par, :],
                                               ps_big[:, par * 512:par * 512 + 512],
                                               AF.Silu))
                    act_s[ncb] = mark("act")
                    if ncb >= 2:
                        waiton("pe", "dve", dve_m[ncb - 2])
                    for cc in range(DCH):
                        R("pe", te.matmul(ps_big[:, 1024 + par * 512:1024 + par * 512 + 512],
                                          wu_sb[:, par, cc * 128:(cc + 1) * 128],
                                          q_rot[:, cc, :],
                                          start=(cc == 0), stop=(cc == DCH - 1)))
                    pe_u[ncb] = mark("pe")
                    waiton("dve", "act", act_s[ncb])
                    waiton("dve", "pe", pe_u[ncb])
                    if ncb >= 2:
                        waiton("dve", "pe", pe_h[ncb - 2])
                    with nodrain():
                        R("dve", ve.tensor_tensor(hf2[:, par, :],
                                                  ps_big[:, 1024 + par * 512:1024 + par * 512 + 512],
                                                  sg2[:, par, :], aop.mult))
                    dve_m[ncb] = mark("dve")
                if 0 <= ncb - 1 < NKB:
                    nch = ncb - 1
                    parh = nch % 2
                    waiton("pe", "dve", dve_m[nch])
                    if nch >= 2:
                        waiton("pe", "dve", dve_e[nch - 2])
                    pto = parh * 512
                    for tcc in range(TC):
                        R("pe", te.transpose(ps_t[:, pto + tcc * 128:pto + (tcc + 1) * 128],
                                             hf2[:, parh, tcc * 128:(tcc + 1) * 128],
                                             ident[:, :]))
                    pe_h[nch] = mark("pe")
                    waiton("dve", "pe", pe_h[nch])
                    with nodrain():
                        R("dve", ve.tensor_copy(
                            bass.AP(h_tm, nch * 128, [[TC * KI, 128], [KI, TC], [1, 128]]),
                            bass.AP(ps_t, pto, [[1024, 128], [128, TC], [1, 128]])))
                    dve_e[nch] = mark("dve")
                    gq_last = (nch == NKB - 1)
                    if QFOLD and (nch % 4 == 3 or (gq_last and nch % 4 != 3)):
                        ng = (nch % 4) + 1
                        c0 = (nch - ng + 1) * 128
                        nbk = ng * 8
                        kw = ng * 128
                        hview = bass.AP(h_tm, c0,
                                        [[TC * KI, 128], [KI, TC], [16, nbk], [1, 16]])
                        R("dve", ve.tensor_reduce(qbm[:, 0:TC * nbk], hview, AX.X,
                                                  aop.max, apply_absolute_value=True))
                        ve.tensor_scalar(qsc[:, 0:TC * nbk], qbm[:, 0:TC * nbk],
                                         1.0 / 6.0, 1e-12, aop.mult, aop.max)
                        R("dve", ve.reciprocal(qrs[:, 0:TC * nbk], qsc[:, 0:TC * nbk]))
                        rsb4 = bass.AP(qrs, 0, [[KI // 16, 128], [nbk, TC], [1, nbk], [0, 16]])
                        R("dve", ve.tensor_tensor(
                            bass.AP(qc1, 0, [[KI, 128], [kw, TC], [16, nbk], [1, 16]]),
                            hview, rsb4, aop.mult))
                        fence(["dve"], ["act"])
                        stair(TC * kw, None, None, qc1[:, 0:TC * kw], None, final3=(
                            bass.AP(h_tm, c0, [[TC * KI, 128], [KI, TC], [16, nbk], [1, 16]]),
                            bass.AP(qc2, 0, [[KI, 128], [kw, TC], [16, nbk], [1, 16]]),
                            bass.AP(qsc, 0, [[KI // 16, 128], [nbk, TC], [1, nbk], [0, 16]])))
            if "htm" in tap_d:
                tap_wait()
                dma("pool", "dmp", tap_d["htm"][bb, mt], h_tm[:, :, :])

            dve_stt = 0
            wd_guard = {}
            for tp in range(TC // 2):
                if tp == 0:
                    if last.get("dve") is not None:
                        fence(["dve"], ["pe"])
                    else:
                        waiton("pe", "dve", cnt["dve"])
                else:
                    waiton("pe", "dve", dve_stt)
                if TPAR:
                    evq = []
                    for sub in range(2):
                        tcc = tp * 2 + sub
                        for k0 in range(0, NKB, 4):
                            kn = min(4, NKB - k0)
                            half = len(evq) % 2
                            if len(evq) >= TDEPTH:
                                waiton("pe", "dve", evq[-TDEPTH])
                            for kk in range(kn):
                                R("pe", te.transpose(
                                    ps_t[:, half * 512 + kk * 128:half * 512 + (kk + 1) * 128],
                                    h_tm[:, tcc, (k0 + kk) * 128:(k0 + kk + 1) * 128],
                                    ident[:, :]))
                            ptb = mark("pe")
                            waiton("dve", "pe", ptb)
                            with nodrain():
                                R("dve", ve.tensor_copy(
                                    bass.AP(hq_f, k0 * 256 + sub * 128,
                                            [[NKB * 256, 128], [256, kn], [1, 128]]),
                                    bass.AP(ps_t, half * 512, [[1024, 128], [128, kn], [1, 128]])))
                            evq.append(mark("dve"))
                    waiton("pe", "dve", evq[-1])
                else:
                    for sub in range(2):
                        tcc = tp * 2 + sub
                        for k0 in range(0, NKB, 8):
                            kn = min(8, NKB - k0)
                            for kk in range(kn):
                                R("pe", te.transpose(ps_t[:, kk * 128:(kk + 1) * 128],
                                                     h_tm[:, tcc, (k0 + kk) * 128:(k0 + kk + 1) * 128],
                                                     ident[:, :]))
                            fence(["pe"], ["dve"])
                            R("dve", ve.tensor_copy(
                                bass.AP(hq_f, k0 * 256 + sub * 128,
                                        [[NKB * 256, 128], [256, kn], [1, 128]]),
                                bass.AP(ps_t, 0, [[1024, 128], [128, kn], [1, 128]])))
                            fence(["dve"], ["pe"])
                for kk in range(NKB):
                    db = kk % 3
                    if db in wd_guard:
                        waiton("pool", "pe", wd_guard[db])
                    dma("pool", "dw", wd_sb[:, db, :],
                        bass.AP(wdq_d.tensor, kk * 128 * D, [[D, 128], [1, D]]))
                    if PREFETCH:
                        te.wait_ge(dsem["dw"][0], dsem["dw"][1])
                    else:
                        dma_wait("pool", "dw")
                        pool_mark()
                        te.wait_ge(done["pool"], cnt["pool"])
                    for sub in range(2):
                        for nh in range(2):
                            R("pe", te.matmul(
                                ps_big[:, sub * 1024 + nh * 512:sub * 1024 + (nh + 1) * 512],
                                hq_f[:, kk, sub * 128:(sub + 1) * 128],
                                wd_sb[:, db, nh * 512:(nh + 1) * 512],
                                start=(kk == 0), stop=(kk == NKB - 1)))
                    last["pe"].then_inc(done["pe"], 1)
                    cnt["pe"] += 1
                    last["pe"] = None
                    wd_guard[db] = cnt["pe"]
                mm_done = cnt["pe"]
                waiton("dve", "pe", mm_done)
                for sub in range(2):
                    tcc = tp * 2 + sub
                    R("dve", ve.scalar_tensor_tensor(
                        x1[:, tcc, :], ps_big[:, sub * 1024:sub * 1024 + D], 0.0,
                        x1[:, tcc, :], aop.add, aop.add,
                        accum_out=rowsum[:, sub * 2:sub * 2 + 1]))
                dve_stt = mark("dve")
                for sub in range(2):
                    tcc = tp * 2 + sub
                    lo = sub * 6
                    if sub == 0:
                        waiton("act", "dve", dve_stt)
                    R("act", ac.activation(f32a[:, sub * D:(sub + 1) * D],
                                           x1[:, tcc, :], AF.Square,
                                           accum_out=rowsum[:, sub * 2 + 1:sub * 2 + 2]))
                    fence(["act"], ["dve"])
                    ve.tensor_scalar(lnsc[:, lo:lo + 1], rowsum[:, sub * 2:sub * 2 + 1],
                                     1.0 / D, None, aop.mult)
                    ve.tensor_scalar(lnsc[:, lo + 1:lo + 2],
                                     rowsum[:, sub * 2 + 1:sub * 2 + 2],
                                     1.0 / D, None, aop.mult)
                    ve.tensor_tensor(lnsc[:, lo + 2:lo + 3], lnsc[:, lo:lo + 1],
                                     lnsc[:, lo:lo + 1], aop.mult)
                    R("dve", ve.tensor_tensor(lnsc[:, lo + 1:lo + 2], lnsc[:, lo + 1:lo + 2],
                                              lnsc[:, lo + 2:lo + 3], aop.subtract))
                    fence(["dve"], ["act"])
                    ac.activation(lnsc[:, lo + 3:lo + 4], lnsc[:, lo + 1:lo + 2], AF.Ln,
                                  bias=1e-5, scale=1.0)
                    R("act", ac.activation(lnsc[:, lo + 4:lo + 5], lnsc[:, lo + 3:lo + 4],
                                           AF.Exp, bias=0.0, scale=-0.5))
                    fence(["act"], ["dve"])
                    ve.tensor_tensor(lnsc[:, lo + 5:lo + 6], lnsc[:, lo:lo + 1],
                                     lnsc[:, lo + 4:lo + 5], aop.mult)
                    R("dve", ve.tensor_scalar(lnsc[:, lo + 5:lo + 6], lnsc[:, lo + 5:lo + 6],
                                              -1.0, None, aop.mult))
                    fence(["dve"], ["act"])
                    R("act", ac.activation(x1[:, tcc, :], x1[:, tcc, :], AF.Identity,
                                           bias=lnsc[:, lo + 5:lo + 6],
                                           scale=lnsc[:, lo + 4:lo + 5]))
                    apm = mark("act")
                    if SPSTORE:
                        waiton("sp", "act", apm)
                        dma("sp", "dout",
                            out_d[bb, t0 + tcc * 128:t0 + (tcc + 1) * 128, :],
                            x1[:, tcc, :])
                    else:
                        waiton("pool", "act", apm)
                        dma("pool", "dmp",
                            out_d[bb, t0 + tcc * 128:t0 + (tcc + 1) * 128, :],
                            x1[:, tcc, :])
            if SPSTORE:
                dma_wait("sp", "dout")
                sp_mark()
                waiton("act", "sp", cnt["sp"])
                waiton("dve", "sp", cnt["sp"])
            else:
                dma_wait("pool", "dmp")
                pool_mark()
                waiton("act", "pool", cnt["pool"])
                waiton("dve", "pool", cnt["pool"])

            ve.wait_ge(done["pe"], cnt["pe"])
            ve.tensor_copy(vh_km[:, 0, :], vh_km[:, WPM, :])
            R("dve", ve.tensor_copy(
                bass.AP(vh_f, 0, [[DCH * (WPM + 1) * WIN, 128],
                                  [(WPM + 1) * WIN, DCH], [1, WIN]]),
                bass.AP(vh_f, WPM * WIN, [[DCH * (WPM + 1) * WIN, 128],
                                          [(WPM + 1) * WIN, DCH], [1, WIN]])))
            fence(["dve"], ["pe", "act", "pool", "sp"])

    if SPSTORE:
        dma_wait("sp", "dout")
    dma_wait("pool", "dmp")
    po.sem_inc(done["pool"], 1)

    for cm in reversed(ctxs):
        cm.__exit__(None, None, None)
    return nc


def kernel(L0_attn_x, L0_attn_Wq, L0_attn_Wkv, L0_attn_Wproj, L0_attn_Wout,
           L0_attn_bout, L0_attn_ln_local_g, L0_attn_ln_local_b,
           L0_attn_ln_global_g, L0_attn_ln_global_b,
           L0_ffn_w_gate, L0_ffn_w_up, L0_ffn_w_down):
    x = np.ascontiguousarray(np.asarray(L0_attn_x, dtype=np.float32))
    lng = np.ascontiguousarray(np.stack(
        [np.asarray(L0_attn_ln_local_g, np.float32),
         np.asarray(L0_attn_ln_local_b, np.float32),
         np.asarray(L0_attn_ln_global_g, np.float32),
         np.asarray(L0_attn_ln_global_b, np.float32)], axis=0))
    common = {
        "wq": np.ascontiguousarray(np.asarray(L0_attn_Wq, np.float32)),
        "wkv": np.ascontiguousarray(np.asarray(L0_attn_Wkv, np.float32)),
        "wproj": np.ascontiguousarray(np.asarray(L0_attn_Wproj, np.float32)[:, 0]),
        "wout": np.ascontiguousarray(np.asarray(L0_attn_Wout, np.float32)),
        "bout": np.ascontiguousarray(np.asarray(L0_attn_bout, np.float32)),
        "lng": lng,
        "wgate": np.ascontiguousarray(np.asarray(L0_ffn_w_gate, np.float32)),
        "wup": np.ascontiguousarray(np.asarray(L0_ffn_w_up, np.float32)),
        "wdown": np.ascontiguousarray(np.asarray(L0_ffn_w_down, np.float32)),
    }
    n_cores = 8
    bpc = x.shape[0] // n_cores
    nc = build_program(B_CORE=bpc, N=x.shape[1])
    in_maps = [dict(common, x=np.ascontiguousarray(x[i * bpc:(i + 1) * bpc]))
               for i in range(n_cores)]
    res = run_bass_kernel_spmd(nc, in_maps, core_ids=list(range(n_cores)))
    return np.concatenate([np.asarray(r["out"]) for r in res.results], axis=0)


# version 23
# speedup vs baseline: 1.0300x; 1.0300x over previous
import math
import numpy as np

import concourse.bass as bass
import concourse.mybir as mybir
from concourse.alu_op_type import AluOpType as aop
from concourse.bass_utils import run_bass_kernel_spmd

F32 = mybir.dt.float32
BF16 = mybir.dt.bfloat16
AF = mybir.ActivationFunctionType
AX = mybir.AxisListType

D = 1024
H = 16
DH = 64
DCH = D // 128
KI = 2816
NKB = KI // 128
WIN = 128
SEGL = 16
SPW = WIN // SEGL
C_SOFT = 12.0





QFOLD = True
SPSTORE = True
PREFETCH = True
TPAR = True
TDEPTH = 1


def build_program(B_CORE=2, N=4096, T=512, taps=()):
    NW = N // WIN
    WPM = T // WIN
    MPB = NW // WPM
    NSEG = N // SEGL
    LMT = (NSEG + 127) // 128
    TC = T // 128
    taps = set(taps)

    nc = bass.Bass("TRN2", target_bir_lowering=False, detect_race_conditions=False)

    last = {}
    drain_on = [True]

    class nodrain:
        def __enter__(self):
            drain_on.append(False)

        def __exit__(self, *a):
            drain_on.pop()

    class EngProxy:
        _SKIP = {"wait_ge", "sem_inc", "wait_op", "nop", "engine_nop", "drain",
                 "sem_clear"}
        _NODRAIN = {"dma_start", "dma_start_transpose"}

        def __init__(self, eng, name):
            self._eng = eng
            self._name = name

        def __getattr__(self, m):
            f = getattr(self._eng, m)
            if not callable(f) or m in self._SKIP:
                return f
            name = self._name
            eng = self._eng
            drainable = name in ("dve", "act") and m not in self._NODRAIN

            def wrap(*a, **k):
                r = f(*a, **k)
                if hasattr(r, "then_inc"):
                    last[name] = r
                    if drainable and a and isinstance(a[0], bass.AP):
                        out = a[0]
                        try:
                            free = 1
                            for s in out.shape[1:]:
                                free *= s
                            nbytes = free * mybir.dt.size(out.dtype)
                        except Exception:
                            nbytes = 0
                        if nbytes < 2048 and drain_on[-1]:
                            eng.drain()
                return r
            return wrap

    te = EngProxy(nc.tensor, "pe")
    ac = EngProxy(nc.scalar, "act")
    ve = EngProxy(nc.vector, "dve")
    po = EngProxy(nc.gpsimd, "pool")
    sp = EngProxy(nc.sync, "sp")

    x_in = nc.dram_tensor("x", [B_CORE, N, D], F32, kind="ExternalInput").ap()
    wq_in = nc.dram_tensor("wq", [D, D], F32, kind="ExternalInput").ap()
    wkv_in = nc.dram_tensor("wkv", [D, D], F32, kind="ExternalInput").ap()
    wproj_in = nc.dram_tensor("wproj", [DH], F32, kind="ExternalInput").ap()
    wout_in = nc.dram_tensor("wout", [D, D], F32, kind="ExternalInput").ap()
    bout_in = nc.dram_tensor("bout", [D], F32, kind="ExternalInput").ap()
    lng_in = nc.dram_tensor("lng", [4, DH], F32, kind="ExternalInput").ap()
    wg_in = nc.dram_tensor("wgate", [KI, D], F32, kind="ExternalInput").ap()
    wu_in = nc.dram_tensor("wup", [KI, D], F32, kind="ExternalInput").ap()
    wd_in = nc.dram_tensor("wdown", [D, KI], F32, kind="ExternalInput").ap()
    out_d = nc.dram_tensor("out", [B_CORE, N, D], F32, kind="ExternalOutput").ap()
    wqkv_d = nc.dram_tensor("wqkv_bf", [D, 2 * D], BF16).ap()
    woutb_d = nc.dram_tensor("wout_bf", [D, D], BF16).ap()
    wgq_d = nc.dram_tensor("wgq_t", [D, KI], BF16).ap()
    costab_d = nc.dram_tensor("costab_d", [128, N], BF16).ap()
    sintab_d = nc.dram_tensor("sintab_d", [128, N], BF16).ap()
    wuq_d = nc.dram_tensor("wuq_t", [D, KI], BF16).ap()
    wdq_d = nc.dram_tensor("wdq_t", [KI, D], BF16).ap()

    tap_d = {}
    def def_tap(name, shape, dt=F32):
        if name in taps:
            tap_d[name] = nc.dram_tensor("tap_" + name, list(shape), dt,
                                         kind="ExternalOutput").ap()
    def_tap("xtm", [B_CORE, MPB, 128, TC, D], dt=BF16)
    def_tap("xf", [B_CORE, MPB, 128, DCH, T], dt=BF16)
    def_tap("qpre", [B_CORE, MPB, 128, DCH, T], dt=BF16)
    def_tap("qrot", [B_CORE, MPB, 128, DCH, T], dt=BF16)
    def_tap("kvkm", [B_CORE, MPB, 128, WPM, H * DH], dt=BF16)
    def_tap("vhkm", [B_CORE, MPB, 128, WPM, H * DH], dt=BF16)
    def_tap("gkm", [B_CORE, MPB, 128, LMT, H * DH], dt=BF16)
    def_tap("vgkm", [B_CORE, MPB, 128, LMT, H * DH], dt=BF16)
    def_tap("pscore", [B_CORE, MPB, H, T], dt=F32)
    def_tap("attnf", [B_CORE, MPB, 128, DCH, T], dt=BF16)
    def_tap("x1", [B_CORE, MPB, 128, TC, D], dt=F32)
    def_tap("x1q", [B_CORE, MPB, TC, 128, D], dt=BF16)
    def_tap("htm", [B_CORE, MPB, 128, TC, KI], dt=BF16)
    def_tap("cost", [B_CORE, MPB, 128, T], dt=BF16)

    ctxs = []
    def mk(cm):
        h = cm.__enter__()
        ctxs.append(cm)
        return h

    done = {e: mk(nc.semaphore("dn_" + e)) for e in ("pe", "act", "dve", "pool", "sp")}
    cnt = {e: 0 for e in done}
    dsem = {s: [mk(nc.semaphore("dm_" + s)), 0] for s in ("din", "dw", "dout", "dm", "dmp")}

    def R(eng, instr):
        return instr

    ENG = {"pe": te, "act": ac, "dve": ve, "pool": po, "sp": sp}

    def fence(producers, consumers):
        for pe_ in producers:
            ins = last.get(pe_)
            assert ins is not None, f"fence: nothing emitted on {pe_}"
            ins.then_inc(done[pe_], 1)
            cnt[pe_] += 1
            last[pe_] = None
        for ce in consumers:
            for pe_ in producers:
                ENG[ce].wait_ge(done[pe_], cnt[pe_])

    def mark(eng):
        ins = last.get(eng)
        assert ins is not None, f"mark: nothing emitted on {eng}"
        ins.then_inc(done[eng], 1)
        cnt[eng] += 1
        last[eng] = None
        return cnt[eng]

    def waiton(consumer, prod, count):
        if count > 0:
            ENG[consumer].wait_ge(done[prod], count)

    def dma(eng, s, out, in_):
        ENG[eng].dma_start(out=out, in_=in_).then_inc(dsem[s][0], 16)
        dsem[s][1] += 16

    def dma_wait(eng, s):
        ENG[eng].wait_ge(dsem[s][0], dsem[s][1])

    def tap_wait():
        for e in ("dve", "act", "pe", "sp"):
            po.wait_ge(done[e], cnt[e])

    def sp_mark():
        sp.sem_inc(done["sp"], 1)
        cnt["sp"] += 1

    def pool_mark():
        po.sem_inc(done["pool"], 1)
        cnt["pool"] += 1

    CONSTS = [0.0, 1.0, 0.125, -float(C_SOFT), 1e-5, -0.5]
    cpool = mk(nc.sbuf_tensor("cpool", [128, len(CONSTS)], F32))
    for ci, cv in enumerate(CONSTS):
        nc.const_aps.register(F32, float(cv), cpool[:, ci:ci + 1])

    ident = mk(nc.sbuf_tensor("ident", [128, 128], BF16))
    p32 = mk(nc.sbuf_tensor("p32", [128, 128], BF16))
    negi = mk(nc.sbuf_tensor("negi", [128, 128], BF16))
    ucur = mk(nc.sbuf_tensor("ucur", [128, 128], BF16))
    useg = mk(nc.sbuf_tensor("useg", [128, SPW], BF16))
    segf = mk(nc.sbuf_tensor("segf", [128, 256], BF16))
    onesr = mk(nc.sbuf_tensor("onesr", [1, 128], BF16))
    onesf = mk(nc.sbuf_tensor("onesf", [1, 128], F32))
    boutr = mk(nc.sbuf_tensor("boutr", [1, D], BF16))
    wpsh = mk(nc.sbuf_tensor("wpsh", [128, 32], BF16))
    lngt = mk(nc.sbuf_tensor("lngt", [128, 4 * DH], BF16))
    iotap = mk(nc.sbuf_tensor("iotap", [128, 1], F32))
    iotaf = mk(nc.sbuf_tensor("iotaf", [128, 128], F32))
    invf = mk(nc.sbuf_tensor("invf", [1, 32], F32))
    invfs = mk(nc.sbuf_tensor("invfs", [1, 32], F32))
    tabsc = mk(nc.sbuf_tensor("tabsc", [128, 128], F32))
    costab = mk(nc.sbuf_tensor("costab", [128, T], BF16))
    sintab = mk(nc.sbuf_tensor("sintab", [128, T], BF16))

    wq_sb = mk(nc.sbuf_tensor("wq_sb", [128, 2, DCH * 128], BF16))
    wo_sb = mk(nc.sbuf_tensor("wo_sb", [128, 2, D], BF16))
    wg_sb = mk(nc.sbuf_tensor("wg_sb", [128, 2, DCH * 128], BF16))
    wu_sb = mk(nc.sbuf_tensor("wu_sb", [128, 2, DCH * 128], BF16))
    sg2 = mk(nc.sbuf_tensor("sg2", [128, 2, 512], BF16))
    hf2 = mk(nc.sbuf_tensor("hf2", [128, 2, 512], BF16))
    wd_sb = mk(nc.sbuf_tensor("wd_sb", [128, 3, D], BF16))

    x_tm = mk(nc.sbuf_tensor("x_tm", [128, TC, D], BF16))
    xfw = mk(nc.sbuf_tensor("xfw", [128, DCH, T], BF16))
    q_rot = mk(nc.sbuf_tensor("q_rot", [128, DCH, T], BF16))
    kvro = mk(nc.sbuf_tensor("kvro", [128, DCH, T], BF16))
    kv_km = mk(nc.sbuf_tensor("kv_km", [128, WPM, H * DH], BF16))
    vh_km = mk(nc.sbuf_tensor("vh_km", [128, WPM + 1, H * DH], BF16))
    vh_f = mk(nc.sbuf_tensor("vh_f", [128, DCH, (WPM + 1) * WIN], BF16))
    gkm = mk(nc.sbuf_tensor("gkm", [128, LMT, H * DH], BF16))
    vg_km = mk(nc.sbuf_tensor("vg_km", [128, LMT, H * DH], BF16))
    vg_f = mk(nc.sbuf_tensor("vg_f", [128, DCH, LMT * 128], BF16))
    pscore = mk(nc.sbuf_tensor("pscore", [H, T], F32))
    pkv = mk(nc.sbuf_tensor("pkv", [H, T], BF16))
    pkv_km = mk(nc.sbuf_tensor("pkv_km", [128, TC, H], BF16))
    pt_loc = mk(nc.sbuf_tensor("pt_loc", [128, WPM + 1, 256], BF16))
    pt_g = mk(nc.sbuf_tensor("pt_g", [128, LMT, T], BF16))
    p_sb = mk(nc.sbuf_tensor("p_sb", [128, 1024], BF16))
    denom = mk(nc.sbuf_tensor("denom", [128, 6], F32))

    x1 = mk(nc.sbuf_tensor("x1", [128, TC, D], F32))
    f32a = mk(nc.sbuf_tensor("f32a", [128, 3072], F32))
    qa = mk(nc.sbuf_tensor("qa", [128, KI], BF16))
    qs = mk(nc.sbuf_tensor("qs", [128, KI], BF16))
    qc1 = mk(nc.sbuf_tensor("qc1", [128, KI], BF16))
    qc2 = mk(nc.sbuf_tensor("qc2", [128, KI], BF16))
    qbm = mk(nc.sbuf_tensor("qbm", [128, KI // 16], F32))
    qsc = mk(nc.sbuf_tensor("qsc", [128, KI // 16], F32))
    qrs = mk(nc.sbuf_tensor("qrs", [128, KI // 16], F32))
    qscb = mk(nc.sbuf_tensor("qscb", [128, KI // 16], BF16))
    qrsb = mk(nc.sbuf_tensor("qrsb", [128, KI // 16], BF16))
    h_tm = mk(nc.sbuf_tensor("h_tm", [128, TC, KI], BF16))
    hq_tm = mk(nc.sbuf_tensor("hq_tm", [128, 1024], BF16))
    hq_f = mk(nc.sbuf_tensor("hq_f", [128, NKB, 256], BF16))
    stats = mk(nc.sbuf_tensor("stats", [128, 8 * 4 * H], F32))
    lnsc = mk(nc.sbuf_tensor("lnsc", [128, 12], F32))
    rowsum = mk(nc.sbuf_tensor("rowsum", [128, 4], F32))

    ps_big = mk(nc.psum_tensor("ps_big", [128, 2048], F32))
    ps_s = mk(nc.psum_tensor("ps_s", [128, 1024], F32))
    ps_t = mk(nc.psum_tensor("ps_t", [128, 1024], BF16))
    ps_p = mk(nc.psum_tensor("ps_p", [128, 512], F32))

    for ci, cv in enumerate(CONSTS):
        ve.memset(cpool[:, ci:ci + 1], float(cv))
    ve.memset(gkm[:, :, :], 0.0)
    ve.memset(vg_km[:, :, :], 0.0)
    ve.memset(pt_g[:, :, :], 0.0)
    ve.memset(onesr[0:1, :], 1.0)
    R("dve", ve.memset(onesf[0:1, :], 1.0))
    fence(["dve"], ["pool", "sp", "act", "pe"])

    po.iota(iotaf[:, :], [[1, 128]], channel_multiplier=0,
            allow_small_or_imprecise_dtypes=True)
    R("pool", po.iota(iotap[:, :], [[0, 1]], channel_multiplier=1,
                      allow_small_or_imprecise_dtypes=True))
    fence(["pool"], ["dve"])

    ipb = bass.AP(iotap, 0, [[1, 128], [0, 128]])
    ve.tensor_tensor(ident[:, :], iotaf[:, :], ipb, aop.is_equal)
    ve.tensor_scalar(negi[:, :], ident[:, :], -30000.0, None, aop.mult)
    ve.tensor_scalar(tabsc[:, 0:1], iotap[:, :], 63.5, 64.0, aop.is_gt, aop.mult)
    ve.tensor_tensor(tabsc[:, 1:2], iotap[:, :], tabsc[:, 0:1], aop.subtract)
    ve.tensor_scalar(tabsc[:, 2:3], tabsc[:, 1:2], 31.5, 64.0, aop.is_gt, aop.mult)
    ve.tensor_tensor(tabsc[:, 3:4], iotap[:, :], tabsc[:, 2:3], aop.subtract)
    ve.tensor_scalar(tabsc[:, 3:4], tabsc[:, 3:4], 32.0, None, aop.add)
    ve.tensor_tensor(p32[:, :], iotaf[:, :],
                     bass.AP(tabsc, 3, [[128, 128], [0, 128]]), aop.is_equal)
    ve.tensor_tensor(ucur[:, :], iotaf[:, :], ipb, aop.is_gt)
    ve.tensor_scalar(tabsc[:, 8:8 + SPW], iotaf[:, 0:SPW], 16.0, 15.0, aop.mult, aop.add)
    ve.tensor_tensor(useg[:, :], tabsc[:, 8:8 + SPW],
                     bass.AP(iotap, 0, [[1, 128], [0, SPW]]), aop.is_ge)
    ve.tensor_scalar(tabsc[:, 4:5], iotap[:, :], 15.5, None, aop.is_gt)
    for k in range(2, 8):
        ve.scalar_tensor_tensor(tabsc[:, 4:5], iotap[:, :], 16.0 * k - 0.5,
                                tabsc[:, 4:5], aop.is_gt, aop.add)
    ve.tensor_scalar(tabsc[:, 5:6], tabsc[:, 4:5], 128.0, None, aop.add)
    fence(["dve"], ["pool"])
    po.iota(f32a[:, 0:256], [[1, 256]], channel_multiplier=0,
            allow_small_or_imprecise_dtypes=True).then_inc(done["pool"], 1)
    cnt["pool"] += 1
    ve.wait_ge(done["pool"], cnt["pool"])
    R("dve", ve.tensor_tensor(segf[:, :], f32a[:, 0:256],
                              bass.AP(tabsc, 5, [[128, 128], [0, 256]]), aop.is_equal))
    fence(["dve"], ["act", "pe", "sp", "pool"])

    po.iota(invf[0:1, :], [[1, 32]], channel_multiplier=0,
            allow_small_or_imprecise_dtypes=True).then_inc(done["pool"], 1)
    cnt["pool"] += 1
    ac.wait_ge(done["pool"], cnt["pool"])
    R("act", ac.activation(invf[0:1, :], invf[0:1, :], AF.Exp,
                           bias=0.0, scale=-math.log(10000.0) / 32.0))
    fence(["act"], ["dve", "pe"])

    dma("sp", "dm", f32a[0:1, 0:D], bout_in[None, :])
    dma("sp", "dm", f32a[0:64, 1024:1025], wproj_in[:, None])
    dma("sp", "dm", f32a[64:128, 1025:1026], wproj_in[:, None])
    dma("sp", "dm", f32a[:, 1026:1282],
        bass.AP(lng_in.tensor, 0, [[0, 128], [1, 4 * DH]]))
    dma_wait("sp", "dm")
    sp_mark()
    ve.wait_ge(done["sp"], cnt["sp"])
    ve.tensor_copy(boutr[0:1, :], f32a[0:1, 0:D])
    R("dve", ve.tensor_copy(lngt[:, :], f32a[:, 1026:1282]))
    ve.tensor_scalar(tabsc[:, 6:7], iotap[:, :], 63.5, 16.0, aop.is_gt, aop.add)
    ve.tensor_tensor(tabsc[:, 7:8], iotaf[:, 0:1], iotaf[:, 0:1], aop.subtract)
    ve.tensor_tensor(qa[:, 0:32], iotaf[:, 0:32],
                     bass.AP(tabsc, 6, [[128, 128], [0, 32]]), aop.is_equal)
    ve.tensor_copy(tabsc[0:64, 7:8], f32a[0:64, 1024:1025])
    ve.tensor_copy(tabsc[64:128, 7:8], f32a[64:128, 1025:1026])
    R("dve", ve.tensor_tensor(wpsh[:, :], qa[:, 0:32],
                              bass.AP(tabsc, 7, [[128, 128], [0, 32]]), aop.mult))
    fence(["dve"], ["pe", "sp"])

    for src, dst, cols in ((wq_in, wqkv_d, 0), (wkv_in, wqkv_d, D), (wout_in, woutb_d, 0)):
        for cc in range(DCH):
            dma("sp", "din", f32a[:, 0:D], src[cc * 128:(cc + 1) * 128, :])
            dma_wait("sp", "din")
            sp_mark()
            ve.wait_ge(done["sp"], cnt["sp"])
            R("dve", ve.tensor_copy(qa[:, 0:D], f32a[:, 0:D]))
            fence(["dve"], ["pool"])
            if dst is wqkv_d:
                dma("pool", "dw", bass.AP(dst.tensor, cc * 128 * 2 * D + cols,
                                          [[2 * D, 128], [1, D]]), qa[:, 0:D])
            else:
                dma("pool", "dw", bass.AP(dst.tensor, cc * 128 * D,
                                          [[D, 128], [1, D]]), qa[:, 0:D])
            dma_wait("pool", "dw")
            pool_mark()
            sp.wait_ge(done["pool"], cnt["pool"])

    for wg0 in range(0, N // WIN, 8):
        for wi in range(8):
            ve.tensor_scalar(f32a[0:1, 1024 + wi * 32:1024 + (wi + 1) * 32],
                             invf[0:1, :], float((wg0 + wi) * WIN), None, aop.mult)
        for wi in range(8):
            ve.tensor_copy(f32a[0:1, 1280 + wi * 32:1280 + (wi + 1) * 32],
                           invf[0:1, :])
        R("dve", ve.tensor_copy(f32a[0:1, 1280:1281], f32a[0:1, 1280:1281]))
        fence(["dve"], ["pe"])
        te.matmul(ps_big[:, 0:256], iotaf[0:1, :], f32a[0:1, 1280:1536],
                  start=True, stop=False)
        R("pe", te.matmul(ps_big[:, 0:256], onesf[0:1, :], f32a[0:1, 1024:1280],
                          start=False, stop=True))
        fence(["pe"], ["dve"])
        ve.tensor_copy(f32a[:, 0:256], ps_big[:, 0:256])
        for j in range(9, -1, -1):
            c = float(2.0 ** j * 2.0 * math.pi)
            ve.tensor_scalar(f32a[:, 256:512], f32a[:, 0:256], c, c,
                             aop.is_ge, aop.mult)
            ve.tensor_tensor(f32a[:, 0:256], f32a[:, 0:256], f32a[:, 256:512],
                             aop.subtract)
        ve.tensor_scalar(f32a[:, 256:512], f32a[:, 0:256], float(math.pi),
                         float(2 * math.pi), aop.is_gt, aop.mult)
        ve.tensor_tensor(f32a[:, 512:768], f32a[:, 0:256], f32a[:, 256:512],
                         aop.subtract)
        ve.tensor_scalar(f32a[:, 256:512], f32a[:, 0:256], float(math.pi / 2),
                         None, aop.add)
        ve.tensor_scalar(f32a[:, 768:1024], f32a[:, 256:512], float(math.pi),
                         float(2 * math.pi), aop.is_gt, aop.mult)
        R("dve", ve.tensor_tensor(f32a[:, 768:1024], f32a[:, 256:512],
                                  f32a[:, 768:1024], aop.subtract))
        fence(["dve"], ["act"])
        ac.activation(f32a[:, 0:256], f32a[:, 512:768], AF.Sin)
        R("act", ac.activation(f32a[:, 256:512], f32a[:, 768:1024], AF.Sin))
        fence(["act"], ["dve"])
        R("dve", ve.tensor_copy(qa[:, 0:512], f32a[:, 0:512]))
        fence(["dve"], ["pe"])
        for tt in range(4):
            R("pe", te.transpose(ps_t[:, tt * 128:(tt + 1) * 128],
                                 qa[:, tt * 128:(tt + 1) * 128], ident[:, :]))
        fence(["pe"], ["dve"])
        with nodrain():
            for wi in range(8):
                blk, rowo = divmod(wi * 32, 128)
                wc = (wg0 + wi) * WIN
                ve.tensor_scalar(sintab[0:32, 0:WIN] if False else qc1[0:32, wc % 512:wc % 512 + WIN], ps_t[0:1, 0:1], 0.0, None, aop.mult) if False else None
                ve.tensor_scalar(qc1[0:32, 0:WIN], ps_t[rowo:rowo + 32, blk * 128:(blk + 1) * 128],
                                 -1.0, None, aop.mult)
                ve.tensor_copy(qc1[32:64, 0:WIN], ps_t[rowo:rowo + 32, blk * 128:(blk + 1) * 128])
                ve.tensor_copy(qc1[64:96, 0:WIN], qc1[0:32, 0:WIN]) if False else None
                ve.tensor_copy(qc2[0:32, 0:WIN], ps_t[rowo:rowo + 32, (2 + blk) * 128:(3 + blk) * 128])
                R("dve", ve.tensor_copy(qc2[32:64, 0:WIN], ps_t[rowo:rowo + 32, (2 + blk) * 128:(3 + blk) * 128]))
                fence(["dve"], ["pool"])
                dma("pool", "dw", bass.AP(sintab_d.tensor, wc, [[N, 64], [1, WIN]]),
                    qc1[0:64, 0:WIN])
                dma("pool", "dw", bass.AP(sintab_d.tensor, 64 * N + wc, [[N, 64], [1, WIN]]),
                    qc1[0:64, 0:WIN])
                dma("pool", "dw", bass.AP(costab_d.tensor, wc, [[N, 64], [1, WIN]]),
                    qc2[0:64, 0:WIN])
                dma("pool", "dw", bass.AP(costab_d.tensor, 64 * N + wc, [[N, 64], [1, WIN]]),
                    qc2[0:64, 0:WIN])
                dma_wait("pool", "dw")
                pool_mark()
                ve.wait_ge(done["pool"], cnt["pool"])

    def stair(K, rsb_t, scb_t, ysrc, out_t, afp32=False, out_flat=None, final3=None):
        if afp32:
            ac.activation(qs[:, 0:K], ysrc, AF.Sign)
            R("act", ac.activation(ysrc, ysrc, AF.Abs))
            fence(["act"], ["dve"])
            a = ysrc
        else:
            ac.activation(qa[:, 0:K], ysrc, AF.Abs)
            R("act", ac.activation(qs[:, 0:K], ysrc, AF.Sign))
            fence(["act"], ["dve"])
            a = qa[:, 0:K]
        ve.tensor_scalar(qc1[:, 0:K], a, 0.25, None, aop.is_gt)
        ve.scalar_tensor_tensor(qc2[:, 0:K], a, 0.75, qc1[:, 0:K], aop.is_gt, aop.add)
        ve.scalar_tensor_tensor(qc1[:, 0:K], a, 1.25, qc2[:, 0:K], aop.is_gt, aop.add)
        ve.scalar_tensor_tensor(qc2[:, 0:K], a, 1.75, qc1[:, 0:K],
                                 aop.is_gt, aop.add)
        ve.tensor_scalar(qc1[:, 0:K], qc2[:, 0:K], 0.5, None, aop.mult)
        ve.scalar_tensor_tensor(qc2[:, 0:K], a, 2.5, qc1[:, 0:K], aop.is_gt, aop.add)
        ve.scalar_tensor_tensor(qc1[:, 0:K], a, 3.5, qc2[:, 0:K], aop.is_gt, aop.add)
        ve.tensor_scalar(qc2[:, 0:K], a, 5.0, 2.0, aop.is_gt, aop.mult)
        ve.tensor_tensor(qc1[:, 0:K], qc1[:, 0:K], qc2[:, 0:K], aop.add)
        ve.tensor_tensor(qc2[:, 0:K], qc1[:, 0:K], qs[:, 0:K], aop.mult)
        if final3 is not None:
            dst, srcv, scb = final3
            return R("dve", ve.tensor_tensor(dst, srcv, scb, aop.mult))
        scb = bass.AP(scb_t, 0, [[KI // 16, 128], [1, K // 16], [0, 16]])
        dst = out_flat if out_flat is not None else out_t.rearrange(
            "p (a b) -> p a b", b=16)
        return R("dve", ve.tensor_tensor(
            dst, qc2[:, 0:K].rearrange("p (a b) -> p a b", b=16),
            scb, aop.mult))

    def wquant(src_dram, dst_dram, nrows, kcols):
        nrt = nrows // 128
        kct = kcols // 128
        dst_rowlen = nrows
        for nr in range(nrt):
            sp.wait_ge(done["act"], cnt["act"])
            sp.wait_ge(done["dve"], cnt["dve"])
            dma("sp", "din", f32a[:, 0:kcols], src_dram[nr * 128:(nr + 1) * 128, :])
            dma_wait("sp", "din")
            sp_mark()
            ve.wait_ge(done["sp"], cnt["sp"])
            ve.wait_ge(done["pool"], cnt["pool"])
            nb = kcols // 16
            ve.tensor_reduce(qbm[:, 0:nb],
                             f32a[:, 0:kcols].rearrange("p (a b) -> p a b", b=16),
                             AX.X, aop.max, apply_absolute_value=True)
            ve.tensor_scalar(qsc[:, 0:nb], qbm[:, 0:nb], 1.0 / 6.0, 1e-12,
                             aop.mult, aop.max)
            ve.reciprocal(qrs[:, 0:nb], qsc[:, 0:nb])
            ve.tensor_copy(qscb[:, 0:nb], qsc[:, 0:nb])
            rsb = bass.AP(qrs, 0, [[KI // 16, 128], [1, nb], [0, 16]])
            R("dve", ve.tensor_tensor(
                f32a[:, 0:kcols].rearrange("p (a b) -> p a b", b=16),
                f32a[:, 0:kcols].rearrange("p (a b) -> p a b", b=16),
                rsb, aop.mult))
            fence(["dve"], ["act"])
            stair(kcols, qrs, qscb, f32a[:, 0:kcols], qc2[:, 0:kcols], afp32=True)
            fence(["dve"], ["pe"])
            for k0 in range(0, kct, 8):
                kn = min(8, kct - k0)
                for kk in range(kn):
                    R("pe", te.transpose(ps_t[:, kk * 128:(kk + 1) * 128],
                                         qc2[:, (k0 + kk) * 128:(k0 + kk + 1) * 128],
                                         ident[:, :]))
                fence(["pe"], ["dve"])
                R("dve", ve.tensor_copy(qa[:, 0:kn * 128],
                                        bass.AP(ps_t, 0, [[1024, 128], [1, kn * 128]])))
                fence(["dve"], ["pool", "pe"])
                dst = bass.AP(dst_dram.tensor, (k0 * 128) * dst_rowlen + nr * 128,
                              [[dst_rowlen, 128], [128 * dst_rowlen, kn], [1, 128]])
                dma("pool", "dw", dst,
                    bass.AP(qa, 0, [[KI, 128], [128, kn], [1, 128]]))
                dma_wait("pool", "dw")
                pool_mark()
                ve.wait_ge(done["pool"], cnt["pool"])
            last["pe"] = None

    wquant(wg_in, wgq_d, KI, D)
    wquant(wu_in, wuq_d, KI, D)
    wquant(wd_in, wdq_d, D, KI)
    po.sem_inc(done["pool"], 1); cnt["pool"] += 1
    for e in ("pe", "act", "dve", "sp"):
        ENG[e].wait_ge(done["pool"], cnt["pool"])

    for bb in range(B_CORE):
        for mt in range(MPB):
            t0 = mt * T
            w0 = mt * WPM
            sg0 = w0 * SPW
            nsg = WPM * SPW
            lt0, lr0 = divmod(sg0, 128)

            sp.wait_ge(done["dve"], cnt["dve"])
            sp.wait_ge(done["pe"], cnt["pe"])
            dma("sp", "dm", costab[:, :], costab_d[:, t0:t0 + T])
            dma("sp", "dm", sintab[:, :], sintab_d[:, t0:t0 + T])
            dma_wait("sp", "dm")
            sp_mark()
            for e in ("dve", "pe", "act"):
                ENG[e].wait_ge(done["sp"], cnt["sp"])
            if "cost" in tap_d:
                tap_wait()
                dma("pool", "dmp", tap_d["cost"][bb, mt], costab[:, :])

            for tcc in range(TC):
                sp.wait_ge(done["dve"], cnt["dve"])
                sp.wait_ge(done["act"], cnt["act"])
                sp.wait_ge(done["pool"], cnt["pool"])
                dma("sp", "din", f32a[:, 0:D],
                    x_in[bb, t0 + tcc * 128:t0 + (tcc + 1) * 128, :])
                dma_wait("sp", "din")
                sp_mark()
                ve.wait_ge(done["sp"], cnt["sp"])
                ve.wait_ge(done["pe"], cnt["pe"])
                R("dve", ve.tensor_copy(x_tm[:, tcc, :], f32a[:, 0:D]))
                fence(["dve"], ["pe"])
                for cc in range(DCH):
                    R("pe", te.transpose(ps_t[:, cc * 128:(cc + 1) * 128],
                                         x_tm[:, tcc, cc * 128:(cc + 1) * 128],
                                         ident[:, :]))
                fence(["pe"], ["dve"])
                R("dve", ve.tensor_copy(
                    bass.AP(xfw, tcc * 128, [[DCH * T, 128], [T, DCH], [1, 128]]),
                    bass.AP(ps_t, 0, [[1024, 128], [128, DCH], [1, 128]])))
                fence(["dve"], ["pe"])

            if "xtm" in tap_d:
                tap_wait()
                dma("pool", "dmp", tap_d["xtm"][bb, mt], x_tm[:, :, :])
            if "xf" in tap_d:
                tap_wait()
                dma("pool", "dmp", tap_d["xf"][bb, mt], xfw[:, :, :])
            ctile = bass.AP(costab, 0, [[T, 128], [0, 2], [1, T]])
            stile = bass.AP(sintab, 0, [[T, 128], [0, 2], [1, T]])
            qk_ev = [0] * (2 * DCH)
            rot_dve = {}
            wq_guard = {}
            for fc in range(2 * DCH):
                db = fc % 2
                if db in wq_guard:
                    waiton("pool", "pe", wq_guard[db])
                dma("pool", "dw", wq_sb[:, db, :],
                    bass.AP(wqkv_d.tensor, fc * 128,
                            [[2 * D, 128], [128 * 2 * D, DCH], [1, 128]]))
                te.wait_ge(dsem["dw"][0], dsem["dw"][1])
                if fc >= 2:
                    waiton("pe", "act", qk_ev[fc - 2])
                for cc in range(DCH):
                    R("pe", te.matmul(ps_big[:, db * 512:db * 512 + T],
                                      wq_sb[:, db, cc * 128:(cc + 1) * 128],
                                      xfw[:, cc, :], start=(cc == 0), stop=(cc == DCH - 1)))
                qmm = mark("pe")
                wq_guard[db] = qmm
                waiton("act", "pe", qmm)
                with nodrain():
                    if fc < DCH:
                        R("act", ac.activation(q_rot[:, fc, :], ps_big[:, db * 512:db * 512 + T],
                                               AF.Copy, bias=0.0, scale=0.125))
                    else:
                        R("act", ac.activation(kvro[:, fc - DCH, :],
                                               ps_big[:, db * 512:db * 512 + T],
                                               AF.Copy, bias=0.0, scale=1.0))
                qk_ev[fc] = mark("act")
                if fc % 2 == 1:
                    rp = fc // 2
                    buf = q_rot if fc < DCH else kvro
                    cc0 = (fc if fc < DCH else fc - DCH) - 1
                    rps = ps_big[:, 1024:2048] if rp % 2 == 0 else ps_s[:, 0:1024]
                    waiton("pe", "act", qk_ev[fc])
                    if rp >= 2:
                        waiton("pe", "dve", rot_dve[rp - 2])
                    te.matmul(rps[:, 0:T], p32[:, :], buf[:, cc0, :],
                              start=True, stop=True)
                    R("pe", te.matmul(rps[:, T:2 * T], p32[:, :], buf[:, cc0 + 1, :],
                                      start=True, stop=True))
                    rmm = mark("pe")
                    waiton("dve", "pe", rmm)
                    bb2 = buf[:, cc0:cc0 + 2, :]
                    ve.tensor_tensor(qa[:, 0:2 * T].rearrange("p (a t) -> p a t", t=T),
                                     bb2, ctile, aop.mult)
                    ve.tensor_tensor(qa[:, 2 * T:4 * T].rearrange("p (a t) -> p a t", t=T),
                                     rps[:, 0:2 * T].rearrange("p (a t) -> p a t", t=T),
                                     stile, aop.mult)
                    R("dve", ve.tensor_tensor(
                        bb2, qa[:, 0:2 * T].rearrange("p (a t) -> p a t", t=T),
                        qa[:, 2 * T:4 * T].rearrange("p (a t) -> p a t", t=T),
                        aop.add))
                    rot_dve[rp] = mark("dve")
            waiton("pe", "dve", rot_dve[DCH - 1])
            waiton("act", "dve", rot_dve[DCH - 1])

            if "qpre" in tap_d:
                tap_wait()
                dma("pool", "dmp", tap_d["qpre"][bb, mt], q_rot[:, :, :])
                dma_wait("pool", "dmp")
                pool_mark()
                for e in ("pe", "dve"):
                    ENG[e].wait_ge(done["pool"], cnt["pool"])
            if "qrot" in tap_d:
                tap_wait()
                dma("pool", "dmp", tap_d["qrot"][bb, mt], q_rot[:, :, :])

            for cc in range(DCH):
                R("pe", te.matmul(ps_p[0:H, 0:T], wpsh[:, 16 - 2 * cc:32 - 2 * cc],
                                  kvro[:, cc, :], start=(cc == 0),
                                  stop=(cc == DCH - 1)))
            fence(["pe"], ["act"])
            R("act", ac.activation(pscore[:, :], ps_p[0:H, 0:T], AF.Copy))
            fence(["act"], ["pe", "act"])
            if "pscore" in tap_d:
                tap_wait()
                dma("pool", "dmp", tap_d["pscore"][bb, mt], pscore[:, :])
                dma_wait("pool", "dmp")
                pool_mark()
                ac.wait_ge(done["pool"], cnt["pool"])
            R("act", ac.activation(pscore[:, :], pscore[:, :], AF.Exp))
            fence(["act"], ["dve"])
            ve.tensor_reduce(qbm[0:H, 0:T // 16],
                             pscore[:, :].rearrange("p (a b) -> p a b", b=16),
                             AX.X, aop.add)
            ve.reciprocal(qrs[0:H, 0:T // 16], qbm[0:H, 0:T // 16])
            rsb = bass.AP(qrs, 0, [[KI // 16, H], [1, T // 16], [0, 16]])
            R("dve", ve.tensor_tensor(
                pkv[:, :].rearrange("p (a b) -> p a b", b=16),
                pscore[:, :].rearrange("p (a b) -> p a b", b=16), rsb, aop.mult))
            fence(["dve"], ["pe"])
            for tcc in range(TC):
                R("pe", te.transpose(ps_t[:, tcc * H:(tcc + 1) * H],
                                     pkv[:, tcc * 128:(tcc + 1) * 128],
                                     ident[0:H, 0:H]))
            fence(["pe"], ["dve"])
            R("dve", ve.tensor_copy(
                bass.AP(pkv_km, 0, [[TC * H, 128], [H, TC], [1, H]]),
                bass.AP(ps_t, 0, [[1024, 128], [H, TC], [1, H]])))
            fence(["dve"], ["pe"])

            for w in range(WPM):
                te.wait_ge(done["dve"], cnt["dve"])
                for cc in range(DCH):
                    R("pe", te.transpose(ps_t[:, cc * 128:(cc + 1) * 128],
                                         kvro[:, cc, w * 128:(w + 1) * 128],
                                         ident[:, :]))
                fence(["pe"], ["dve"])
                R("dve", ve.tensor_copy(kv_km[:, w, :],
                                        bass.AP(ps_t, 0, [[1024, 128], [1, H * DH]])))
                fence(["dve"], ["pe"])
            if "kvkm" in tap_d:
                tap_wait()
                dma("pool", "dmp", tap_d["kvkm"][bb, mt], kv_km[:, :, :])

            for w in range(WPM):
                pk = bass.AP(pkv_km, w * H, [[TC * H, 128], [1, H], [0, DH]])
                R("dve", ve.tensor_tensor(
                    bass.AP(xfw, w * H * DH, [[DCH * T, 128], [DH, H], [1, DH]]),
                    kv_km[:, w, :].rearrange("p (h d) -> p h d", d=DH),
                    pk, aop.mult))
            fence(["dve"], ["pe"])
            for w in range(WPM):
                lr = w * SPW
                st = segf[:, 128 - lr:256 - lr]
                te.matmul(ps_p[:, 0:512], st,
                          bass.AP(xfw, w * H * DH, [[DCH * T, 128], [1, 512]]),
                          start=(w == 0), stop=(w == WPM - 1))
                R("pe", te.matmul(ps_big[:, 1536:2048], st,
                                  bass.AP(xfw, w * H * DH + 512,
                                          [[DCH * T, 128], [1, 512]]),
                                  start=(w == 0), stop=(w == WPM - 1)))
            fence(["pe"], ["act"])
            with nodrain():
                ac.activation(gkm[lr0:lr0 + nsg, lt0, 0:512], ps_p[0:nsg, 0:512],
                              AF.Copy)
                R("act", ac.activation(gkm[lr0:lr0 + nsg, lt0, 512:1024],
                                       ps_big[0:nsg, 1536:2048], AF.Copy))
            fence(["act"], ["pe", "dve"])
            if "gkm" in tap_d:
                tap_wait()
                dma("pool", "dmp", tap_d["gkm"][bb, mt], gkm[:, :, :])

            def ln64(src_ap, dst_ap, r0, nr, goff):
                R("dve", ve.tensor_reduce(
                    stats[r0:r0 + nr, 0:H],
                    src_ap.rearrange("p (h d) -> p h d", d=DH), AX.X, aop.add))
                fence(["dve"], ["act"])
                R("act", ac.activation(f32a[r0:r0 + nr, 0:H * DH], src_ap, AF.Square))
                fence(["act"], ["dve"])
                ve.tensor_reduce(stats[r0:r0 + nr, H:2 * H],
                                 f32a[r0:r0 + nr, 0:H * DH].rearrange(
                                     "p (h d) -> p h d", d=DH), AX.X, aop.add)
                ve.tensor_scalar(stats[r0:r0 + nr, 2 * H:3 * H],
                                 stats[r0:r0 + nr, 0:H], 1.0 / DH, None, aop.mult)
                ve.tensor_scalar(stats[r0:r0 + nr, 3 * H:4 * H],
                                 stats[r0:r0 + nr, H:2 * H], 1.0 / DH, None, aop.mult)
                ve.tensor_tensor(stats[r0:r0 + nr, 4 * H:5 * H],
                                 stats[r0:r0 + nr, 2 * H:3 * H],
                                 stats[r0:r0 + nr, 2 * H:3 * H], aop.mult)
                R("dve", ve.tensor_tensor(stats[r0:r0 + nr, 3 * H:4 * H],
                                          stats[r0:r0 + nr, 3 * H:4 * H],
                                          stats[r0:r0 + nr, 4 * H:5 * H], aop.subtract))
                fence(["dve"], ["act"])
                R("act", ac.activation(stats[r0:r0 + nr, 4 * H:5 * H],
                                       stats[r0:r0 + nr, 3 * H:4 * H], AF.Ln,
                                       bias=1e-5, scale=1.0))
                R("act", ac.activation(stats[r0:r0 + nr, 5 * H:6 * H],
                                       stats[r0:r0 + nr, 4 * H:5 * H], AF.Exp,
                                       bias=0.0, scale=-0.5))
                fence(["act"], ["dve"])
                srow = 8 * 4 * H
                mu_b = bass.AP(stats, r0 * srow + 2 * H, [[srow, nr], [1, H], [0, DH]])
                rs_b = bass.AP(stats, r0 * srow + 5 * H, [[srow, nr], [1, H], [0, DH]])
                g_b = bass.AP(lngt, r0 * 4 * DH + goff, [[4 * DH, nr], [0, H], [1, DH]])
                b_b = bass.AP(lngt, r0 * 4 * DH + goff + DH, [[4 * DH, nr], [0, H], [1, DH]])
                sq = f32a[r0:r0 + nr, 0:H * DH].rearrange("p (h d) -> p h d", d=DH)
                ve.tensor_tensor(sq, src_ap.rearrange("p (h d) -> p h d", d=DH),
                                 mu_b, aop.subtract)
                ve.tensor_tensor(sq, sq, rs_b, aop.mult)
                ve.tensor_tensor(sq, sq, g_b, aop.mult)
                R("dve", ve.tensor_tensor(dst_ap.rearrange("p (h d) -> p h d", d=DH),
                                          sq, b_b, aop.add))

            WH = WPM * H
            W2 = WPM // 2
            GH = W2 * H
            kvv = kv_km[:, :, :].rearrange("p w (h d) -> p (w h) d", d=DH)
            R("dve", ve.tensor_reduce(stats[:, 0:WH], kvv, AX.X, aop.add))
            fence(["dve"], ["act"])
            for hw in range(2):
                R("act", ac.activation(
                    f32a[:, 0:GH * DH],
                    kv_km[:, hw * W2:(hw + 1) * W2, :].rearrange(
                        "p w hd -> p (w hd)"), AF.Square))
                fence(["act"], ["dve"])
                R("dve", ve.tensor_reduce(
                    stats[:, WH + hw * GH:WH + (hw + 1) * GH],
                    f32a[:, 0:GH * DH].rearrange("p (g d) -> p g d", d=DH),
                    AX.X, aop.add))
                if hw == 0:
                    fence(["dve"], ["act"])
            ve.tensor_scalar(stats[:, 2 * WH:3 * WH], stats[:, 0:WH],
                             1.0 / DH, None, aop.mult)
            ve.tensor_scalar(stats[:, 3 * WH:4 * WH], stats[:, WH:2 * WH],
                             1.0 / DH, None, aop.mult)
            ve.tensor_tensor(stats[:, 4 * WH:5 * WH], stats[:, 2 * WH:3 * WH],
                             stats[:, 2 * WH:3 * WH], aop.mult)
            R("dve", ve.tensor_tensor(stats[:, 3 * WH:4 * WH],
                                      stats[:, 3 * WH:4 * WH],
                                      stats[:, 4 * WH:5 * WH], aop.subtract))
            fence(["dve"], ["act"])
            ac.activation(stats[:, 4 * WH:5 * WH], stats[:, 3 * WH:4 * WH],
                          AF.Ln, bias=1e-5, scale=1.0)
            R("act", ac.activation(stats[:, 5 * WH:6 * WH], stats[:, 4 * WH:5 * WH],
                                   AF.Exp, bias=0.0, scale=-0.5))
            fence(["act"], ["dve"])
            for hw in range(2):
                kvh = kv_km[:, hw * W2:(hw + 1) * W2, :].rearrange(
                    "p w (h d) -> p (w h) d", d=DH)
                mu_b = bass.AP(stats, 2 * WH + hw * GH,
                               [[8 * 4 * H, 128], [1, GH], [0, DH]])
                rs_b = bass.AP(stats, 5 * WH + hw * GH,
                               [[8 * 4 * H, 128], [1, GH], [0, DH]])
                g_b = bass.AP(lngt, 0, [[4 * DH, 128], [0, GH], [1, DH]])
                b_b = bass.AP(lngt, DH, [[4 * DH, 128], [0, GH], [1, DH]])
                sqv = f32a[:, 0:GH * DH].rearrange("p (g d) -> p g d", d=DH)
                ve.tensor_tensor(sqv, kvh, mu_b, aop.subtract)
                ve.tensor_tensor(sqv, sqv, rs_b, aop.mult)
                ve.tensor_tensor(sqv, sqv, g_b, aop.mult)
                R("dve", ve.tensor_tensor(
                    vh_km[:, 1 + hw * W2:1 + (hw + 1) * W2, :].rearrange(
                        "p w (h d) -> p (w h) d", d=DH),
                    sqv, b_b, aop.add))
            fence(["dve"], ["pe", "act"])
            if "vhkm" in tap_d:
                tap_wait()
                dma("pool", "dmp", tap_d["vhkm"][bb, mt], vh_km[:, 1:WPM + 1, :])
            ln64(gkm[lr0:lr0 + nsg, lt0, :], vg_km[lr0:lr0 + nsg, lt0, :],
                 lr0, nsg, 2 * DH)
            fence(["dve"], ["pe", "act"])
            if "vgkm" in tap_d:
                tap_wait()
                dma("pool", "dmp", tap_d["vgkm"][bb, mt], vg_km[:, :, :])

            for w in range(WPM):
                for cc in range(DCH):
                    R("pe", te.transpose(ps_t[:, cc * 128:(cc + 1) * 128],
                                         vh_km[:, w + 1, cc * 128:(cc + 1) * 128],
                                         ident[:, :]))
                fence(["pe"], ["dve"])
                R("dve", ve.tensor_copy(
                    bass.AP(vh_f, (w + 1) * WIN,
                            [[DCH * (WPM + 1) * WIN, 128], [(WPM + 1) * WIN, DCH],
                             [1, WIN]]),
                    bass.AP(ps_t, 0, [[1024, 128], [128, DCH], [1, 128]])))
                fence(["dve"], ["pe"])
            for cc in range(DCH):
                R("pe", te.transpose(ps_t[:, cc * 128:(cc + 1) * 128],
                                     vg_km[:, lt0, cc * 128:(cc + 1) * 128],
                                     ident[:, :]))
            fence(["pe"], ["dve"])
            R("dve", ve.tensor_copy(
                bass.AP(vg_f, lt0 * 128 + lr0,
                        [[DCH * LMT * 128, 128], [LMT * 128, DCH], [1, nsg]]),
                bass.AP(ps_t, lr0, [[1024, 128], [128, DCH], [1, nsg]])))
            fence(["dve"], ["pe"])

            av_pe = [0] * DCH
            av_ev = [0] * DCH
            NW = 2 * DCH * WPM
            pe_sg = [0] * NW
            act_eg = [0] * NW
            dve_ng = [0] * NW
            pe_tg = [0] * NW
            dve_vg = [0] * NW

            def geom(w):
                wg = w0 + w
                has_prev = wg > 0
                loc0 = 128 if has_prev else 0
                gn_old = wg * SPW
                gn = gn_old + SPW
                return wg, has_prev, loc0, gn_old, gn, loc0 + 128

            for g in range(NW + 1):
                if g < NW:
                    hidx = g // WPM
                    w = g % WPM
                    cc = hidx // 2
                    ro = (hidx % 2) * 64
                    wg, has_prev, loc0, gn_old, gn, off2 = geom(w)
                    par = g % 2
                    pso = par * 512
                    qs_ap = q_rot[ro:ro + 64, cc, w * 128:(w + 1) * 128]
                    if g >= 2:
                        waiton("pe", "act", act_eg[g - 2])
                    kst = (w if has_prev else w + 1) * WIN
                    te.matmul(ps_s[:, pso:pso + loc0 + 128], qs_ap,
                              vh_f[ro:ro + 64, cc, kst:kst + loc0 + 128],
                              start=True, stop=False)
                    te.matmul(ps_s[:, pso + off2:pso + off2 + gn], qs_ap,
                              vg_f[ro:ro + 64, cc, 0:gn], start=False,
                              stop=False)
                    te.matmul(ps_s[:, pso + loc0:pso + loc0 + 128],
                              negi[:, :], ucur[:, :], start=False, stop=False)
                    R("pe", te.matmul(
                        ps_s[:, pso + off2 + gn_old:pso + off2 + gn],
                        negi[:, :], useg[:, :], start=False, stop=True))
                    pe_sg[g] = mark("pe")
                if g >= 1:
                    gq = g - 1
                    hidx = gq // WPM
                    w = gq % WPM
                    cc = hidx // 2
                    hp = hidx % 2
                    hh = hidx
                    ro = hp * 64
                    wg, has_prev, loc0, gn_old, gn, off2 = geom(w)
                    par = gq % 2
                    pso = par * 512
                    ncols = off2 + gn
                    waiton("act", "pe", pe_sg[gq])
                    if gq >= 2:
                        waiton("act", "pe", pe_tg[gq - 2])
                    with nodrain():
                        R("act", ac.activation(
                            p_sb[:, pso:pso + ncols], ps_s[:, pso:pso + ncols],
                            AF.Exp, bias=-C_SOFT, scale=1.0,
                            accum_out=denom[:, par:par + 1]))
                    act_eg[gq] = mark("act")
                    waiton("dve", "act", act_eg[gq])
                    ve.reciprocal(denom[:, 2 + par:3 + par],
                                  denom[:, par:par + 1])
                    with nodrain():
                        R("dve", ve.tensor_scalar(
                            p_sb[:, pso:pso + ncols], p_sb[:, pso:pso + ncols],
                            denom[:, 2 + par:3 + par], None, aop.mult))
                    dve_ng[gq] = mark("dve")
                    waiton("pe", "dve", dve_ng[gq])
                    if gq >= 1:
                        waiton("pe", "dve", dve_vg[gq - 1])
                    nt = 0
                    if has_prev:
                        te.transpose(ps_t[:, 0:128],
                                     p_sb[:, pso:pso + 128], ident[:, :])
                        nt += 1
                    te.transpose(ps_t[:, nt * 128:nt * 128 + 128],
                                 p_sb[:, pso + loc0:pso + loc0 + 128],
                                 ident[:, :])
                    nt += 1
                    gdone = 0
                    while gdone < gn:
                        take = min(128, gn - gdone)
                        R("pe", te.transpose(
                            ps_t[0:take, nt * 128:nt * 128 + 128],
                            p_sb[:, pso + off2 + gdone:pso + off2 + gdone + take],
                            ident[:, :]))
                        nt += 1
                        gdone += take
                    pe_tg[gq] = mark("pe")
                    waiton("dve", "pe", pe_tg[gq])
                    ve.memset(pt_g[:, :, w * 128:(w + 1) * 128], 0.0)
                    ve.drain()
                    with nodrain():
                        nt = 0
                        if has_prev:
                            ve.tensor_copy(pt_loc[:, w, 128:256],
                                           ps_t[:, 0:128])
                            nt += 1
                        ve.tensor_copy(pt_loc[:, w + 1, 0:128],
                                       ps_t[:, nt * 128:nt * 128 + 128])
                        nt += 1
                        gdone = 0
                        while gdone < gn:
                            take = min(128, gn - gdone)
                            ve.tensor_copy(pt_g[0:take, gdone // 128,
                                                w * 128:(w + 1) * 128],
                                           ps_t[0:take, nt * 128:nt * 128 + 128])
                            nt += 1
                            gdone += take
                        R("dve", ve.tensor_copy(denom[:, 4 + par:5 + par],
                                                denom[:, 2 + par:3 + par]))
                    dve_vg[gq] = mark("dve")
                    if w == WPM - 1:
                        avo = (cc % 2) * 1024
                        waiton("pe", "dve", dve_vg[gq])
                        if cc >= 2 and hp == 0:
                            waiton("pe", "dve", av_ev[cc - 2])
                        for aw in range(WPM):
                            awg = w0 + aw
                            te.matmul(ps_big[ro:ro + 64, avo + aw * 128:avo + (aw + 1) * 128],
                                      vh_km[:, aw + 1, hh * DH:(hh + 1) * DH],
                                      pt_loc[:, aw + 1, 0:128], start=(aw == 0), stop=False)
                            if awg > 0:
                                te.matmul(ps_big[ro:ro + 64, avo + aw * 128:avo + (aw + 1) * 128],
                                          vh_km[:, aw, hh * DH:(hh + 1) * DH],
                                          pt_loc[:, aw, 128:256], start=False, stop=False)
                        ngl = (w0 + WPM) * SPW
                        gdone = 0
                        while gdone < ngl:
                            take = min(128, ngl - gdone)
                            R("pe", te.matmul(ps_big[ro:ro + 64, avo:avo + T],
                                              vg_km[0:take, gdone // 128,
                                                    hh * DH:(hh + 1) * DH],
                                              pt_g[0:take, gdone // 128, 0:T],
                                              start=False, stop=(gdone + take >= ngl)))
                            gdone += take
                        av_pe[cc] = mark("pe")
                        if hp == 1:
                            waiton("dve", "pe", av_pe[cc])
                            with nodrain():
                                R("dve", ve.tensor_copy(kvro[:, cc, :],
                                                        ps_big[:, avo:avo + T]))
                            av_ev[cc] = mark("dve")
            waiton("pe", "dve", av_ev[DCH - 1])
            waiton("act", "dve", av_ev[DCH - 1])
            if "attnf" in tap_d:
                tap_wait()
                dma("pool", "dmp", tap_d["attnf"][bb, mt], kvro[:, :, :])

            wo_guard = {}
            stt_m = [0] * TC
            xq_dve = [0] * TC
            for tcc in range(TC):
                half = (tcc % 2) * 1024
                sub = tcc % 2
                lo = sub * 6
                for cc in range(DCH):
                    db = cc % 2
                    if db in wo_guard:
                        waiton("pool", "pe", wo_guard[db])
                    dma("pool", "dw", wo_sb[:, db, :],
                        bass.AP(woutb_d.tensor, cc * 128 * D, [[D, 128], [1, D]]))
                    te.wait_ge(dsem["dw"][0], dsem["dw"][1])
                    if cc == 0:
                        if tcc == 0:
                            te.wait_ge(done["dve"], cnt["dve"])
                            te.wait_ge(done["act"], cnt["act"])
                        elif tcc >= 2:
                            waiton("pe", "dve", stt_m[tcc - 2])
                    for nh in range(2):
                        R("pe", te.matmul(
                            ps_big[:, half + nh * 512:half + (nh + 1) * 512],
                            kvro[:, cc, tcc * 128:(tcc + 1) * 128],
                            wo_sb[:, db, nh * 512:(nh + 1) * 512],
                            start=(cc == 0), stop=False))
                    last["pe"].then_inc(done["pe"], 1)
                    cnt["pe"] += 1
                    last["pe"] = None
                    wo_guard[db] = cnt["pe"]
                te.matmul(ps_big[:, half:half + 512], onesr[0:1, :],
                          boutr[0:1, 0:512], start=False, stop=True)
                R("pe", te.matmul(ps_big[:, half + 512:half + 1024],
                                  onesr[0:1, :], boutr[0:1, 512:1024],
                                  start=False, stop=True))
                wmm = mark("pe")
                waiton("dve", "pe", wmm)
                R("dve", ve.scalar_tensor_tensor(
                    x1[:, tcc, :], ps_big[:, half:half + D], 0.0,
                    x_tm[:, tcc, :], aop.add, aop.add,
                    accum_out=rowsum[:, sub * 2:sub * 2 + 1]))
                stt_m[tcc] = mark("dve")
                waiton("act", "dve", stt_m[tcc])
                R("act", ac.activation(f32a[:, 2 * D:3 * D], x1[:, tcc, :], AF.Square,
                                       accum_out=rowsum[:, sub * 2 + 1:sub * 2 + 2]))
                fence(["act"], ["dve"])
                ve.tensor_scalar(lnsc[:, lo:lo + 1], rowsum[:, sub * 2:sub * 2 + 1],
                                 1.0 / D, None, aop.mult)
                ve.tensor_scalar(lnsc[:, lo + 1:lo + 2],
                                 rowsum[:, sub * 2 + 1:sub * 2 + 2],
                                 1.0 / D, None, aop.mult)
                ve.tensor_tensor(lnsc[:, lo + 2:lo + 3], lnsc[:, lo:lo + 1],
                                 lnsc[:, lo:lo + 1], aop.mult)
                R("dve", ve.tensor_tensor(lnsc[:, lo + 1:lo + 2], lnsc[:, lo + 1:lo + 2],
                                          lnsc[:, lo + 2:lo + 3], aop.subtract))
                fence(["dve"], ["act"])
                ac.activation(lnsc[:, lo + 3:lo + 4], lnsc[:, lo + 1:lo + 2], AF.Ln,
                              bias=1e-5, scale=1.0)
                R("act", ac.activation(lnsc[:, lo + 4:lo + 5], lnsc[:, lo + 3:lo + 4],
                                       AF.Exp, bias=0.0, scale=-0.5))
                fence(["act"], ["dve"])
                ve.tensor_tensor(lnsc[:, lo + 5:lo + 6], lnsc[:, lo:lo + 1],
                                 lnsc[:, lo + 4:lo + 5], aop.mult)
                R("dve", ve.tensor_scalar(lnsc[:, lo + 5:lo + 6], lnsc[:, lo + 5:lo + 6],
                                          -1.0, None, aop.mult))
                fence(["dve"], ["act"])
                R("act", ac.activation(x1[:, tcc, :], x1[:, tcc, :], AF.Identity,
                                       bias=lnsc[:, lo + 5:lo + 6],
                                       scale=lnsc[:, lo + 4:lo + 5]))
                apm = mark("act")
                yst = sub * D
                nb = D // 16
                waiton("dve", "act", apm)
                ve.tensor_reduce(qbm[:, 0:nb],
                                 x1[:, tcc, :].rearrange("p (a b) -> p a b", b=16),
                                 AX.X, aop.max, apply_absolute_value=True)
                ve.tensor_scalar(qsc[:, 0:nb], qbm[:, 0:nb], 1.0 / 6.0, 1e-12,
                                 aop.mult, aop.max)
                ve.reciprocal(qrs[:, 0:nb], qsc[:, 0:nb])
                rsb = bass.AP(qrs, 0, [[KI // 16, 128], [1, nb], [0, 16]])
                R("dve", ve.tensor_tensor(
                    f32a[:, yst:yst + D].rearrange("p (a b) -> p a b", b=16),
                    x1[:, tcc, :].rearrange("p (a b) -> p a b", b=16),
                    rsb, aop.mult))
                fence(["dve"], ["act"])
                stair(D, qrs, qsc, f32a[:, yst:yst + D], None, afp32=True,
                      out_flat=bass.AP(xfw, tcc * D,
                                       [[DCH * T, 128], [16, D // 16], [1, 16]]))
                xq_dve[tcc] = mark("dve")
            if "x1" in tap_d:
                tap_wait()
                dma("pool", "dmp", tap_d["x1"][bb, mt], x1[:, :, :])

            for tcc in range(TC):
                waiton("pe", "dve", xq_dve[tcc])
                for cc in range(DCH):
                    R("pe", te.transpose(
                        ps_t[:, cc * 128:(cc + 1) * 128],
                        bass.AP(xfw, tcc * D + cc * 128, [[DCH * T, 128], [1, 128]]),
                        ident[:, :]))
                fence(["pe"], ["dve"])
                R("dve", ve.tensor_copy(
                    bass.AP(q_rot, tcc * 128, [[DCH * T, 128], [T, DCH], [1, 128]]),
                    bass.AP(ps_t, 0, [[1024, 128], [128, DCH], [1, 128]])))
                fence(["dve"], ["pe"])
            if "x1q" in tap_d:
                for tcc in range(TC):
                    dma("pool", "dmp", tap_d["x1q"][bb, mt, tcc],
                        bass.AP(q_rot, tcc * 128, [[DCH * T, 128], [T, DCH], [1, 128]]))

            pe_g = [0] * (NKB + 2)
            pe_u = [0] * (NKB + 2)
            act_s = [0] * (NKB + 2)
            dve_m = [0] * (NKB + 2)
            pe_h = [0] * (NKB + 2)
            dve_e = [0] * (NKB + 2)
            pool_w = [0] * (NKB + 2)
            for ncb in range(NKB + 2):
                par = ncb % 2
                if ncb < NKB:
                    if ncb >= 2:
                        waiton("pool", "pe", pe_g[ncb - 2])
                        waiton("pool", "pe", pe_u[ncb - 2])
                    dma("pool", "dw", wg_sb[:, par, :],
                        bass.AP(wgq_d.tensor, ncb * 128,
                                [[KI, 128], [128 * KI, DCH], [1, 128]]))
                    dma("pool", "dw", wu_sb[:, par, :],
                        bass.AP(wuq_d.tensor, ncb * 128,
                                [[KI, 128], [128 * KI, DCH], [1, 128]]))
                    pool_w[ncb] = dsem["dw"][1]
                    te.wait_ge(dsem["dw"][0], pool_w[ncb])
                    if ncb >= 2:
                        waiton("pe", "act", act_s[ncb - 2])
                    for cc in range(DCH):
                        R("pe", te.matmul(ps_big[:, par * 512:par * 512 + 512],
                                          wg_sb[:, par, cc * 128:(cc + 1) * 128],
                                          q_rot[:, cc, :],
                                          start=(cc == 0), stop=(cc == DCH - 1)))
                    pe_g[ncb] = mark("pe")
                    waiton("act", "pe", pe_g[ncb])
                    if ncb >= 2:
                        waiton("act", "dve", dve_m[ncb - 2])
                    with nodrain():
                        R("act", ac.activation(sg2[:, par, :],
                                               ps_big[:, par * 512:par * 512 + 512],
                                               AF.Silu))
                    act_s[ncb] = mark("act")
                    if ncb >= 2:
                        waiton("pe", "dve", dve_m[ncb - 2])
                    for cc in range(DCH):
                        R("pe", te.matmul(ps_big[:, 1024 + par * 512:1024 + par * 512 + 512],
                                          wu_sb[:, par, cc * 128:(cc + 1) * 128],
                                          q_rot[:, cc, :],
                                          start=(cc == 0), stop=(cc == DCH - 1)))
                    pe_u[ncb] = mark("pe")
                    waiton("dve", "act", act_s[ncb])
                    waiton("dve", "pe", pe_u[ncb])
                    if ncb >= 2:
                        waiton("dve", "pe", pe_h[ncb - 2])
                    with nodrain():
                        R("dve", ve.tensor_tensor(hf2[:, par, :],
                                                  ps_big[:, 1024 + par * 512:1024 + par * 512 + 512],
                                                  sg2[:, par, :], aop.mult))
                    dve_m[ncb] = mark("dve")
                if 0 <= ncb - 1 < NKB:
                    nch = ncb - 1
                    parh = nch % 2
                    waiton("pe", "dve", dve_m[nch])
                    if nch >= 2:
                        waiton("pe", "dve", dve_e[nch - 2])
                    pto = parh * 512
                    for tcc in range(TC):
                        R("pe", te.transpose(ps_t[:, pto + tcc * 128:pto + (tcc + 1) * 128],
                                             hf2[:, parh, tcc * 128:(tcc + 1) * 128],
                                             ident[:, :]))
                    pe_h[nch] = mark("pe")
                    waiton("dve", "pe", pe_h[nch])
                    with nodrain():
                        R("dve", ve.tensor_copy(
                            bass.AP(h_tm, nch * 128, [[TC * KI, 128], [KI, TC], [1, 128]]),
                            bass.AP(ps_t, pto, [[1024, 128], [128, TC], [1, 128]])))
                    dve_e[nch] = mark("dve")
                    if QFOLD and nch % 2 == 1:
                        ng = 2
                        c0 = (nch - ng + 1) * 128
                        nbk = ng * 8
                        kw = ng * 128
                        hview = bass.AP(h_tm, c0,
                                        [[TC * KI, 128], [KI, TC], [16, nbk], [1, 16]])
                        R("dve", ve.tensor_reduce(qbm[:, 0:TC * nbk], hview, AX.X,
                                                  aop.max, apply_absolute_value=True))
                        ve.tensor_scalar(qsc[:, 0:TC * nbk], qbm[:, 0:TC * nbk],
                                         1.0 / 6.0, 1e-12, aop.mult, aop.max)
                        R("dve", ve.reciprocal(qrs[:, 0:TC * nbk], qsc[:, 0:TC * nbk]))
                        rsb4 = bass.AP(qrs, 0, [[KI // 16, 128], [nbk, TC], [1, nbk], [0, 16]])
                        R("dve", ve.tensor_tensor(
                            bass.AP(qc1, 0, [[KI, 128], [kw, TC], [16, nbk], [1, 16]]),
                            hview, rsb4, aop.mult))
                        fence(["dve"], ["act"])
                        stair(TC * kw, None, None, qc1[:, 0:TC * kw], None, final3=(
                            bass.AP(h_tm, c0, [[TC * KI, 128], [KI, TC], [16, nbk], [1, 16]]),
                            bass.AP(qc2, 0, [[KI, 128], [kw, TC], [16, nbk], [1, 16]]),
                            bass.AP(qsc, 0, [[KI // 16, 128], [nbk, TC], [1, nbk], [0, 16]])))
            if "htm" in tap_d:
                tap_wait()
                dma("pool", "dmp", tap_d["htm"][bb, mt], h_tm[:, :, :])

            dve_stt = 0
            wd_guard = {}
            for tp in range(TC // 2):
                if tp == 0:
                    if last.get("dve") is not None:
                        fence(["dve"], ["pe"])
                    else:
                        waiton("pe", "dve", cnt["dve"])
                else:
                    waiton("pe", "dve", dve_stt)
                if TPAR:
                    evq = []
                    for sub in range(2):
                        tcc = tp * 2 + sub
                        for k0 in range(0, NKB, 4):
                            kn = min(4, NKB - k0)
                            half = len(evq) % 2
                            if len(evq) >= TDEPTH:
                                waiton("pe", "dve", evq[-TDEPTH])
                            for kk in range(kn):
                                R("pe", te.transpose(
                                    ps_t[:, half * 512 + kk * 128:half * 512 + (kk + 1) * 128],
                                    h_tm[:, tcc, (k0 + kk) * 128:(k0 + kk + 1) * 128],
                                    ident[:, :]))
                            ptb = mark("pe")
                            waiton("dve", "pe", ptb)
                            with nodrain():
                                R("dve", ve.tensor_copy(
                                    bass.AP(hq_f, k0 * 256 + sub * 128,
                                            [[NKB * 256, 128], [256, kn], [1, 128]]),
                                    bass.AP(ps_t, half * 512, [[1024, 128], [128, kn], [1, 128]])))
                            evq.append(mark("dve"))
                    waiton("pe", "dve", evq[-1])
                else:
                    for sub in range(2):
                        tcc = tp * 2 + sub
                        for k0 in range(0, NKB, 8):
                            kn = min(8, NKB - k0)
                            for kk in range(kn):
                                R("pe", te.transpose(ps_t[:, kk * 128:(kk + 1) * 128],
                                                     h_tm[:, tcc, (k0 + kk) * 128:(k0 + kk + 1) * 128],
                                                     ident[:, :]))
                            fence(["pe"], ["dve"])
                            R("dve", ve.tensor_copy(
                                bass.AP(hq_f, k0 * 256 + sub * 128,
                                        [[NKB * 256, 128], [256, kn], [1, 128]]),
                                bass.AP(ps_t, 0, [[1024, 128], [128, kn], [1, 128]])))
                            fence(["dve"], ["pe"])
                for kk in range(NKB):
                    db = kk % 3
                    if db in wd_guard:
                        waiton("pool", "pe", wd_guard[db])
                    dma("pool", "dw", wd_sb[:, db, :],
                        bass.AP(wdq_d.tensor, kk * 128 * D, [[D, 128], [1, D]]))
                    if PREFETCH:
                        te.wait_ge(dsem["dw"][0], dsem["dw"][1])
                    else:
                        dma_wait("pool", "dw")
                        pool_mark()
                        te.wait_ge(done["pool"], cnt["pool"])
                    for sub in range(2):
                        for nh in range(2):
                            R("pe", te.matmul(
                                ps_big[:, sub * 1024 + nh * 512:sub * 1024 + (nh + 1) * 512],
                                hq_f[:, kk, sub * 128:(sub + 1) * 128],
                                wd_sb[:, db, nh * 512:(nh + 1) * 512],
                                start=(kk == 0), stop=(kk == NKB - 1)))
                    last["pe"].then_inc(done["pe"], 1)
                    cnt["pe"] += 1
                    last["pe"] = None
                    wd_guard[db] = cnt["pe"]
                mm_done = cnt["pe"]
                waiton("dve", "pe", mm_done)
                for sub in range(2):
                    tcc = tp * 2 + sub
                    R("dve", ve.scalar_tensor_tensor(
                        x1[:, tcc, :], ps_big[:, sub * 1024:sub * 1024 + D], 0.0,
                        x1[:, tcc, :], aop.add, aop.add,
                        accum_out=rowsum[:, sub * 2:sub * 2 + 1]))
                dve_stt = mark("dve")
                for sub in range(2):
                    tcc = tp * 2 + sub
                    lo = sub * 6
                    if sub == 0:
                        waiton("act", "dve", dve_stt)
                    R("act", ac.activation(f32a[:, sub * D:(sub + 1) * D],
                                           x1[:, tcc, :], AF.Square,
                                           accum_out=rowsum[:, sub * 2 + 1:sub * 2 + 2]))
                    fence(["act"], ["dve"])
                    ve.tensor_scalar(lnsc[:, lo:lo + 1], rowsum[:, sub * 2:sub * 2 + 1],
                                     1.0 / D, None, aop.mult)
                    ve.tensor_scalar(lnsc[:, lo + 1:lo + 2],
                                     rowsum[:, sub * 2 + 1:sub * 2 + 2],
                                     1.0 / D, None, aop.mult)
                    ve.tensor_tensor(lnsc[:, lo + 2:lo + 3], lnsc[:, lo:lo + 1],
                                     lnsc[:, lo:lo + 1], aop.mult)
                    R("dve", ve.tensor_tensor(lnsc[:, lo + 1:lo + 2], lnsc[:, lo + 1:lo + 2],
                                              lnsc[:, lo + 2:lo + 3], aop.subtract))
                    fence(["dve"], ["act"])
                    ac.activation(lnsc[:, lo + 3:lo + 4], lnsc[:, lo + 1:lo + 2], AF.Ln,
                                  bias=1e-5, scale=1.0)
                    R("act", ac.activation(lnsc[:, lo + 4:lo + 5], lnsc[:, lo + 3:lo + 4],
                                           AF.Exp, bias=0.0, scale=-0.5))
                    fence(["act"], ["dve"])
                    ve.tensor_tensor(lnsc[:, lo + 5:lo + 6], lnsc[:, lo:lo + 1],
                                     lnsc[:, lo + 4:lo + 5], aop.mult)
                    R("dve", ve.tensor_scalar(lnsc[:, lo + 5:lo + 6], lnsc[:, lo + 5:lo + 6],
                                              -1.0, None, aop.mult))
                    fence(["dve"], ["act"])
                    R("act", ac.activation(x1[:, tcc, :], x1[:, tcc, :], AF.Identity,
                                           bias=lnsc[:, lo + 5:lo + 6],
                                           scale=lnsc[:, lo + 4:lo + 5]))
                    apm = mark("act")
                    if SPSTORE:
                        waiton("sp", "act", apm)
                        dma("sp", "dout",
                            out_d[bb, t0 + tcc * 128:t0 + (tcc + 1) * 128, :],
                            x1[:, tcc, :])
                    else:
                        waiton("pool", "act", apm)
                        dma("pool", "dmp",
                            out_d[bb, t0 + tcc * 128:t0 + (tcc + 1) * 128, :],
                            x1[:, tcc, :])
            if SPSTORE:
                dma_wait("sp", "dout")
                sp_mark()
                waiton("act", "sp", cnt["sp"])
                waiton("dve", "sp", cnt["sp"])
            else:
                dma_wait("pool", "dmp")
                pool_mark()
                waiton("act", "pool", cnt["pool"])
                waiton("dve", "pool", cnt["pool"])

            ve.wait_ge(done["pe"], cnt["pe"])
            ve.tensor_copy(vh_km[:, 0, :], vh_km[:, WPM, :])
            R("dve", ve.tensor_copy(
                bass.AP(vh_f, 0, [[DCH * (WPM + 1) * WIN, 128],
                                  [(WPM + 1) * WIN, DCH], [1, WIN]]),
                bass.AP(vh_f, WPM * WIN, [[DCH * (WPM + 1) * WIN, 128],
                                          [(WPM + 1) * WIN, DCH], [1, WIN]])))
            fence(["dve"], ["pe", "act", "pool", "sp"])

    if SPSTORE:
        dma_wait("sp", "dout")
    dma_wait("pool", "dmp")
    po.sem_inc(done["pool"], 1)

    for cm in reversed(ctxs):
        cm.__exit__(None, None, None)
    return nc


def kernel(L0_attn_x, L0_attn_Wq, L0_attn_Wkv, L0_attn_Wproj, L0_attn_Wout,
           L0_attn_bout, L0_attn_ln_local_g, L0_attn_ln_local_b,
           L0_attn_ln_global_g, L0_attn_ln_global_b,
           L0_ffn_w_gate, L0_ffn_w_up, L0_ffn_w_down):
    x = np.ascontiguousarray(np.asarray(L0_attn_x, dtype=np.float32))
    lng = np.ascontiguousarray(np.stack(
        [np.asarray(L0_attn_ln_local_g, np.float32),
         np.asarray(L0_attn_ln_local_b, np.float32),
         np.asarray(L0_attn_ln_global_g, np.float32),
         np.asarray(L0_attn_ln_global_b, np.float32)], axis=0))
    common = {
        "wq": np.ascontiguousarray(np.asarray(L0_attn_Wq, np.float32)),
        "wkv": np.ascontiguousarray(np.asarray(L0_attn_Wkv, np.float32)),
        "wproj": np.ascontiguousarray(np.asarray(L0_attn_Wproj, np.float32)[:, 0]),
        "wout": np.ascontiguousarray(np.asarray(L0_attn_Wout, np.float32)),
        "bout": np.ascontiguousarray(np.asarray(L0_attn_bout, np.float32)),
        "lng": lng,
        "wgate": np.ascontiguousarray(np.asarray(L0_ffn_w_gate, np.float32)),
        "wup": np.ascontiguousarray(np.asarray(L0_ffn_w_up, np.float32)),
        "wdown": np.ascontiguousarray(np.asarray(L0_ffn_w_down, np.float32)),
    }
    n_cores = 8
    bpc = x.shape[0] // n_cores
    nc = build_program(B_CORE=bpc, N=x.shape[1])
    in_maps = [dict(common, x=np.ascontiguousarray(x[i * bpc:(i + 1) * bpc]))
               for i in range(n_cores)]
    res = run_bass_kernel_spmd(nc, in_maps, core_ids=list(range(n_cores)))
    return np.concatenate([np.asarray(r["out"]) for r in res.results], axis=0)


# version 24
# speedup vs baseline: 1.0470x; 1.0165x over previous
import math
import numpy as np

import concourse.bass as bass
import concourse.mybir as mybir
from concourse.alu_op_type import AluOpType as aop
from concourse.bass_utils import run_bass_kernel_spmd

F32 = mybir.dt.float32
BF16 = mybir.dt.bfloat16
AF = mybir.ActivationFunctionType
AX = mybir.AxisListType

D = 1024
H = 16
DH = 64
DCH = D // 128
KI = 2816
NKB = KI // 128
WIN = 128
SEGL = 16
SPW = WIN // SEGL
C_SOFT = 12.0





QFOLD = True
SPSTORE = True
PREFETCH = True
TPAR = True
TDEPTH = 1


def build_program(B_CORE=2, N=4096, T=512, taps=()):
    NW = N // WIN
    WPM = T // WIN
    MPB = NW // WPM
    NSEG = N // SEGL
    LMT = (NSEG + 127) // 128
    TC = T // 128
    taps = set(taps)

    nc = bass.Bass("TRN2", target_bir_lowering=False, detect_race_conditions=False)

    last = {}
    drain_on = [True]

    class nodrain:
        def __enter__(self):
            drain_on.append(False)

        def __exit__(self, *a):
            drain_on.pop()

    class EngProxy:
        _SKIP = {"wait_ge", "sem_inc", "wait_op", "nop", "engine_nop", "drain",
                 "sem_clear"}
        _NODRAIN = {"dma_start", "dma_start_transpose"}

        def __init__(self, eng, name):
            self._eng = eng
            self._name = name

        def __getattr__(self, m):
            f = getattr(self._eng, m)
            if not callable(f) or m in self._SKIP:
                return f
            name = self._name
            eng = self._eng
            drainable = name in ("dve", "act") and m not in self._NODRAIN

            def wrap(*a, **k):
                r = f(*a, **k)
                if hasattr(r, "then_inc"):
                    last[name] = r
                    if drainable and a and isinstance(a[0], bass.AP):
                        out = a[0]
                        try:
                            free = 1
                            for s in out.shape[1:]:
                                free *= s
                            nbytes = free * mybir.dt.size(out.dtype)
                        except Exception:
                            nbytes = 0
                        if nbytes < 2048 and drain_on[-1]:
                            eng.drain()
                return r
            return wrap

    te = EngProxy(nc.tensor, "pe")
    ac = EngProxy(nc.scalar, "act")
    ve = EngProxy(nc.vector, "dve")
    po = EngProxy(nc.gpsimd, "pool")
    sp = EngProxy(nc.sync, "sp")

    x_in = nc.dram_tensor("x", [B_CORE, N, D], F32, kind="ExternalInput").ap()
    wq_in = nc.dram_tensor("wq", [D, D], F32, kind="ExternalInput").ap()
    wkv_in = nc.dram_tensor("wkv", [D, D], F32, kind="ExternalInput").ap()
    wproj_in = nc.dram_tensor("wproj", [DH], F32, kind="ExternalInput").ap()
    wout_in = nc.dram_tensor("wout", [D, D], F32, kind="ExternalInput").ap()
    bout_in = nc.dram_tensor("bout", [D], F32, kind="ExternalInput").ap()
    lng_in = nc.dram_tensor("lng", [4, DH], F32, kind="ExternalInput").ap()
    wg_in = nc.dram_tensor("wgate", [KI, D], F32, kind="ExternalInput").ap()
    wu_in = nc.dram_tensor("wup", [KI, D], F32, kind="ExternalInput").ap()
    wd_in = nc.dram_tensor("wdown", [D, KI], F32, kind="ExternalInput").ap()
    out_d = nc.dram_tensor("out", [B_CORE, N, D], F32, kind="ExternalOutput").ap()
    wqkv_d = nc.dram_tensor("wqkv_bf", [D, 2 * D], BF16).ap()
    woutb_d = nc.dram_tensor("wout_bf", [D, D], BF16).ap()
    wgq_d = nc.dram_tensor("wgq_t", [D, KI], BF16).ap()
    costab_d = nc.dram_tensor("costab_d", [128, N], BF16).ap()
    sintab_d = nc.dram_tensor("sintab_d", [128, N], BF16).ap()
    wuq_d = nc.dram_tensor("wuq_t", [D, KI], BF16).ap()
    wdq_d = nc.dram_tensor("wdq_t", [KI, D], BF16).ap()

    tap_d = {}
    def def_tap(name, shape, dt=F32):
        if name in taps:
            tap_d[name] = nc.dram_tensor("tap_" + name, list(shape), dt,
                                         kind="ExternalOutput").ap()
    def_tap("xtm", [B_CORE, MPB, 128, TC, D], dt=BF16)
    def_tap("xf", [B_CORE, MPB, 128, DCH, T], dt=BF16)
    def_tap("qpre", [B_CORE, MPB, 128, DCH, T], dt=BF16)
    def_tap("qrot", [B_CORE, MPB, 128, DCH, T], dt=BF16)
    def_tap("kvkm", [B_CORE, MPB, 128, WPM, H * DH], dt=BF16)
    def_tap("vhkm", [B_CORE, MPB, 128, WPM, H * DH], dt=BF16)
    def_tap("gkm", [B_CORE, MPB, 128, LMT, H * DH], dt=BF16)
    def_tap("vgkm", [B_CORE, MPB, 128, LMT, H * DH], dt=BF16)
    def_tap("pscore", [B_CORE, MPB, H, T], dt=F32)
    def_tap("attnf", [B_CORE, MPB, 128, DCH, T], dt=BF16)
    def_tap("x1", [B_CORE, MPB, 128, TC, D], dt=F32)
    def_tap("x1q", [B_CORE, MPB, TC, 128, D], dt=BF16)
    def_tap("htm", [B_CORE, MPB, 128, TC, KI], dt=BF16)
    def_tap("cost", [B_CORE, MPB, 128, T], dt=BF16)

    ctxs = []
    def mk(cm):
        h = cm.__enter__()
        ctxs.append(cm)
        return h

    done = {e: mk(nc.semaphore("dn_" + e)) for e in ("pe", "act", "dve", "pool", "sp")}
    cnt = {e: 0 for e in done}
    dsem = {s: [mk(nc.semaphore("dm_" + s)), 0]
            for s in ("din", "dw", "dout", "dm", "dmp", "dinp")}

    def R(eng, instr):
        return instr

    ENG = {"pe": te, "act": ac, "dve": ve, "pool": po, "sp": sp}

    def fence(producers, consumers):
        for pe_ in producers:
            ins = last.get(pe_)
            assert ins is not None, f"fence: nothing emitted on {pe_}"
            ins.then_inc(done[pe_], 1)
            cnt[pe_] += 1
            last[pe_] = None
        for ce in consumers:
            for pe_ in producers:
                ENG[ce].wait_ge(done[pe_], cnt[pe_])

    def mark(eng):
        ins = last.get(eng)
        assert ins is not None, f"mark: nothing emitted on {eng}"
        ins.then_inc(done[eng], 1)
        cnt[eng] += 1
        last[eng] = None
        return cnt[eng]

    def waiton(consumer, prod, count):
        if count > 0:
            ENG[consumer].wait_ge(done[prod], count)

    def dma(eng, s, out, in_):
        ENG[eng].dma_start(out=out, in_=in_).then_inc(dsem[s][0], 16)
        dsem[s][1] += 16

    def dma_wait(eng, s):
        ENG[eng].wait_ge(dsem[s][0], dsem[s][1])

    def tap_wait():
        for e in ("dve", "act", "pe", "sp"):
            po.wait_ge(done[e], cnt[e])

    def sp_mark():
        sp.sem_inc(done["sp"], 1)
        cnt["sp"] += 1

    def pool_mark():
        po.sem_inc(done["pool"], 1)
        cnt["pool"] += 1

    CONSTS = [0.0, 1.0, 0.125, -float(C_SOFT), 1e-5, -0.5]
    cpool = mk(nc.sbuf_tensor("cpool", [128, len(CONSTS)], F32))
    for ci, cv in enumerate(CONSTS):
        nc.const_aps.register(F32, float(cv), cpool[:, ci:ci + 1])

    ident = mk(nc.sbuf_tensor("ident", [128, 128], BF16))
    p32 = mk(nc.sbuf_tensor("p32", [128, 128], BF16))
    negi = mk(nc.sbuf_tensor("negi", [128, 128], BF16))
    ucur = mk(nc.sbuf_tensor("ucur", [128, 128], BF16))
    useg = mk(nc.sbuf_tensor("useg", [128, SPW], BF16))
    segf = mk(nc.sbuf_tensor("segf", [128, 256], BF16))
    onesr = mk(nc.sbuf_tensor("onesr", [1, 128], BF16))
    onesf = mk(nc.sbuf_tensor("onesf", [1, 128], F32))
    boutr = mk(nc.sbuf_tensor("boutr", [1, D], BF16))
    wpsh = mk(nc.sbuf_tensor("wpsh", [128, 32], BF16))
    lngt = mk(nc.sbuf_tensor("lngt", [128, 4 * DH], BF16))
    iotap = mk(nc.sbuf_tensor("iotap", [128, 1], F32))
    iotaf = mk(nc.sbuf_tensor("iotaf", [128, 128], F32))
    invf = mk(nc.sbuf_tensor("invf", [1, 32], F32))
    invfs = mk(nc.sbuf_tensor("invfs", [1, 32], F32))
    tabsc = mk(nc.sbuf_tensor("tabsc", [128, 128], F32))
    costab = mk(nc.sbuf_tensor("costab", [128, T], BF16))
    sintab = mk(nc.sbuf_tensor("sintab", [128, T], BF16))

    wq_sb = mk(nc.sbuf_tensor("wq_sb", [128, 2, DCH * 128], BF16))
    wo_sb = mk(nc.sbuf_tensor("wo_sb", [128, 2, D], BF16))
    wg_sb = mk(nc.sbuf_tensor("wg_sb", [128, 2, DCH * 128], BF16))
    wu_sb = mk(nc.sbuf_tensor("wu_sb", [128, 2, DCH * 128], BF16))
    sg2 = mk(nc.sbuf_tensor("sg2", [128, 2, 512], BF16))
    hf2 = mk(nc.sbuf_tensor("hf2", [128, 2, 512], BF16))
    wd_sb = mk(nc.sbuf_tensor("wd_sb", [128, 3, D], BF16))

    x_tm = mk(nc.sbuf_tensor("x_tm", [128, TC, D], BF16))
    xfw = mk(nc.sbuf_tensor("xfw", [128, DCH, T], BF16))
    q_rot = mk(nc.sbuf_tensor("q_rot", [128, DCH, T], BF16))
    kvro = mk(nc.sbuf_tensor("kvro", [128, DCH, T], BF16))
    kv_km = mk(nc.sbuf_tensor("kv_km", [128, WPM, H * DH], BF16))
    vh_km = mk(nc.sbuf_tensor("vh_km", [128, WPM + 1, H * DH], BF16))
    vh_f = mk(nc.sbuf_tensor("vh_f", [128, DCH, (WPM + 1) * WIN], BF16))
    gkm = mk(nc.sbuf_tensor("gkm", [128, LMT, H * DH], BF16))
    vg_km = mk(nc.sbuf_tensor("vg_km", [128, LMT, H * DH], BF16))
    vg_f = mk(nc.sbuf_tensor("vg_f", [128, DCH, LMT * 128], BF16))
    pscore = mk(nc.sbuf_tensor("pscore", [H, T], F32))
    pkv = mk(nc.sbuf_tensor("pkv", [H, T], BF16))
    pkv_km = mk(nc.sbuf_tensor("pkv_km", [128, TC, H], BF16))
    pt_loc = mk(nc.sbuf_tensor("pt_loc", [128, WPM + 1, 256], BF16))
    pt_g = mk(nc.sbuf_tensor("pt_g", [128, LMT, T], BF16))
    p_sb = mk(nc.sbuf_tensor("p_sb", [128, 1024], BF16))
    denom = mk(nc.sbuf_tensor("denom", [128, 6], F32))

    x1 = mk(nc.sbuf_tensor("x1", [128, TC, D], F32))
    f32a = mk(nc.sbuf_tensor("f32a", [128, 3072], F32))
    qa = mk(nc.sbuf_tensor("qa", [128, KI], BF16))
    qs = mk(nc.sbuf_tensor("qs", [128, KI], BF16))
    qc1 = mk(nc.sbuf_tensor("qc1", [128, KI], BF16))
    qc2 = mk(nc.sbuf_tensor("qc2", [128, KI], BF16))
    qbm = mk(nc.sbuf_tensor("qbm", [128, KI // 16], F32))
    qsc = mk(nc.sbuf_tensor("qsc", [128, KI // 16], F32))
    qrs = mk(nc.sbuf_tensor("qrs", [128, KI // 16], F32))
    qscb = mk(nc.sbuf_tensor("qscb", [128, KI // 16], BF16))
    qrsb = mk(nc.sbuf_tensor("qrsb", [128, KI // 16], BF16))
    h_tm = mk(nc.sbuf_tensor("h_tm", [128, TC, KI], BF16))
    hq_tm = mk(nc.sbuf_tensor("hq_tm", [128, 1024], BF16))
    hq_f = mk(nc.sbuf_tensor("hq_f", [128, NKB, 256], BF16))
    stats = mk(nc.sbuf_tensor("stats", [128, 8 * 4 * H], F32))
    lnsc = mk(nc.sbuf_tensor("lnsc", [128, 12], F32))
    rowsum = mk(nc.sbuf_tensor("rowsum", [128, 4], F32))

    ps_big = mk(nc.psum_tensor("ps_big", [128, 2048], F32))
    ps_s = mk(nc.psum_tensor("ps_s", [128, 1024], F32))
    ps_t = mk(nc.psum_tensor("ps_t", [128, 1024], BF16))
    ps_p = mk(nc.psum_tensor("ps_p", [128, 512], F32))

    for ci, cv in enumerate(CONSTS):
        ve.memset(cpool[:, ci:ci + 1], float(cv))
    ve.memset(gkm[:, :, :], 0.0)
    ve.memset(vg_km[:, :, :], 0.0)
    ve.memset(pt_g[:, :, :], 0.0)
    ve.memset(onesr[0:1, :], 1.0)
    R("dve", ve.memset(onesf[0:1, :], 1.0))
    fence(["dve"], ["pool", "sp", "act", "pe"])

    po.iota(iotaf[:, :], [[1, 128]], channel_multiplier=0,
            allow_small_or_imprecise_dtypes=True)
    R("pool", po.iota(iotap[:, :], [[0, 1]], channel_multiplier=1,
                      allow_small_or_imprecise_dtypes=True))
    fence(["pool"], ["dve"])

    ipb = bass.AP(iotap, 0, [[1, 128], [0, 128]])
    ve.tensor_tensor(ident[:, :], iotaf[:, :], ipb, aop.is_equal)
    ve.tensor_scalar(negi[:, :], ident[:, :], -30000.0, None, aop.mult)
    ve.tensor_scalar(tabsc[:, 0:1], iotap[:, :], 63.5, 64.0, aop.is_gt, aop.mult)
    ve.tensor_tensor(tabsc[:, 1:2], iotap[:, :], tabsc[:, 0:1], aop.subtract)
    ve.tensor_scalar(tabsc[:, 2:3], tabsc[:, 1:2], 31.5, 64.0, aop.is_gt, aop.mult)
    ve.tensor_tensor(tabsc[:, 3:4], iotap[:, :], tabsc[:, 2:3], aop.subtract)
    ve.tensor_scalar(tabsc[:, 3:4], tabsc[:, 3:4], 32.0, None, aop.add)
    ve.tensor_tensor(p32[:, :], iotaf[:, :],
                     bass.AP(tabsc, 3, [[128, 128], [0, 128]]), aop.is_equal)
    ve.tensor_tensor(ucur[:, :], iotaf[:, :], ipb, aop.is_gt)
    ve.tensor_scalar(tabsc[:, 8:8 + SPW], iotaf[:, 0:SPW], 16.0, 15.0, aop.mult, aop.add)
    ve.tensor_tensor(useg[:, :], tabsc[:, 8:8 + SPW],
                     bass.AP(iotap, 0, [[1, 128], [0, SPW]]), aop.is_ge)
    ve.tensor_scalar(tabsc[:, 4:5], iotap[:, :], 15.5, None, aop.is_gt)
    for k in range(2, 8):
        ve.scalar_tensor_tensor(tabsc[:, 4:5], iotap[:, :], 16.0 * k - 0.5,
                                tabsc[:, 4:5], aop.is_gt, aop.add)
    ve.tensor_scalar(tabsc[:, 5:6], tabsc[:, 4:5], 128.0, None, aop.add)
    fence(["dve"], ["pool"])
    po.iota(f32a[:, 0:256], [[1, 256]], channel_multiplier=0,
            allow_small_or_imprecise_dtypes=True).then_inc(done["pool"], 1)
    cnt["pool"] += 1
    ve.wait_ge(done["pool"], cnt["pool"])
    R("dve", ve.tensor_tensor(segf[:, :], f32a[:, 0:256],
                              bass.AP(tabsc, 5, [[128, 128], [0, 256]]), aop.is_equal))
    fence(["dve"], ["act", "pe", "sp", "pool"])

    po.iota(invf[0:1, :], [[1, 32]], channel_multiplier=0,
            allow_small_or_imprecise_dtypes=True).then_inc(done["pool"], 1)
    cnt["pool"] += 1
    ac.wait_ge(done["pool"], cnt["pool"])
    R("act", ac.activation(invf[0:1, :], invf[0:1, :], AF.Exp,
                           bias=0.0, scale=-math.log(10000.0) / 32.0))
    fence(["act"], ["dve", "pe"])

    dma("sp", "dm", f32a[0:1, 0:D], bout_in[None, :])
    dma("sp", "dm", f32a[0:64, 1024:1025], wproj_in[:, None])
    dma("sp", "dm", f32a[64:128, 1025:1026], wproj_in[:, None])
    dma("sp", "dm", f32a[:, 1026:1282],
        bass.AP(lng_in.tensor, 0, [[0, 128], [1, 4 * DH]]))
    dma_wait("sp", "dm")
    sp_mark()
    ve.wait_ge(done["sp"], cnt["sp"])
    ve.tensor_copy(boutr[0:1, :], f32a[0:1, 0:D])
    R("dve", ve.tensor_copy(lngt[:, :], f32a[:, 1026:1282]))
    ve.tensor_scalar(tabsc[:, 6:7], iotap[:, :], 63.5, 16.0, aop.is_gt, aop.add)
    ve.tensor_tensor(tabsc[:, 7:8], iotaf[:, 0:1], iotaf[:, 0:1], aop.subtract)
    ve.tensor_tensor(qa[:, 0:32], iotaf[:, 0:32],
                     bass.AP(tabsc, 6, [[128, 128], [0, 32]]), aop.is_equal)
    ve.tensor_copy(tabsc[0:64, 7:8], f32a[0:64, 1024:1025])
    ve.tensor_copy(tabsc[64:128, 7:8], f32a[64:128, 1025:1026])
    R("dve", ve.tensor_tensor(wpsh[:, :], qa[:, 0:32],
                              bass.AP(tabsc, 7, [[128, 128], [0, 32]]), aop.mult))
    fence(["dve"], ["pe", "sp"])

    for src, dst, cols in ((wq_in, wqkv_d, 0), (wkv_in, wqkv_d, D), (wout_in, woutb_d, 0)):
        for cc in range(DCH):
            dma("sp", "din", f32a[:, 0:D], src[cc * 128:(cc + 1) * 128, :])
            dma_wait("sp", "din")
            sp_mark()
            ve.wait_ge(done["sp"], cnt["sp"])
            R("dve", ve.tensor_copy(qa[:, 0:D], f32a[:, 0:D]))
            fence(["dve"], ["pool"])
            if dst is wqkv_d:
                dma("pool", "dw", bass.AP(dst.tensor, cc * 128 * 2 * D + cols,
                                          [[2 * D, 128], [1, D]]), qa[:, 0:D])
            else:
                dma("pool", "dw", bass.AP(dst.tensor, cc * 128 * D,
                                          [[D, 128], [1, D]]), qa[:, 0:D])
            dma_wait("pool", "dw")
            pool_mark()
            sp.wait_ge(done["pool"], cnt["pool"])

    for wg0 in range(0, N // WIN, 8):
        for wi in range(8):
            ve.tensor_scalar(f32a[0:1, 1024 + wi * 32:1024 + (wi + 1) * 32],
                             invf[0:1, :], float((wg0 + wi) * WIN), None, aop.mult)
        for wi in range(8):
            ve.tensor_copy(f32a[0:1, 1280 + wi * 32:1280 + (wi + 1) * 32],
                           invf[0:1, :])
        R("dve", ve.tensor_copy(f32a[0:1, 1280:1281], f32a[0:1, 1280:1281]))
        fence(["dve"], ["pe"])
        te.matmul(ps_big[:, 0:256], iotaf[0:1, :], f32a[0:1, 1280:1536],
                  start=True, stop=False)
        R("pe", te.matmul(ps_big[:, 0:256], onesf[0:1, :], f32a[0:1, 1024:1280],
                          start=False, stop=True))
        fence(["pe"], ["dve"])
        ve.tensor_copy(f32a[:, 0:256], ps_big[:, 0:256])
        for j in range(9, -1, -1):
            c = float(2.0 ** j * 2.0 * math.pi)
            ve.tensor_scalar(f32a[:, 256:512], f32a[:, 0:256], c, c,
                             aop.is_ge, aop.mult)
            ve.tensor_tensor(f32a[:, 0:256], f32a[:, 0:256], f32a[:, 256:512],
                             aop.subtract)
        ve.tensor_scalar(f32a[:, 256:512], f32a[:, 0:256], float(math.pi),
                         float(2 * math.pi), aop.is_gt, aop.mult)
        ve.tensor_tensor(f32a[:, 512:768], f32a[:, 0:256], f32a[:, 256:512],
                         aop.subtract)
        ve.tensor_scalar(f32a[:, 256:512], f32a[:, 0:256], float(math.pi / 2),
                         None, aop.add)
        ve.tensor_scalar(f32a[:, 768:1024], f32a[:, 256:512], float(math.pi),
                         float(2 * math.pi), aop.is_gt, aop.mult)
        R("dve", ve.tensor_tensor(f32a[:, 768:1024], f32a[:, 256:512],
                                  f32a[:, 768:1024], aop.subtract))
        fence(["dve"], ["act"])
        ac.activation(f32a[:, 0:256], f32a[:, 512:768], AF.Sin)
        R("act", ac.activation(f32a[:, 256:512], f32a[:, 768:1024], AF.Sin))
        fence(["act"], ["dve"])
        R("dve", ve.tensor_copy(qa[:, 0:512], f32a[:, 0:512]))
        fence(["dve"], ["pe"])
        for tt in range(4):
            R("pe", te.transpose(ps_t[:, tt * 128:(tt + 1) * 128],
                                 qa[:, tt * 128:(tt + 1) * 128], ident[:, :]))
        fence(["pe"], ["dve"])
        with nodrain():
            for wi in range(8):
                blk, rowo = divmod(wi * 32, 128)
                wc = (wg0 + wi) * WIN
                ve.tensor_scalar(sintab[0:32, 0:WIN] if False else qc1[0:32, wc % 512:wc % 512 + WIN], ps_t[0:1, 0:1], 0.0, None, aop.mult) if False else None
                ve.tensor_scalar(qc1[0:32, 0:WIN], ps_t[rowo:rowo + 32, blk * 128:(blk + 1) * 128],
                                 -1.0, None, aop.mult)
                ve.tensor_copy(qc1[32:64, 0:WIN], ps_t[rowo:rowo + 32, blk * 128:(blk + 1) * 128])
                ve.tensor_copy(qc1[64:96, 0:WIN], qc1[0:32, 0:WIN]) if False else None
                ve.tensor_copy(qc2[0:32, 0:WIN], ps_t[rowo:rowo + 32, (2 + blk) * 128:(3 + blk) * 128])
                R("dve", ve.tensor_copy(qc2[32:64, 0:WIN], ps_t[rowo:rowo + 32, (2 + blk) * 128:(3 + blk) * 128]))
                fence(["dve"], ["pool"])
                dma("pool", "dw", bass.AP(sintab_d.tensor, wc, [[N, 64], [1, WIN]]),
                    qc1[0:64, 0:WIN])
                dma("pool", "dw", bass.AP(sintab_d.tensor, 64 * N + wc, [[N, 64], [1, WIN]]),
                    qc1[0:64, 0:WIN])
                dma("pool", "dw", bass.AP(costab_d.tensor, wc, [[N, 64], [1, WIN]]),
                    qc2[0:64, 0:WIN])
                dma("pool", "dw", bass.AP(costab_d.tensor, 64 * N + wc, [[N, 64], [1, WIN]]),
                    qc2[0:64, 0:WIN])
                dma_wait("pool", "dw")
                pool_mark()
                ve.wait_ge(done["pool"], cnt["pool"])

    def stair(K, rsb_t, scb_t, ysrc, out_t, afp32=False, out_flat=None, final3=None):
        if afp32:
            ac.activation(qs[:, 0:K], ysrc, AF.Sign)
            R("act", ac.activation(ysrc, ysrc, AF.Abs))
            fence(["act"], ["dve"])
            a = ysrc
        else:
            ac.activation(qa[:, 0:K], ysrc, AF.Abs)
            R("act", ac.activation(qs[:, 0:K], ysrc, AF.Sign))
            fence(["act"], ["dve"])
            a = qa[:, 0:K]
        ve.tensor_scalar(qc1[:, 0:K], a, 0.25, None, aop.is_gt)
        ve.scalar_tensor_tensor(qc2[:, 0:K], a, 0.75, qc1[:, 0:K], aop.is_gt, aop.add)
        ve.scalar_tensor_tensor(qc1[:, 0:K], a, 1.25, qc2[:, 0:K], aop.is_gt, aop.add)
        ve.scalar_tensor_tensor(qc2[:, 0:K], a, 1.75, qc1[:, 0:K],
                                 aop.is_gt, aop.add)
        ve.tensor_scalar(qc1[:, 0:K], qc2[:, 0:K], 0.5, None, aop.mult)
        ve.scalar_tensor_tensor(qc2[:, 0:K], a, 2.5, qc1[:, 0:K], aop.is_gt, aop.add)
        ve.scalar_tensor_tensor(qc1[:, 0:K], a, 3.5, qc2[:, 0:K], aop.is_gt, aop.add)
        ve.tensor_scalar(qc2[:, 0:K], a, 5.0, 2.0, aop.is_gt, aop.mult)
        ve.tensor_tensor(qc1[:, 0:K], qc1[:, 0:K], qc2[:, 0:K], aop.add)
        ve.tensor_tensor(qc2[:, 0:K], qc1[:, 0:K], qs[:, 0:K], aop.mult)
        if final3 is not None:
            dst, srcv, scb = final3
            return R("dve", ve.tensor_tensor(dst, srcv, scb, aop.mult))
        scb = bass.AP(scb_t, 0, [[KI // 16, 128], [1, K // 16], [0, 16]])
        dst = out_flat if out_flat is not None else out_t.rearrange(
            "p (a b) -> p a b", b=16)
        return R("dve", ve.tensor_tensor(
            dst, qc2[:, 0:K].rearrange("p (a b) -> p a b", b=16),
            scb, aop.mult))

    def wquant(src_dram, dst_dram, nrows, kcols):
        nrt = nrows // 128
        kct = kcols // 128
        dst_rowlen = nrows
        for nr in range(nrt):
            sp.wait_ge(done["act"], cnt["act"])
            sp.wait_ge(done["dve"], cnt["dve"])
            dma("sp", "din", f32a[:, 0:kcols], src_dram[nr * 128:(nr + 1) * 128, :])
            dma_wait("sp", "din")
            sp_mark()
            ve.wait_ge(done["sp"], cnt["sp"])
            ve.wait_ge(done["pool"], cnt["pool"])
            nb = kcols // 16
            ve.tensor_reduce(qbm[:, 0:nb],
                             f32a[:, 0:kcols].rearrange("p (a b) -> p a b", b=16),
                             AX.X, aop.max, apply_absolute_value=True)
            ve.tensor_scalar(qsc[:, 0:nb], qbm[:, 0:nb], 1.0 / 6.0, 1e-12,
                             aop.mult, aop.max)
            ve.reciprocal(qrs[:, 0:nb], qsc[:, 0:nb])
            ve.tensor_copy(qscb[:, 0:nb], qsc[:, 0:nb])
            rsb = bass.AP(qrs, 0, [[KI // 16, 128], [1, nb], [0, 16]])
            R("dve", ve.tensor_tensor(
                f32a[:, 0:kcols].rearrange("p (a b) -> p a b", b=16),
                f32a[:, 0:kcols].rearrange("p (a b) -> p a b", b=16),
                rsb, aop.mult))
            fence(["dve"], ["act"])
            stair(kcols, qrs, qscb, f32a[:, 0:kcols], qc2[:, 0:kcols], afp32=True)
            fence(["dve"], ["pe"])
            for k0 in range(0, kct, 8):
                kn = min(8, kct - k0)
                for kk in range(kn):
                    R("pe", te.transpose(ps_t[:, kk * 128:(kk + 1) * 128],
                                         qc2[:, (k0 + kk) * 128:(k0 + kk + 1) * 128],
                                         ident[:, :]))
                fence(["pe"], ["dve"])
                R("dve", ve.tensor_copy(qa[:, 0:kn * 128],
                                        bass.AP(ps_t, 0, [[1024, 128], [1, kn * 128]])))
                fence(["dve"], ["pool", "pe"])
                dst = bass.AP(dst_dram.tensor, (k0 * 128) * dst_rowlen + nr * 128,
                              [[dst_rowlen, 128], [128 * dst_rowlen, kn], [1, 128]])
                dma("pool", "dw", dst,
                    bass.AP(qa, 0, [[KI, 128], [128, kn], [1, 128]]))
                dma_wait("pool", "dw")
                pool_mark()
                ve.wait_ge(done["pool"], cnt["pool"])
            last["pe"] = None

    wquant(wg_in, wgq_d, KI, D)
    wquant(wu_in, wuq_d, KI, D)
    wquant(wd_in, wdq_d, D, KI)
    po.sem_inc(done["pool"], 1); cnt["pool"] += 1
    for e in ("pe", "act", "dve", "sp"):
        ENG[e].wait_ge(done["pool"], cnt["pool"])

    for bb in range(B_CORE):
        for mt in range(MPB):
            t0 = mt * T
            w0 = mt * WPM
            sg0 = w0 * SPW
            nsg = WPM * SPW
            lt0, lr0 = divmod(sg0, 128)

            sp.wait_ge(done["dve"], cnt["dve"])
            sp.wait_ge(done["pe"], cnt["pe"])
            dma("sp", "dm", costab[:, :], costab_d[:, t0:t0 + T])
            dma("sp", "dm", sintab[:, :], sintab_d[:, t0:t0 + T])
            dma_wait("sp", "dm")
            sp_mark()
            for e in ("dve", "pe", "act"):
                ENG[e].wait_ge(done["sp"], cnt["sp"])
            if "cost" in tap_d:
                tap_wait()
                dma("pool", "dmp", tap_d["cost"][bb, mt], costab[:, :])

            po.wait_ge(done["dve"], cnt["dve"])
            po.wait_ge(done["act"], cnt["act"])
            xv = []
            for tcc in range(TC):
                dma("pool", "dinp", x_tm[:, tcc, :],
                    x_in[bb, t0 + tcc * 128:t0 + (tcc + 1) * 128, :])
                xv.append(dsem["dinp"][1])
            for tcc in range(TC):
                te.wait_ge(dsem["dinp"][0], xv[tcc])
                if tcc == 0:
                    te.wait_ge(done["dve"], cnt["dve"])
                for cc in range(DCH):
                    R("pe", te.transpose(ps_t[:, cc * 128:(cc + 1) * 128],
                                         x_tm[:, tcc, cc * 128:(cc + 1) * 128],
                                         ident[:, :]))
                fence(["pe"], ["dve"])
                R("dve", ve.tensor_copy(
                    bass.AP(xfw, tcc * 128, [[DCH * T, 128], [T, DCH], [1, 128]]),
                    bass.AP(ps_t, 0, [[1024, 128], [128, DCH], [1, 128]])))
                fence(["dve"], ["pe"])

            if "xtm" in tap_d:
                tap_wait()
                dma("pool", "dmp", tap_d["xtm"][bb, mt], x_tm[:, :, :])
            if "xf" in tap_d:
                tap_wait()
                dma("pool", "dmp", tap_d["xf"][bb, mt], xfw[:, :, :])
            ctile = bass.AP(costab, 0, [[T, 128], [0, 2], [1, T]])
            stile = bass.AP(sintab, 0, [[T, 128], [0, 2], [1, T]])
            qk_ev = [0] * (2 * DCH)
            rot_dve = {}
            wq_guard = {}
            for fc in range(2 * DCH):
                db = fc % 2
                if db in wq_guard:
                    waiton("pool", "pe", wq_guard[db])
                dma("pool", "dw", wq_sb[:, db, :],
                    bass.AP(wqkv_d.tensor, fc * 128,
                            [[2 * D, 128], [128 * 2 * D, DCH], [1, 128]]))
                te.wait_ge(dsem["dw"][0], dsem["dw"][1])
                if fc >= 2:
                    waiton("pe", "act", qk_ev[fc - 2])
                for cc in range(DCH):
                    R("pe", te.matmul(ps_big[:, db * 512:db * 512 + T],
                                      wq_sb[:, db, cc * 128:(cc + 1) * 128],
                                      xfw[:, cc, :], start=(cc == 0), stop=(cc == DCH - 1)))
                qmm = mark("pe")
                wq_guard[db] = qmm
                waiton("act", "pe", qmm)
                with nodrain():
                    if fc < DCH:
                        R("act", ac.activation(q_rot[:, fc, :], ps_big[:, db * 512:db * 512 + T],
                                               AF.Copy, bias=0.0, scale=0.125))
                    else:
                        R("act", ac.activation(kvro[:, fc - DCH, :],
                                               ps_big[:, db * 512:db * 512 + T],
                                               AF.Copy, bias=0.0, scale=1.0))
                qk_ev[fc] = mark("act")
                if fc % 2 == 1:
                    rp = fc // 2
                    buf = q_rot if fc < DCH else kvro
                    cc0 = (fc if fc < DCH else fc - DCH) - 1
                    rps = ps_big[:, 1024:2048] if rp % 2 == 0 else ps_s[:, 0:1024]
                    waiton("pe", "act", qk_ev[fc])
                    if rp >= 2:
                        waiton("pe", "dve", rot_dve[rp - 2])
                    te.matmul(rps[:, 0:T], p32[:, :], buf[:, cc0, :],
                              start=True, stop=True)
                    R("pe", te.matmul(rps[:, T:2 * T], p32[:, :], buf[:, cc0 + 1, :],
                                      start=True, stop=True))
                    rmm = mark("pe")
                    waiton("dve", "pe", rmm)
                    bb2 = buf[:, cc0:cc0 + 2, :]
                    ve.tensor_tensor(qa[:, 0:2 * T].rearrange("p (a t) -> p a t", t=T),
                                     bb2, ctile, aop.mult)
                    ve.tensor_tensor(qa[:, 2 * T:4 * T].rearrange("p (a t) -> p a t", t=T),
                                     rps[:, 0:2 * T].rearrange("p (a t) -> p a t", t=T),
                                     stile, aop.mult)
                    R("dve", ve.tensor_tensor(
                        bb2, qa[:, 0:2 * T].rearrange("p (a t) -> p a t", t=T),
                        qa[:, 2 * T:4 * T].rearrange("p (a t) -> p a t", t=T),
                        aop.add))
                    rot_dve[rp] = mark("dve")
            waiton("pe", "dve", rot_dve[DCH - 1])
            waiton("act", "dve", rot_dve[DCH - 1])

            if "qpre" in tap_d:
                tap_wait()
                dma("pool", "dmp", tap_d["qpre"][bb, mt], q_rot[:, :, :])
                dma_wait("pool", "dmp")
                pool_mark()
                for e in ("pe", "dve"):
                    ENG[e].wait_ge(done["pool"], cnt["pool"])
            if "qrot" in tap_d:
                tap_wait()
                dma("pool", "dmp", tap_d["qrot"][bb, mt], q_rot[:, :, :])

            for cc in range(DCH):
                R("pe", te.matmul(ps_p[0:H, 0:T], wpsh[:, 16 - 2 * cc:32 - 2 * cc],
                                  kvro[:, cc, :], start=(cc == 0),
                                  stop=(cc == DCH - 1)))
            fence(["pe"], ["act"])
            R("act", ac.activation(pscore[:, :], ps_p[0:H, 0:T], AF.Copy))
            fence(["act"], ["pe", "act"])
            if "pscore" in tap_d:
                tap_wait()
                dma("pool", "dmp", tap_d["pscore"][bb, mt], pscore[:, :])
                dma_wait("pool", "dmp")
                pool_mark()
                ac.wait_ge(done["pool"], cnt["pool"])
            R("act", ac.activation(pscore[:, :], pscore[:, :], AF.Exp))
            fence(["act"], ["dve"])
            ve.tensor_reduce(qbm[0:H, 0:T // 16],
                             pscore[:, :].rearrange("p (a b) -> p a b", b=16),
                             AX.X, aop.add)
            ve.reciprocal(qrs[0:H, 0:T // 16], qbm[0:H, 0:T // 16])
            rsb = bass.AP(qrs, 0, [[KI // 16, H], [1, T // 16], [0, 16]])
            R("dve", ve.tensor_tensor(
                pkv[:, :].rearrange("p (a b) -> p a b", b=16),
                pscore[:, :].rearrange("p (a b) -> p a b", b=16), rsb, aop.mult))
            fence(["dve"], ["pe"])
            for tcc in range(TC):
                R("pe", te.transpose(ps_t[:, tcc * H:(tcc + 1) * H],
                                     pkv[:, tcc * 128:(tcc + 1) * 128],
                                     ident[0:H, 0:H]))
            fence(["pe"], ["dve"])
            R("dve", ve.tensor_copy(
                bass.AP(pkv_km, 0, [[TC * H, 128], [H, TC], [1, H]]),
                bass.AP(ps_t, 0, [[1024, 128], [H, TC], [1, H]])))
            fence(["dve"], ["pe"])

            for w in range(WPM):
                te.wait_ge(done["dve"], cnt["dve"])
                for cc in range(DCH):
                    R("pe", te.transpose(ps_t[:, cc * 128:(cc + 1) * 128],
                                         kvro[:, cc, w * 128:(w + 1) * 128],
                                         ident[:, :]))
                fence(["pe"], ["dve"])
                R("dve", ve.tensor_copy(kv_km[:, w, :],
                                        bass.AP(ps_t, 0, [[1024, 128], [1, H * DH]])))
                fence(["dve"], ["pe"])
            if "kvkm" in tap_d:
                tap_wait()
                dma("pool", "dmp", tap_d["kvkm"][bb, mt], kv_km[:, :, :])

            for w in range(WPM):
                pk = bass.AP(pkv_km, w * H, [[TC * H, 128], [1, H], [0, DH]])
                R("dve", ve.tensor_tensor(
                    bass.AP(xfw, w * H * DH, [[DCH * T, 128], [DH, H], [1, DH]]),
                    kv_km[:, w, :].rearrange("p (h d) -> p h d", d=DH),
                    pk, aop.mult))
            fence(["dve"], ["pe"])
            for w in range(WPM):
                lr = w * SPW
                st = segf[:, 128 - lr:256 - lr]
                te.matmul(ps_p[:, 0:512], st,
                          bass.AP(xfw, w * H * DH, [[DCH * T, 128], [1, 512]]),
                          start=(w == 0), stop=(w == WPM - 1))
                R("pe", te.matmul(ps_big[:, 1536:2048], st,
                                  bass.AP(xfw, w * H * DH + 512,
                                          [[DCH * T, 128], [1, 512]]),
                                  start=(w == 0), stop=(w == WPM - 1)))
            fence(["pe"], ["act"])
            with nodrain():
                ac.activation(gkm[lr0:lr0 + nsg, lt0, 0:512], ps_p[0:nsg, 0:512],
                              AF.Copy)
                R("act", ac.activation(gkm[lr0:lr0 + nsg, lt0, 512:1024],
                                       ps_big[0:nsg, 1536:2048], AF.Copy))
            fence(["act"], ["pe", "dve"])
            if "gkm" in tap_d:
                tap_wait()
                dma("pool", "dmp", tap_d["gkm"][bb, mt], gkm[:, :, :])

            def ln64(src_ap, dst_ap, r0, nr, goff):
                R("dve", ve.tensor_reduce(
                    stats[r0:r0 + nr, 0:H],
                    src_ap.rearrange("p (h d) -> p h d", d=DH), AX.X, aop.add))
                fence(["dve"], ["act"])
                R("act", ac.activation(f32a[r0:r0 + nr, 0:H * DH], src_ap, AF.Square))
                fence(["act"], ["dve"])
                ve.tensor_reduce(stats[r0:r0 + nr, H:2 * H],
                                 f32a[r0:r0 + nr, 0:H * DH].rearrange(
                                     "p (h d) -> p h d", d=DH), AX.X, aop.add)
                ve.tensor_scalar(stats[r0:r0 + nr, 2 * H:3 * H],
                                 stats[r0:r0 + nr, 0:H], 1.0 / DH, None, aop.mult)
                ve.tensor_scalar(stats[r0:r0 + nr, 3 * H:4 * H],
                                 stats[r0:r0 + nr, H:2 * H], 1.0 / DH, None, aop.mult)
                ve.tensor_tensor(stats[r0:r0 + nr, 4 * H:5 * H],
                                 stats[r0:r0 + nr, 2 * H:3 * H],
                                 stats[r0:r0 + nr, 2 * H:3 * H], aop.mult)
                R("dve", ve.tensor_tensor(stats[r0:r0 + nr, 3 * H:4 * H],
                                          stats[r0:r0 + nr, 3 * H:4 * H],
                                          stats[r0:r0 + nr, 4 * H:5 * H], aop.subtract))
                fence(["dve"], ["act"])
                R("act", ac.activation(stats[r0:r0 + nr, 4 * H:5 * H],
                                       stats[r0:r0 + nr, 3 * H:4 * H], AF.Ln,
                                       bias=1e-5, scale=1.0))
                R("act", ac.activation(stats[r0:r0 + nr, 5 * H:6 * H],
                                       stats[r0:r0 + nr, 4 * H:5 * H], AF.Exp,
                                       bias=0.0, scale=-0.5))
                fence(["act"], ["dve"])
                srow = 8 * 4 * H
                mu_b = bass.AP(stats, r0 * srow + 2 * H, [[srow, nr], [1, H], [0, DH]])
                rs_b = bass.AP(stats, r0 * srow + 5 * H, [[srow, nr], [1, H], [0, DH]])
                g_b = bass.AP(lngt, r0 * 4 * DH + goff, [[4 * DH, nr], [0, H], [1, DH]])
                b_b = bass.AP(lngt, r0 * 4 * DH + goff + DH, [[4 * DH, nr], [0, H], [1, DH]])
                sq = f32a[r0:r0 + nr, 0:H * DH].rearrange("p (h d) -> p h d", d=DH)
                ve.tensor_tensor(sq, src_ap.rearrange("p (h d) -> p h d", d=DH),
                                 mu_b, aop.subtract)
                ve.tensor_tensor(sq, sq, rs_b, aop.mult)
                ve.tensor_tensor(sq, sq, g_b, aop.mult)
                R("dve", ve.tensor_tensor(dst_ap.rearrange("p (h d) -> p h d", d=DH),
                                          sq, b_b, aop.add))

            WH = WPM * H
            W2 = WPM // 2
            GH = W2 * H
            kvv = kv_km[:, :, :].rearrange("p w (h d) -> p (w h) d", d=DH)
            R("dve", ve.tensor_reduce(stats[:, 0:WH], kvv, AX.X, aop.add))
            fence(["dve"], ["act"])
            for hw in range(2):
                R("act", ac.activation(
                    f32a[:, 0:GH * DH],
                    kv_km[:, hw * W2:(hw + 1) * W2, :].rearrange(
                        "p w hd -> p (w hd)"), AF.Square))
                fence(["act"], ["dve"])
                R("dve", ve.tensor_reduce(
                    stats[:, WH + hw * GH:WH + (hw + 1) * GH],
                    f32a[:, 0:GH * DH].rearrange("p (g d) -> p g d", d=DH),
                    AX.X, aop.add))
                if hw == 0:
                    fence(["dve"], ["act"])
            ve.tensor_scalar(stats[:, 2 * WH:3 * WH], stats[:, 0:WH],
                             1.0 / DH, None, aop.mult)
            ve.tensor_scalar(stats[:, 3 * WH:4 * WH], stats[:, WH:2 * WH],
                             1.0 / DH, None, aop.mult)
            ve.tensor_tensor(stats[:, 4 * WH:5 * WH], stats[:, 2 * WH:3 * WH],
                             stats[:, 2 * WH:3 * WH], aop.mult)
            R("dve", ve.tensor_tensor(stats[:, 3 * WH:4 * WH],
                                      stats[:, 3 * WH:4 * WH],
                                      stats[:, 4 * WH:5 * WH], aop.subtract))
            fence(["dve"], ["act"])
            ac.activation(stats[:, 4 * WH:5 * WH], stats[:, 3 * WH:4 * WH],
                          AF.Ln, bias=1e-5, scale=1.0)
            R("act", ac.activation(stats[:, 5 * WH:6 * WH], stats[:, 4 * WH:5 * WH],
                                   AF.Exp, bias=0.0, scale=-0.5))
            fence(["act"], ["dve"])
            for hw in range(2):
                kvh = kv_km[:, hw * W2:(hw + 1) * W2, :].rearrange(
                    "p w (h d) -> p (w h) d", d=DH)
                mu_b = bass.AP(stats, 2 * WH + hw * GH,
                               [[8 * 4 * H, 128], [1, GH], [0, DH]])
                rs_b = bass.AP(stats, 5 * WH + hw * GH,
                               [[8 * 4 * H, 128], [1, GH], [0, DH]])
                g_b = bass.AP(lngt, 0, [[4 * DH, 128], [0, GH], [1, DH]])
                b_b = bass.AP(lngt, DH, [[4 * DH, 128], [0, GH], [1, DH]])
                sqv = f32a[:, 0:GH * DH].rearrange("p (g d) -> p g d", d=DH)
                ve.tensor_tensor(sqv, kvh, mu_b, aop.subtract)
                ve.tensor_tensor(sqv, sqv, rs_b, aop.mult)
                ve.tensor_tensor(sqv, sqv, g_b, aop.mult)
                R("dve", ve.tensor_tensor(
                    vh_km[:, 1 + hw * W2:1 + (hw + 1) * W2, :].rearrange(
                        "p w (h d) -> p (w h) d", d=DH),
                    sqv, b_b, aop.add))
            fence(["dve"], ["pe", "act"])
            if "vhkm" in tap_d:
                tap_wait()
                dma("pool", "dmp", tap_d["vhkm"][bb, mt], vh_km[:, 1:WPM + 1, :])
            ln64(gkm[lr0:lr0 + nsg, lt0, :], vg_km[lr0:lr0 + nsg, lt0, :],
                 lr0, nsg, 2 * DH)
            fence(["dve"], ["pe", "act"])
            if "vgkm" in tap_d:
                tap_wait()
                dma("pool", "dmp", tap_d["vgkm"][bb, mt], vg_km[:, :, :])

            for w in range(WPM):
                for cc in range(DCH):
                    R("pe", te.transpose(ps_t[:, cc * 128:(cc + 1) * 128],
                                         vh_km[:, w + 1, cc * 128:(cc + 1) * 128],
                                         ident[:, :]))
                fence(["pe"], ["dve"])
                R("dve", ve.tensor_copy(
                    bass.AP(vh_f, (w + 1) * WIN,
                            [[DCH * (WPM + 1) * WIN, 128], [(WPM + 1) * WIN, DCH],
                             [1, WIN]]),
                    bass.AP(ps_t, 0, [[1024, 128], [128, DCH], [1, 128]])))
                fence(["dve"], ["pe"])
            for cc in range(DCH):
                R("pe", te.transpose(ps_t[:, cc * 128:(cc + 1) * 128],
                                     vg_km[:, lt0, cc * 128:(cc + 1) * 128],
                                     ident[:, :]))
            fence(["pe"], ["dve"])
            R("dve", ve.tensor_copy(
                bass.AP(vg_f, lt0 * 128 + lr0,
                        [[DCH * LMT * 128, 128], [LMT * 128, DCH], [1, nsg]]),
                bass.AP(ps_t, lr0, [[1024, 128], [128, DCH], [1, nsg]])))
            fence(["dve"], ["pe"])

            av_pe = [0] * DCH
            av_ev = [0] * DCH
            NW = 2 * DCH * WPM
            pe_sg = [0] * NW
            act_eg = [0] * NW
            dve_ng = [0] * NW
            pe_tg = [0] * NW
            dve_vg = [0] * NW

            def geom(w):
                wg = w0 + w
                has_prev = wg > 0
                loc0 = 128 if has_prev else 0
                gn_old = wg * SPW
                gn = gn_old + SPW
                return wg, has_prev, loc0, gn_old, gn, loc0 + 128

            for g in range(NW + 1):
                if g < NW:
                    hidx = g // WPM
                    w = g % WPM
                    cc = hidx // 2
                    ro = (hidx % 2) * 64
                    wg, has_prev, loc0, gn_old, gn, off2 = geom(w)
                    par = g % 2
                    pso = par * 512
                    qs_ap = q_rot[ro:ro + 64, cc, w * 128:(w + 1) * 128]
                    if g >= 2:
                        waiton("pe", "act", act_eg[g - 2])
                    kst = (w if has_prev else w + 1) * WIN
                    te.matmul(ps_s[:, pso:pso + loc0 + 128], qs_ap,
                              vh_f[ro:ro + 64, cc, kst:kst + loc0 + 128],
                              start=True, stop=False)
                    te.matmul(ps_s[:, pso + off2:pso + off2 + gn], qs_ap,
                              vg_f[ro:ro + 64, cc, 0:gn], start=False,
                              stop=False)
                    te.matmul(ps_s[:, pso + loc0:pso + loc0 + 128],
                              negi[:, :], ucur[:, :], start=False, stop=False)
                    R("pe", te.matmul(
                        ps_s[:, pso + off2 + gn_old:pso + off2 + gn],
                        negi[:, :], useg[:, :], start=False, stop=True))
                    pe_sg[g] = mark("pe")
                if g >= 1:
                    gq = g - 1
                    hidx = gq // WPM
                    w = gq % WPM
                    cc = hidx // 2
                    hp = hidx % 2
                    hh = hidx
                    ro = hp * 64
                    wg, has_prev, loc0, gn_old, gn, off2 = geom(w)
                    par = gq % 2
                    pso = par * 512
                    ncols = off2 + gn
                    waiton("act", "pe", pe_sg[gq])
                    if gq >= 2:
                        waiton("act", "pe", pe_tg[gq - 2])
                    with nodrain():
                        R("act", ac.activation(
                            p_sb[:, pso:pso + ncols], ps_s[:, pso:pso + ncols],
                            AF.Exp, bias=-C_SOFT, scale=1.0,
                            accum_out=denom[:, par:par + 1]))
                    act_eg[gq] = mark("act")
                    waiton("dve", "act", act_eg[gq])
                    ve.reciprocal(denom[:, 2 + par:3 + par],
                                  denom[:, par:par + 1])
                    with nodrain():
                        R("dve", ve.tensor_scalar(
                            p_sb[:, pso:pso + ncols], p_sb[:, pso:pso + ncols],
                            denom[:, 2 + par:3 + par], None, aop.mult))
                    dve_ng[gq] = mark("dve")
                    waiton("pe", "dve", dve_ng[gq])
                    if gq >= 1:
                        waiton("pe", "dve", dve_vg[gq - 1])
                    nt = 0
                    if has_prev:
                        te.transpose(ps_t[:, 0:128],
                                     p_sb[:, pso:pso + 128], ident[:, :])
                        nt += 1
                    te.transpose(ps_t[:, nt * 128:nt * 128 + 128],
                                 p_sb[:, pso + loc0:pso + loc0 + 128],
                                 ident[:, :])
                    nt += 1
                    gdone = 0
                    while gdone < gn:
                        take = min(128, gn - gdone)
                        R("pe", te.transpose(
                            ps_t[0:take, nt * 128:nt * 128 + 128],
                            p_sb[:, pso + off2 + gdone:pso + off2 + gdone + take],
                            ident[:, :]))
                        nt += 1
                        gdone += take
                    pe_tg[gq] = mark("pe")
                    waiton("dve", "pe", pe_tg[gq])
                    ve.memset(pt_g[:, :, w * 128:(w + 1) * 128], 0.0)
                    ve.drain()
                    with nodrain():
                        nt = 0
                        if has_prev:
                            ve.tensor_copy(pt_loc[:, w, 128:256],
                                           ps_t[:, 0:128])
                            nt += 1
                        ve.tensor_copy(pt_loc[:, w + 1, 0:128],
                                       ps_t[:, nt * 128:nt * 128 + 128])
                        nt += 1
                        gdone = 0
                        while gdone < gn:
                            take = min(128, gn - gdone)
                            ve.tensor_copy(pt_g[0:take, gdone // 128,
                                                w * 128:(w + 1) * 128],
                                           ps_t[0:take, nt * 128:nt * 128 + 128])
                            nt += 1
                            gdone += take
                        R("dve", ve.tensor_copy(denom[:, 4 + par:5 + par],
                                                denom[:, 2 + par:3 + par]))
                    dve_vg[gq] = mark("dve")
                    if w == WPM - 1:
                        avo = (cc % 2) * 1024
                        waiton("pe", "dve", dve_vg[gq])
                        if cc >= 2 and hp == 0:
                            waiton("pe", "dve", av_ev[cc - 2])
                        for aw in range(WPM):
                            awg = w0 + aw
                            te.matmul(ps_big[ro:ro + 64, avo + aw * 128:avo + (aw + 1) * 128],
                                      vh_km[:, aw + 1, hh * DH:(hh + 1) * DH],
                                      pt_loc[:, aw + 1, 0:128], start=(aw == 0), stop=False)
                            if awg > 0:
                                te.matmul(ps_big[ro:ro + 64, avo + aw * 128:avo + (aw + 1) * 128],
                                          vh_km[:, aw, hh * DH:(hh + 1) * DH],
                                          pt_loc[:, aw, 128:256], start=False, stop=False)
                        ngl = (w0 + WPM) * SPW
                        gdone = 0
                        while gdone < ngl:
                            take = min(128, ngl - gdone)
                            R("pe", te.matmul(ps_big[ro:ro + 64, avo:avo + T],
                                              vg_km[0:take, gdone // 128,
                                                    hh * DH:(hh + 1) * DH],
                                              pt_g[0:take, gdone // 128, 0:T],
                                              start=False, stop=(gdone + take >= ngl)))
                            gdone += take
                        av_pe[cc] = mark("pe")
                        if hp == 1:
                            waiton("dve", "pe", av_pe[cc])
                            with nodrain():
                                R("dve", ve.tensor_copy(kvro[:, cc, :],
                                                        ps_big[:, avo:avo + T]))
                            av_ev[cc] = mark("dve")
            waiton("pe", "dve", av_ev[DCH - 1])
            waiton("act", "dve", av_ev[DCH - 1])
            if "attnf" in tap_d:
                tap_wait()
                dma("pool", "dmp", tap_d["attnf"][bb, mt], kvro[:, :, :])

            wo_guard = {}
            stt_m = [0] * TC
            xq_dve = [0] * TC
            for tcc in range(TC):
                half = (tcc % 2) * 1024
                sub = tcc % 2
                lo = sub * 6
                for cc in range(DCH):
                    db = cc % 2
                    if db in wo_guard:
                        waiton("pool", "pe", wo_guard[db])
                    dma("pool", "dw", wo_sb[:, db, :],
                        bass.AP(woutb_d.tensor, cc * 128 * D, [[D, 128], [1, D]]))
                    te.wait_ge(dsem["dw"][0], dsem["dw"][1])
                    if cc == 0:
                        if tcc == 0:
                            te.wait_ge(done["dve"], cnt["dve"])
                            te.wait_ge(done["act"], cnt["act"])
                        elif tcc >= 2:
                            waiton("pe", "dve", stt_m[tcc - 2])
                    for nh in range(2):
                        R("pe", te.matmul(
                            ps_big[:, half + nh * 512:half + (nh + 1) * 512],
                            kvro[:, cc, tcc * 128:(tcc + 1) * 128],
                            wo_sb[:, db, nh * 512:(nh + 1) * 512],
                            start=(cc == 0), stop=False))
                    last["pe"].then_inc(done["pe"], 1)
                    cnt["pe"] += 1
                    last["pe"] = None
                    wo_guard[db] = cnt["pe"]
                te.matmul(ps_big[:, half:half + 512], onesr[0:1, :],
                          boutr[0:1, 0:512], start=False, stop=True)
                R("pe", te.matmul(ps_big[:, half + 512:half + 1024],
                                  onesr[0:1, :], boutr[0:1, 512:1024],
                                  start=False, stop=True))
                wmm = mark("pe")
                waiton("dve", "pe", wmm)
                R("dve", ve.scalar_tensor_tensor(
                    x1[:, tcc, :], ps_big[:, half:half + D], 0.0,
                    x_tm[:, tcc, :], aop.add, aop.add,
                    accum_out=rowsum[:, sub * 2:sub * 2 + 1]))
                stt_m[tcc] = mark("dve")
                waiton("act", "dve", stt_m[tcc])
                R("act", ac.activation(f32a[:, 2 * D:3 * D], x1[:, tcc, :], AF.Square,
                                       accum_out=rowsum[:, sub * 2 + 1:sub * 2 + 2]))
                fence(["act"], ["dve"])
                ve.tensor_scalar(lnsc[:, lo:lo + 1], rowsum[:, sub * 2:sub * 2 + 1],
                                 1.0 / D, None, aop.mult)
                ve.tensor_scalar(lnsc[:, lo + 1:lo + 2],
                                 rowsum[:, sub * 2 + 1:sub * 2 + 2],
                                 1.0 / D, None, aop.mult)
                ve.tensor_tensor(lnsc[:, lo + 2:lo + 3], lnsc[:, lo:lo + 1],
                                 lnsc[:, lo:lo + 1], aop.mult)
                R("dve", ve.tensor_tensor(lnsc[:, lo + 1:lo + 2], lnsc[:, lo + 1:lo + 2],
                                          lnsc[:, lo + 2:lo + 3], aop.subtract))
                fence(["dve"], ["act"])
                ac.activation(lnsc[:, lo + 3:lo + 4], lnsc[:, lo + 1:lo + 2], AF.Ln,
                              bias=1e-5, scale=1.0)
                R("act", ac.activation(lnsc[:, lo + 4:lo + 5], lnsc[:, lo + 3:lo + 4],
                                       AF.Exp, bias=0.0, scale=-0.5))
                fence(["act"], ["dve"])
                ve.tensor_tensor(lnsc[:, lo + 5:lo + 6], lnsc[:, lo:lo + 1],
                                 lnsc[:, lo + 4:lo + 5], aop.mult)
                R("dve", ve.tensor_scalar(lnsc[:, lo + 5:lo + 6], lnsc[:, lo + 5:lo + 6],
                                          -1.0, None, aop.mult))
                fence(["dve"], ["act"])
                R("act", ac.activation(x1[:, tcc, :], x1[:, tcc, :], AF.Identity,
                                       bias=lnsc[:, lo + 5:lo + 6],
                                       scale=lnsc[:, lo + 4:lo + 5]))
                apm = mark("act")
                yst = sub * D
                nb = D // 16
                waiton("dve", "act", apm)
                ve.tensor_reduce(qbm[:, 0:nb],
                                 x1[:, tcc, :].rearrange("p (a b) -> p a b", b=16),
                                 AX.X, aop.max, apply_absolute_value=True)
                ve.tensor_scalar(qsc[:, 0:nb], qbm[:, 0:nb], 1.0 / 6.0, 1e-12,
                                 aop.mult, aop.max)
                ve.reciprocal(qrs[:, 0:nb], qsc[:, 0:nb])
                rsb = bass.AP(qrs, 0, [[KI // 16, 128], [1, nb], [0, 16]])
                R("dve", ve.tensor_tensor(
                    f32a[:, yst:yst + D].rearrange("p (a b) -> p a b", b=16),
                    x1[:, tcc, :].rearrange("p (a b) -> p a b", b=16),
                    rsb, aop.mult))
                fence(["dve"], ["act"])
                stair(D, qrs, qsc, f32a[:, yst:yst + D], None, afp32=True,
                      out_flat=bass.AP(xfw, tcc * D,
                                       [[DCH * T, 128], [16, D // 16], [1, 16]]))
                xq_dve[tcc] = mark("dve")
            if "x1" in tap_d:
                tap_wait()
                dma("pool", "dmp", tap_d["x1"][bb, mt], x1[:, :, :])

            for tcc in range(TC):
                waiton("pe", "dve", xq_dve[tcc])
                for cc in range(DCH):
                    R("pe", te.transpose(
                        ps_t[:, cc * 128:(cc + 1) * 128],
                        bass.AP(xfw, tcc * D + cc * 128, [[DCH * T, 128], [1, 128]]),
                        ident[:, :]))
                fence(["pe"], ["dve"])
                R("dve", ve.tensor_copy(
                    bass.AP(q_rot, tcc * 128, [[DCH * T, 128], [T, DCH], [1, 128]]),
                    bass.AP(ps_t, 0, [[1024, 128], [128, DCH], [1, 128]])))
                fence(["dve"], ["pe"])
            if "x1q" in tap_d:
                for tcc in range(TC):
                    dma("pool", "dmp", tap_d["x1q"][bb, mt, tcc],
                        bass.AP(q_rot, tcc * 128, [[DCH * T, 128], [T, DCH], [1, 128]]))

            pe_g = [0] * (NKB + 2)
            pe_u = [0] * (NKB + 2)
            act_s = [0] * (NKB + 2)
            dve_m = [0] * (NKB + 2)
            pe_h = [0] * (NKB + 2)
            dve_e = [0] * (NKB + 2)
            pool_w = [0] * (NKB + 2)
            for ncb in range(NKB + 2):
                par = ncb % 2
                if ncb < NKB:
                    if ncb >= 2:
                        waiton("pool", "pe", pe_g[ncb - 2])
                        waiton("pool", "pe", pe_u[ncb - 2])
                    dma("pool", "dw", wg_sb[:, par, :],
                        bass.AP(wgq_d.tensor, ncb * 128,
                                [[KI, 128], [128 * KI, DCH], [1, 128]]))
                    dma("pool", "dw", wu_sb[:, par, :],
                        bass.AP(wuq_d.tensor, ncb * 128,
                                [[KI, 128], [128 * KI, DCH], [1, 128]]))
                    pool_w[ncb] = dsem["dw"][1]
                    te.wait_ge(dsem["dw"][0], pool_w[ncb])
                    if ncb >= 2:
                        waiton("pe", "act", act_s[ncb - 2])
                    for cc in range(DCH):
                        R("pe", te.matmul(ps_big[:, par * 512:par * 512 + 512],
                                          wg_sb[:, par, cc * 128:(cc + 1) * 128],
                                          q_rot[:, cc, :],
                                          start=(cc == 0), stop=(cc == DCH - 1)))
                    pe_g[ncb] = mark("pe")
                    waiton("act", "pe", pe_g[ncb])
                    if ncb >= 2:
                        waiton("act", "dve", dve_m[ncb - 2])
                    with nodrain():
                        R("act", ac.activation(sg2[:, par, :],
                                               ps_big[:, par * 512:par * 512 + 512],
                                               AF.Silu))
                    act_s[ncb] = mark("act")
                    if ncb >= 2:
                        waiton("pe", "dve", dve_m[ncb - 2])
                    for cc in range(DCH):
                        R("pe", te.matmul(ps_big[:, 1024 + par * 512:1024 + par * 512 + 512],
                                          wu_sb[:, par, cc * 128:(cc + 1) * 128],
                                          q_rot[:, cc, :],
                                          start=(cc == 0), stop=(cc == DCH - 1)))
                    pe_u[ncb] = mark("pe")
                    waiton("dve", "act", act_s[ncb])
                    waiton("dve", "pe", pe_u[ncb])
                    if ncb >= 2:
                        waiton("dve", "pe", pe_h[ncb - 2])
                    with nodrain():
                        R("dve", ve.tensor_tensor(hf2[:, par, :],
                                                  ps_big[:, 1024 + par * 512:1024 + par * 512 + 512],
                                                  sg2[:, par, :], aop.mult))
                    dve_m[ncb] = mark("dve")
                if 0 <= ncb - 1 < NKB:
                    nch = ncb - 1
                    parh = nch % 2
                    waiton("pe", "dve", dve_m[nch])
                    if nch >= 2:
                        waiton("pe", "dve", dve_e[nch - 2])
                    pto = parh * 512
                    for tcc in range(TC):
                        R("pe", te.transpose(ps_t[:, pto + tcc * 128:pto + (tcc + 1) * 128],
                                             hf2[:, parh, tcc * 128:(tcc + 1) * 128],
                                             ident[:, :]))
                    pe_h[nch] = mark("pe")
                    waiton("dve", "pe", pe_h[nch])
                    with nodrain():
                        R("dve", ve.tensor_copy(
                            bass.AP(h_tm, nch * 128, [[TC * KI, 128], [KI, TC], [1, 128]]),
                            bass.AP(ps_t, pto, [[1024, 128], [128, TC], [1, 128]])))
                    dve_e[nch] = mark("dve")
                    if QFOLD and nch % 2 == 1:
                        ng = 2
                        c0 = (nch - ng + 1) * 128
                        nbk = ng * 8
                        kw = ng * 128
                        hview = bass.AP(h_tm, c0,
                                        [[TC * KI, 128], [KI, TC], [16, nbk], [1, 16]])
                        R("dve", ve.tensor_reduce(qbm[:, 0:TC * nbk], hview, AX.X,
                                                  aop.max, apply_absolute_value=True))
                        ve.tensor_scalar(qsc[:, 0:TC * nbk], qbm[:, 0:TC * nbk],
                                         1.0 / 6.0, 1e-12, aop.mult, aop.max)
                        R("dve", ve.reciprocal(qrs[:, 0:TC * nbk], qsc[:, 0:TC * nbk]))
                        rsb4 = bass.AP(qrs, 0, [[KI // 16, 128], [nbk, TC], [1, nbk], [0, 16]])
                        R("dve", ve.tensor_tensor(
                            bass.AP(qc1, 0, [[KI, 128], [kw, TC], [16, nbk], [1, 16]]),
                            hview, rsb4, aop.mult))
                        fence(["dve"], ["act"])
                        stair(TC * kw, None, None, qc1[:, 0:TC * kw], None, final3=(
                            bass.AP(h_tm, c0, [[TC * KI, 128], [KI, TC], [16, nbk], [1, 16]]),
                            bass.AP(qc2, 0, [[KI, 128], [kw, TC], [16, nbk], [1, 16]]),
                            bass.AP(qsc, 0, [[KI // 16, 128], [nbk, TC], [1, nbk], [0, 16]])))
            if "htm" in tap_d:
                tap_wait()
                dma("pool", "dmp", tap_d["htm"][bb, mt], h_tm[:, :, :])

            dve_stt = 0
            wd_guard = {}
            for tp in range(TC // 2):
                if tp == 0:
                    if last.get("dve") is not None:
                        fence(["dve"], ["pe"])
                    else:
                        waiton("pe", "dve", cnt["dve"])
                else:
                    waiton("pe", "dve", dve_stt)
                if TPAR:
                    evq = []
                    for sub in range(2):
                        tcc = tp * 2 + sub
                        for k0 in range(0, NKB, 4):
                            kn = min(4, NKB - k0)
                            half = len(evq) % 2
                            if len(evq) >= TDEPTH:
                                waiton("pe", "dve", evq[-TDEPTH])
                            for kk in range(kn):
                                R("pe", te.transpose(
                                    ps_t[:, half * 512 + kk * 128:half * 512 + (kk + 1) * 128],
                                    h_tm[:, tcc, (k0 + kk) * 128:(k0 + kk + 1) * 128],
                                    ident[:, :]))
                            ptb = mark("pe")
                            waiton("dve", "pe", ptb)
                            with nodrain():
                                R("dve", ve.tensor_copy(
                                    bass.AP(hq_f, k0 * 256 + sub * 128,
                                            [[NKB * 256, 128], [256, kn], [1, 128]]),
                                    bass.AP(ps_t, half * 512, [[1024, 128], [128, kn], [1, 128]])))
                            evq.append(mark("dve"))
                    waiton("pe", "dve", evq[-1])
                else:
                    for sub in range(2):
                        tcc = tp * 2 + sub
                        for k0 in range(0, NKB, 8):
                            kn = min(8, NKB - k0)
                            for kk in range(kn):
                                R("pe", te.transpose(ps_t[:, kk * 128:(kk + 1) * 128],
                                                     h_tm[:, tcc, (k0 + kk) * 128:(k0 + kk + 1) * 128],
                                                     ident[:, :]))
                            fence(["pe"], ["dve"])
                            R("dve", ve.tensor_copy(
                                bass.AP(hq_f, k0 * 256 + sub * 128,
                                        [[NKB * 256, 128], [256, kn], [1, 128]]),
                                bass.AP(ps_t, 0, [[1024, 128], [128, kn], [1, 128]])))
                            fence(["dve"], ["pe"])
                for kk in range(NKB):
                    db = kk % 3
                    if db in wd_guard:
                        waiton("pool", "pe", wd_guard[db])
                    dma("pool", "dw", wd_sb[:, db, :],
                        bass.AP(wdq_d.tensor, kk * 128 * D, [[D, 128], [1, D]]))
                    if PREFETCH:
                        te.wait_ge(dsem["dw"][0], dsem["dw"][1])
                    else:
                        dma_wait("pool", "dw")
                        pool_mark()
                        te.wait_ge(done["pool"], cnt["pool"])
                    for sub in range(2):
                        for nh in range(2):
                            R("pe", te.matmul(
                                ps_big[:, sub * 1024 + nh * 512:sub * 1024 + (nh + 1) * 512],
                                hq_f[:, kk, sub * 128:(sub + 1) * 128],
                                wd_sb[:, db, nh * 512:(nh + 1) * 512],
                                start=(kk == 0), stop=(kk == NKB - 1)))
                    last["pe"].then_inc(done["pe"], 1)
                    cnt["pe"] += 1
                    last["pe"] = None
                    wd_guard[db] = cnt["pe"]
                mm_done = cnt["pe"]
                waiton("dve", "pe", mm_done)
                for sub in range(2):
                    tcc = tp * 2 + sub
                    R("dve", ve.scalar_tensor_tensor(
                        x1[:, tcc, :], ps_big[:, sub * 1024:sub * 1024 + D], 0.0,
                        x1[:, tcc, :], aop.add, aop.add,
                        accum_out=rowsum[:, sub * 2:sub * 2 + 1]))
                dve_stt = mark("dve")
                for sub in range(2):
                    tcc = tp * 2 + sub
                    lo = sub * 6
                    if sub == 0:
                        waiton("act", "dve", dve_stt)
                    R("act", ac.activation(f32a[:, sub * D:(sub + 1) * D],
                                           x1[:, tcc, :], AF.Square,
                                           accum_out=rowsum[:, sub * 2 + 1:sub * 2 + 2]))
                    fence(["act"], ["dve"])
                    ve.tensor_scalar(lnsc[:, lo:lo + 1], rowsum[:, sub * 2:sub * 2 + 1],
                                     1.0 / D, None, aop.mult)
                    ve.tensor_scalar(lnsc[:, lo + 1:lo + 2],
                                     rowsum[:, sub * 2 + 1:sub * 2 + 2],
                                     1.0 / D, None, aop.mult)
                    ve.tensor_tensor(lnsc[:, lo + 2:lo + 3], lnsc[:, lo:lo + 1],
                                     lnsc[:, lo:lo + 1], aop.mult)
                    R("dve", ve.tensor_tensor(lnsc[:, lo + 1:lo + 2], lnsc[:, lo + 1:lo + 2],
                                              lnsc[:, lo + 2:lo + 3], aop.subtract))
                    fence(["dve"], ["act"])
                    ac.activation(lnsc[:, lo + 3:lo + 4], lnsc[:, lo + 1:lo + 2], AF.Ln,
                                  bias=1e-5, scale=1.0)
                    R("act", ac.activation(lnsc[:, lo + 4:lo + 5], lnsc[:, lo + 3:lo + 4],
                                           AF.Exp, bias=0.0, scale=-0.5))
                    fence(["act"], ["dve"])
                    ve.tensor_tensor(lnsc[:, lo + 5:lo + 6], lnsc[:, lo:lo + 1],
                                     lnsc[:, lo + 4:lo + 5], aop.mult)
                    R("dve", ve.tensor_scalar(lnsc[:, lo + 5:lo + 6], lnsc[:, lo + 5:lo + 6],
                                              -1.0, None, aop.mult))
                    fence(["dve"], ["act"])
                    R("act", ac.activation(x1[:, tcc, :], x1[:, tcc, :], AF.Identity,
                                           bias=lnsc[:, lo + 5:lo + 6],
                                           scale=lnsc[:, lo + 4:lo + 5]))
                    apm = mark("act")
                    if SPSTORE:
                        waiton("sp", "act", apm)
                        dma("sp", "dout",
                            out_d[bb, t0 + tcc * 128:t0 + (tcc + 1) * 128, :],
                            x1[:, tcc, :])
                    else:
                        waiton("pool", "act", apm)
                        dma("pool", "dmp",
                            out_d[bb, t0 + tcc * 128:t0 + (tcc + 1) * 128, :],
                            x1[:, tcc, :])
            if SPSTORE:
                dma_wait("sp", "dout")
                sp_mark()
                waiton("act", "sp", cnt["sp"])
                waiton("dve", "sp", cnt["sp"])
            else:
                dma_wait("pool", "dmp")
                pool_mark()
                waiton("act", "pool", cnt["pool"])
                waiton("dve", "pool", cnt["pool"])

            ve.wait_ge(done["pe"], cnt["pe"])
            ve.tensor_copy(vh_km[:, 0, :], vh_km[:, WPM, :])
            R("dve", ve.tensor_copy(
                bass.AP(vh_f, 0, [[DCH * (WPM + 1) * WIN, 128],
                                  [(WPM + 1) * WIN, DCH], [1, WIN]]),
                bass.AP(vh_f, WPM * WIN, [[DCH * (WPM + 1) * WIN, 128],
                                          [(WPM + 1) * WIN, DCH], [1, WIN]])))
            fence(["dve"], ["pe", "act", "pool", "sp"])

    if SPSTORE:
        dma_wait("sp", "dout")
    dma_wait("pool", "dmp")
    po.sem_inc(done["pool"], 1)

    for cm in reversed(ctxs):
        cm.__exit__(None, None, None)
    return nc


def kernel(L0_attn_x, L0_attn_Wq, L0_attn_Wkv, L0_attn_Wproj, L0_attn_Wout,
           L0_attn_bout, L0_attn_ln_local_g, L0_attn_ln_local_b,
           L0_attn_ln_global_g, L0_attn_ln_global_b,
           L0_ffn_w_gate, L0_ffn_w_up, L0_ffn_w_down):
    x = np.ascontiguousarray(np.asarray(L0_attn_x, dtype=np.float32))
    lng = np.ascontiguousarray(np.stack(
        [np.asarray(L0_attn_ln_local_g, np.float32),
         np.asarray(L0_attn_ln_local_b, np.float32),
         np.asarray(L0_attn_ln_global_g, np.float32),
         np.asarray(L0_attn_ln_global_b, np.float32)], axis=0))
    common = {
        "wq": np.ascontiguousarray(np.asarray(L0_attn_Wq, np.float32)),
        "wkv": np.ascontiguousarray(np.asarray(L0_attn_Wkv, np.float32)),
        "wproj": np.ascontiguousarray(np.asarray(L0_attn_Wproj, np.float32)[:, 0]),
        "wout": np.ascontiguousarray(np.asarray(L0_attn_Wout, np.float32)),
        "bout": np.ascontiguousarray(np.asarray(L0_attn_bout, np.float32)),
        "lng": lng,
        "wgate": np.ascontiguousarray(np.asarray(L0_ffn_w_gate, np.float32)),
        "wup": np.ascontiguousarray(np.asarray(L0_ffn_w_up, np.float32)),
        "wdown": np.ascontiguousarray(np.asarray(L0_ffn_w_down, np.float32)),
    }
    n_cores = 8
    bpc = x.shape[0] // n_cores
    nc = build_program(B_CORE=bpc, N=x.shape[1])
    in_maps = [dict(common, x=np.ascontiguousarray(x[i * bpc:(i + 1) * bpc]))
               for i in range(n_cores)]
    res = run_bass_kernel_spmd(nc, in_maps, core_ids=list(range(n_cores)))
    return np.concatenate([np.asarray(r["out"]) for r in res.results], axis=0)


# version 25
# speedup vs baseline: 1.0652x; 1.0174x over previous
import math
import numpy as np

import concourse.bass as bass
import concourse.mybir as mybir
from concourse.alu_op_type import AluOpType as aop
from concourse.bass_utils import run_bass_kernel_spmd

F32 = mybir.dt.float32
BF16 = mybir.dt.bfloat16
AF = mybir.ActivationFunctionType
AX = mybir.AxisListType

D = 1024
H = 16
DH = 64
DCH = D // 128
KI = 2816
NKB = KI // 128
WIN = 128
SEGL = 16
SPW = WIN // SEGL
C_SOFT = 12.0





QFOLD = True
SPSTORE = True
PREFETCH = True
TPAR = False
TDEPTH = 1


def build_program(B_CORE=2, N=4096, T=512, taps=()):
    NW = N // WIN
    WPM = T // WIN
    MPB = NW // WPM
    NSEG = N // SEGL
    LMT = (NSEG + 127) // 128
    TC = T // 128
    taps = set(taps)

    nc = bass.Bass("TRN2", target_bir_lowering=False, detect_race_conditions=False)

    last = {}
    drain_on = [True]

    class nodrain:
        def __enter__(self):
            drain_on.append(False)

        def __exit__(self, *a):
            drain_on.pop()

    class EngProxy:
        _SKIP = {"wait_ge", "sem_inc", "wait_op", "nop", "engine_nop", "drain",
                 "sem_clear"}
        _NODRAIN = {"dma_start", "dma_start_transpose"}

        def __init__(self, eng, name):
            self._eng = eng
            self._name = name

        def __getattr__(self, m):
            f = getattr(self._eng, m)
            if not callable(f) or m in self._SKIP:
                return f
            name = self._name
            eng = self._eng
            drainable = name in ("dve", "act") and m not in self._NODRAIN

            def wrap(*a, **k):
                r = f(*a, **k)
                if hasattr(r, "then_inc"):
                    last[name] = r
                    if drainable and a and isinstance(a[0], bass.AP):
                        out = a[0]
                        try:
                            free = 1
                            for s in out.shape[1:]:
                                free *= s
                            nbytes = free * mybir.dt.size(out.dtype)
                        except Exception:
                            nbytes = 0
                        if nbytes < 2048 and drain_on[-1]:
                            eng.drain()
                return r
            return wrap

    te = EngProxy(nc.tensor, "pe")
    ac = EngProxy(nc.scalar, "act")
    ve = EngProxy(nc.vector, "dve")
    po = EngProxy(nc.gpsimd, "pool")
    sp = EngProxy(nc.sync, "sp")

    x_in = nc.dram_tensor("x", [B_CORE, N, D], F32, kind="ExternalInput").ap()
    wq_in = nc.dram_tensor("wq", [D, D], F32, kind="ExternalInput").ap()
    wkv_in = nc.dram_tensor("wkv", [D, D], F32, kind="ExternalInput").ap()
    wproj_in = nc.dram_tensor("wproj", [DH], F32, kind="ExternalInput").ap()
    wout_in = nc.dram_tensor("wout", [D, D], F32, kind="ExternalInput").ap()
    bout_in = nc.dram_tensor("bout", [D], F32, kind="ExternalInput").ap()
    lng_in = nc.dram_tensor("lng", [4, DH], F32, kind="ExternalInput").ap()
    wg_in = nc.dram_tensor("wgate", [KI, D], F32, kind="ExternalInput").ap()
    wu_in = nc.dram_tensor("wup", [KI, D], F32, kind="ExternalInput").ap()
    wd_in = nc.dram_tensor("wdown", [D, KI], F32, kind="ExternalInput").ap()
    out_d = nc.dram_tensor("out", [B_CORE, N, D], F32, kind="ExternalOutput").ap()
    wqkv_d = nc.dram_tensor("wqkv_bf", [D, 2 * D], BF16).ap()
    woutb_d = nc.dram_tensor("wout_bf", [D, D], BF16).ap()
    wgq_d = nc.dram_tensor("wgq_t", [D, KI], BF16).ap()
    costab_d = nc.dram_tensor("costab_d", [128, N], BF16).ap()
    sintab_d = nc.dram_tensor("sintab_d", [128, N], BF16).ap()
    wuq_d = nc.dram_tensor("wuq_t", [D, KI], BF16).ap()
    wdq_d = nc.dram_tensor("wdq_t", [KI, D], BF16).ap()

    tap_d = {}
    def def_tap(name, shape, dt=F32):
        if name in taps:
            tap_d[name] = nc.dram_tensor("tap_" + name, list(shape), dt,
                                         kind="ExternalOutput").ap()
    def_tap("xtm", [B_CORE, MPB, 128, TC, D], dt=BF16)
    def_tap("xf", [B_CORE, MPB, 128, DCH, T], dt=BF16)
    def_tap("qpre", [B_CORE, MPB, 128, DCH, T], dt=BF16)
    def_tap("qrot", [B_CORE, MPB, 128, DCH, T], dt=BF16)
    def_tap("kvkm", [B_CORE, MPB, 128, WPM, H * DH], dt=BF16)
    def_tap("vhkm", [B_CORE, MPB, 128, WPM, H * DH], dt=BF16)
    def_tap("gkm", [B_CORE, MPB, 128, LMT, H * DH], dt=BF16)
    def_tap("vgkm", [B_CORE, MPB, 128, LMT, H * DH], dt=BF16)
    def_tap("pscore", [B_CORE, MPB, H, T], dt=F32)
    def_tap("attnf", [B_CORE, MPB, 128, DCH, T], dt=BF16)
    def_tap("x1", [B_CORE, MPB, 128, TC, D], dt=F32)
    def_tap("x1q", [B_CORE, MPB, TC, 128, D], dt=BF16)
    def_tap("htm", [B_CORE, MPB, 128, TC, KI], dt=BF16)
    def_tap("cost", [B_CORE, MPB, 128, T], dt=BF16)

    ctxs = []
    def mk(cm):
        h = cm.__enter__()
        ctxs.append(cm)
        return h

    done = {e: mk(nc.semaphore("dn_" + e)) for e in ("pe", "act", "dve", "pool", "sp")}
    cnt = {e: 0 for e in done}
    dsem = {s: [mk(nc.semaphore("dm_" + s)), 0]
            for s in ("din", "dw", "dout", "dm", "dmp", "dinp")}

    def R(eng, instr):
        return instr

    ENG = {"pe": te, "act": ac, "dve": ve, "pool": po, "sp": sp}

    def fence(producers, consumers):
        for pe_ in producers:
            ins = last.get(pe_)
            assert ins is not None, f"fence: nothing emitted on {pe_}"
            ins.then_inc(done[pe_], 1)
            cnt[pe_] += 1
            last[pe_] = None
        for ce in consumers:
            for pe_ in producers:
                ENG[ce].wait_ge(done[pe_], cnt[pe_])

    def mark(eng):
        ins = last.get(eng)
        assert ins is not None, f"mark: nothing emitted on {eng}"
        ins.then_inc(done[eng], 1)
        cnt[eng] += 1
        last[eng] = None
        return cnt[eng]

    def waiton(consumer, prod, count):
        if count > 0:
            ENG[consumer].wait_ge(done[prod], count)

    def dma(eng, s, out, in_):
        ENG[eng].dma_start(out=out, in_=in_).then_inc(dsem[s][0], 16)
        dsem[s][1] += 16

    def dma_wait(eng, s):
        ENG[eng].wait_ge(dsem[s][0], dsem[s][1])

    def tap_wait():
        for e in ("dve", "act", "pe", "sp"):
            po.wait_ge(done[e], cnt[e])

    def sp_mark():
        sp.sem_inc(done["sp"], 1)
        cnt["sp"] += 1

    def pool_mark():
        po.sem_inc(done["pool"], 1)
        cnt["pool"] += 1

    CONSTS = [0.0, 1.0, 0.125, -float(C_SOFT), 1e-5, -0.5]
    cpool = mk(nc.sbuf_tensor("cpool", [128, len(CONSTS)], F32))
    for ci, cv in enumerate(CONSTS):
        nc.const_aps.register(F32, float(cv), cpool[:, ci:ci + 1])

    ident = mk(nc.sbuf_tensor("ident", [128, 128], BF16))
    p32 = mk(nc.sbuf_tensor("p32", [128, 128], BF16))
    negi = mk(nc.sbuf_tensor("negi", [128, 128], BF16))
    ucur = mk(nc.sbuf_tensor("ucur", [128, 128], BF16))
    useg = mk(nc.sbuf_tensor("useg", [128, SPW], BF16))
    segf = mk(nc.sbuf_tensor("segf", [128, 256], BF16))
    onesr = mk(nc.sbuf_tensor("onesr", [1, 128], BF16))
    onesf = mk(nc.sbuf_tensor("onesf", [1, 128], F32))
    boutr = mk(nc.sbuf_tensor("boutr", [1, D], BF16))
    wpsh = mk(nc.sbuf_tensor("wpsh", [128, 32], BF16))
    lngt = mk(nc.sbuf_tensor("lngt", [128, 4 * DH], BF16))
    iotap = mk(nc.sbuf_tensor("iotap", [128, 1], F32))
    iotaf = mk(nc.sbuf_tensor("iotaf", [128, 128], F32))
    invf = mk(nc.sbuf_tensor("invf", [1, 32], F32))
    invfs = mk(nc.sbuf_tensor("invfs", [1, 32], F32))
    tabsc = mk(nc.sbuf_tensor("tabsc", [128, 128], F32))
    costab = mk(nc.sbuf_tensor("costab", [128, T], BF16))
    sintab = mk(nc.sbuf_tensor("sintab", [128, T], BF16))

    wq_sb = mk(nc.sbuf_tensor("wq_sb", [128, 2, DCH * 128], BF16))
    wo_sb = mk(nc.sbuf_tensor("wo_sb", [128, 2, D], BF16))
    wg_sb = mk(nc.sbuf_tensor("wg_sb", [128, 2, DCH * 128], BF16))
    wu_sb = mk(nc.sbuf_tensor("wu_sb", [128, 2, DCH * 128], BF16))
    sg2 = mk(nc.sbuf_tensor("sg2", [128, 2, 512], BF16))
    hf2 = mk(nc.sbuf_tensor("hf2", [128, 2, 512], BF16))
    wd_sb = mk(nc.sbuf_tensor("wd_sb", [128, 3, D], BF16))

    x_tm = mk(nc.sbuf_tensor("x_tm", [128, TC, D], BF16))
    xfw = mk(nc.sbuf_tensor("xfw", [128, DCH, T], BF16))
    q_rot = mk(nc.sbuf_tensor("q_rot", [128, DCH, T], BF16))
    kvro = mk(nc.sbuf_tensor("kvro", [128, DCH, T], BF16))
    kv_km = mk(nc.sbuf_tensor("kv_km", [128, WPM, H * DH], BF16))
    vh_km = mk(nc.sbuf_tensor("vh_km", [128, WPM + 1, H * DH], BF16))
    vh_f = mk(nc.sbuf_tensor("vh_f", [128, DCH, (WPM + 1) * WIN], BF16))
    gkm = mk(nc.sbuf_tensor("gkm", [128, LMT, H * DH], BF16))
    vg_km = mk(nc.sbuf_tensor("vg_km", [128, LMT, H * DH], BF16))
    vg_f = mk(nc.sbuf_tensor("vg_f", [128, DCH, LMT * 128], BF16))
    pscore = mk(nc.sbuf_tensor("pscore", [H, T], F32))
    pkv = mk(nc.sbuf_tensor("pkv", [H, T], BF16))
    pkv_km = mk(nc.sbuf_tensor("pkv_km", [128, TC, H], BF16))
    pt_loc = mk(nc.sbuf_tensor("pt_loc", [128, WPM + 1, 256], BF16))
    pt_g = mk(nc.sbuf_tensor("pt_g", [128, LMT, T], BF16))
    p_sb = mk(nc.sbuf_tensor("p_sb", [128, 1024], BF16))
    denom = mk(nc.sbuf_tensor("denom", [128, 6], F32))

    x1 = mk(nc.sbuf_tensor("x1", [128, TC, D], F32))
    f32a = mk(nc.sbuf_tensor("f32a", [128, 3072], F32))
    qa = mk(nc.sbuf_tensor("qa", [128, KI], BF16))
    qs = mk(nc.sbuf_tensor("qs", [128, KI], BF16))
    qc1 = mk(nc.sbuf_tensor("qc1", [128, KI], BF16))
    qc2 = mk(nc.sbuf_tensor("qc2", [128, KI], BF16))
    qbm = mk(nc.sbuf_tensor("qbm", [128, KI // 16], F32))
    qsc = mk(nc.sbuf_tensor("qsc", [128, KI // 16], F32))
    qrs = mk(nc.sbuf_tensor("qrs", [128, KI // 16], F32))
    qscb = mk(nc.sbuf_tensor("qscb", [128, KI // 16], BF16))
    qrsb = mk(nc.sbuf_tensor("qrsb", [128, KI // 16], BF16))
    h_tm = mk(nc.sbuf_tensor("h_tm", [128, TC, KI], BF16))
    hq_tm = mk(nc.sbuf_tensor("hq_tm", [128, 1024], BF16))
    hq_f = mk(nc.sbuf_tensor("hq_f", [128, NKB, 256], BF16))
    stats = mk(nc.sbuf_tensor("stats", [128, 8 * 4 * H], F32))
    lnsc = mk(nc.sbuf_tensor("lnsc", [128, 12], F32))
    rowsum = mk(nc.sbuf_tensor("rowsum", [128, 4], F32))

    ps_big = mk(nc.psum_tensor("ps_big", [128, 2048], F32))
    ps_s = mk(nc.psum_tensor("ps_s", [128, 1024], F32))
    ps_t = mk(nc.psum_tensor("ps_t", [128, 1024], BF16))
    ps_p = mk(nc.psum_tensor("ps_p", [128, 512], F32))

    for ci, cv in enumerate(CONSTS):
        ve.memset(cpool[:, ci:ci + 1], float(cv))
    ve.memset(gkm[:, :, :], 0.0)
    ve.memset(vg_km[:, :, :], 0.0)
    ve.memset(pt_g[:, :, :], 0.0)
    ve.memset(onesr[0:1, :], 1.0)
    R("dve", ve.memset(onesf[0:1, :], 1.0))
    fence(["dve"], ["pool", "sp", "act", "pe"])

    po.iota(iotaf[:, :], [[1, 128]], channel_multiplier=0,
            allow_small_or_imprecise_dtypes=True)
    R("pool", po.iota(iotap[:, :], [[0, 1]], channel_multiplier=1,
                      allow_small_or_imprecise_dtypes=True))
    fence(["pool"], ["dve"])

    ipb = bass.AP(iotap, 0, [[1, 128], [0, 128]])
    ve.tensor_tensor(ident[:, :], iotaf[:, :], ipb, aop.is_equal)
    ve.tensor_scalar(negi[:, :], ident[:, :], -30000.0, None, aop.mult)
    ve.tensor_scalar(tabsc[:, 0:1], iotap[:, :], 63.5, 64.0, aop.is_gt, aop.mult)
    ve.tensor_tensor(tabsc[:, 1:2], iotap[:, :], tabsc[:, 0:1], aop.subtract)
    ve.tensor_scalar(tabsc[:, 2:3], tabsc[:, 1:2], 31.5, 64.0, aop.is_gt, aop.mult)
    ve.tensor_tensor(tabsc[:, 3:4], iotap[:, :], tabsc[:, 2:3], aop.subtract)
    ve.tensor_scalar(tabsc[:, 3:4], tabsc[:, 3:4], 32.0, None, aop.add)
    ve.tensor_tensor(p32[:, :], iotaf[:, :],
                     bass.AP(tabsc, 3, [[128, 128], [0, 128]]), aop.is_equal)
    ve.tensor_tensor(ucur[:, :], iotaf[:, :], ipb, aop.is_gt)
    ve.tensor_scalar(tabsc[:, 8:8 + SPW], iotaf[:, 0:SPW], 16.0, 15.0, aop.mult, aop.add)
    ve.tensor_tensor(useg[:, :], tabsc[:, 8:8 + SPW],
                     bass.AP(iotap, 0, [[1, 128], [0, SPW]]), aop.is_ge)
    ve.tensor_scalar(tabsc[:, 4:5], iotap[:, :], 15.5, None, aop.is_gt)
    for k in range(2, 8):
        ve.scalar_tensor_tensor(tabsc[:, 4:5], iotap[:, :], 16.0 * k - 0.5,
                                tabsc[:, 4:5], aop.is_gt, aop.add)
    ve.tensor_scalar(tabsc[:, 5:6], tabsc[:, 4:5], 128.0, None, aop.add)
    fence(["dve"], ["pool"])
    po.iota(f32a[:, 0:256], [[1, 256]], channel_multiplier=0,
            allow_small_or_imprecise_dtypes=True).then_inc(done["pool"], 1)
    cnt["pool"] += 1
    ve.wait_ge(done["pool"], cnt["pool"])
    R("dve", ve.tensor_tensor(segf[:, :], f32a[:, 0:256],
                              bass.AP(tabsc, 5, [[128, 128], [0, 256]]), aop.is_equal))
    fence(["dve"], ["act", "pe", "sp", "pool"])

    po.iota(invf[0:1, :], [[1, 32]], channel_multiplier=0,
            allow_small_or_imprecise_dtypes=True).then_inc(done["pool"], 1)
    cnt["pool"] += 1
    ac.wait_ge(done["pool"], cnt["pool"])
    R("act", ac.activation(invf[0:1, :], invf[0:1, :], AF.Exp,
                           bias=0.0, scale=-math.log(10000.0) / 32.0))
    fence(["act"], ["dve", "pe"])

    dma("sp", "dm", f32a[0:1, 0:D], bout_in[None, :])
    dma("sp", "dm", f32a[0:64, 1024:1025], wproj_in[:, None])
    dma("sp", "dm", f32a[64:128, 1025:1026], wproj_in[:, None])
    dma("sp", "dm", f32a[:, 1026:1282],
        bass.AP(lng_in.tensor, 0, [[0, 128], [1, 4 * DH]]))
    dma_wait("sp", "dm")
    sp_mark()
    ve.wait_ge(done["sp"], cnt["sp"])
    ve.tensor_copy(boutr[0:1, :], f32a[0:1, 0:D])
    R("dve", ve.tensor_copy(lngt[:, :], f32a[:, 1026:1282]))
    ve.tensor_scalar(tabsc[:, 6:7], iotap[:, :], 63.5, 16.0, aop.is_gt, aop.add)
    ve.tensor_tensor(tabsc[:, 7:8], iotaf[:, 0:1], iotaf[:, 0:1], aop.subtract)
    ve.tensor_tensor(qa[:, 0:32], iotaf[:, 0:32],
                     bass.AP(tabsc, 6, [[128, 128], [0, 32]]), aop.is_equal)
    ve.tensor_copy(tabsc[0:64, 7:8], f32a[0:64, 1024:1025])
    ve.tensor_copy(tabsc[64:128, 7:8], f32a[64:128, 1025:1026])
    R("dve", ve.tensor_tensor(wpsh[:, :], qa[:, 0:32],
                              bass.AP(tabsc, 7, [[128, 128], [0, 32]]), aop.mult))
    fence(["dve"], ["pe", "sp"])

    for src, dst, cols in ((wq_in, wqkv_d, 0), (wkv_in, wqkv_d, D), (wout_in, woutb_d, 0)):
        for cc in range(DCH):
            dma("sp", "din", f32a[:, 0:D], src[cc * 128:(cc + 1) * 128, :])
            dma_wait("sp", "din")
            sp_mark()
            ve.wait_ge(done["sp"], cnt["sp"])
            R("dve", ve.tensor_copy(qa[:, 0:D], f32a[:, 0:D]))
            fence(["dve"], ["pool"])
            if dst is wqkv_d:
                dma("pool", "dw", bass.AP(dst.tensor, cc * 128 * 2 * D + cols,
                                          [[2 * D, 128], [1, D]]), qa[:, 0:D])
            else:
                dma("pool", "dw", bass.AP(dst.tensor, cc * 128 * D,
                                          [[D, 128], [1, D]]), qa[:, 0:D])
            dma_wait("pool", "dw")
            pool_mark()
            sp.wait_ge(done["pool"], cnt["pool"])

    for wg0 in range(0, N // WIN, 8):
        for wi in range(8):
            ve.tensor_scalar(f32a[0:1, 1024 + wi * 32:1024 + (wi + 1) * 32],
                             invf[0:1, :], float((wg0 + wi) * WIN), None, aop.mult)
        for wi in range(8):
            ve.tensor_copy(f32a[0:1, 1280 + wi * 32:1280 + (wi + 1) * 32],
                           invf[0:1, :])
        R("dve", ve.tensor_copy(f32a[0:1, 1280:1281], f32a[0:1, 1280:1281]))
        fence(["dve"], ["pe"])
        te.matmul(ps_big[:, 0:256], iotaf[0:1, :], f32a[0:1, 1280:1536],
                  start=True, stop=False)
        R("pe", te.matmul(ps_big[:, 0:256], onesf[0:1, :], f32a[0:1, 1024:1280],
                          start=False, stop=True))
        fence(["pe"], ["dve"])
        ve.tensor_copy(f32a[:, 0:256], ps_big[:, 0:256])
        for j in range(9, -1, -1):
            c = float(2.0 ** j * 2.0 * math.pi)
            ve.tensor_scalar(f32a[:, 256:512], f32a[:, 0:256], c, c,
                             aop.is_ge, aop.mult)
            ve.tensor_tensor(f32a[:, 0:256], f32a[:, 0:256], f32a[:, 256:512],
                             aop.subtract)
        ve.tensor_scalar(f32a[:, 256:512], f32a[:, 0:256], float(math.pi),
                         float(2 * math.pi), aop.is_gt, aop.mult)
        ve.tensor_tensor(f32a[:, 512:768], f32a[:, 0:256], f32a[:, 256:512],
                         aop.subtract)
        ve.tensor_scalar(f32a[:, 256:512], f32a[:, 0:256], float(math.pi / 2),
                         None, aop.add)
        ve.tensor_scalar(f32a[:, 768:1024], f32a[:, 256:512], float(math.pi),
                         float(2 * math.pi), aop.is_gt, aop.mult)
        R("dve", ve.tensor_tensor(f32a[:, 768:1024], f32a[:, 256:512],
                                  f32a[:, 768:1024], aop.subtract))
        fence(["dve"], ["act"])
        ac.activation(f32a[:, 0:256], f32a[:, 512:768], AF.Sin)
        R("act", ac.activation(f32a[:, 256:512], f32a[:, 768:1024], AF.Sin))
        fence(["act"], ["dve"])
        R("dve", ve.tensor_copy(qa[:, 0:512], f32a[:, 0:512]))
        fence(["dve"], ["pe"])
        for tt in range(4):
            R("pe", te.transpose(ps_t[:, tt * 128:(tt + 1) * 128],
                                 qa[:, tt * 128:(tt + 1) * 128], ident[:, :]))
        fence(["pe"], ["dve"])
        with nodrain():
            for wi in range(8):
                blk, rowo = divmod(wi * 32, 128)
                wc = (wg0 + wi) * WIN
                ve.tensor_scalar(sintab[0:32, 0:WIN] if False else qc1[0:32, wc % 512:wc % 512 + WIN], ps_t[0:1, 0:1], 0.0, None, aop.mult) if False else None
                ve.tensor_scalar(qc1[0:32, 0:WIN], ps_t[rowo:rowo + 32, blk * 128:(blk + 1) * 128],
                                 -1.0, None, aop.mult)
                ve.tensor_copy(qc1[32:64, 0:WIN], ps_t[rowo:rowo + 32, blk * 128:(blk + 1) * 128])
                ve.tensor_copy(qc1[64:96, 0:WIN], qc1[0:32, 0:WIN]) if False else None
                ve.tensor_copy(qc2[0:32, 0:WIN], ps_t[rowo:rowo + 32, (2 + blk) * 128:(3 + blk) * 128])
                R("dve", ve.tensor_copy(qc2[32:64, 0:WIN], ps_t[rowo:rowo + 32, (2 + blk) * 128:(3 + blk) * 128]))
                fence(["dve"], ["pool"])
                dma("pool", "dw", bass.AP(sintab_d.tensor, wc, [[N, 64], [1, WIN]]),
                    qc1[0:64, 0:WIN])
                dma("pool", "dw", bass.AP(sintab_d.tensor, 64 * N + wc, [[N, 64], [1, WIN]]),
                    qc1[0:64, 0:WIN])
                dma("pool", "dw", bass.AP(costab_d.tensor, wc, [[N, 64], [1, WIN]]),
                    qc2[0:64, 0:WIN])
                dma("pool", "dw", bass.AP(costab_d.tensor, 64 * N + wc, [[N, 64], [1, WIN]]),
                    qc2[0:64, 0:WIN])
                dma_wait("pool", "dw")
                pool_mark()
                ve.wait_ge(done["pool"], cnt["pool"])

    def stair(K, rsb_t, scb_t, ysrc, out_t, afp32=False, out_flat=None, final3=None):
        if afp32:
            ac.activation(qs[:, 0:K], ysrc, AF.Sign)
            R("act", ac.activation(ysrc, ysrc, AF.Abs))
            fence(["act"], ["dve"])
            a = ysrc
        else:
            ac.activation(qa[:, 0:K], ysrc, AF.Abs)
            R("act", ac.activation(qs[:, 0:K], ysrc, AF.Sign))
            fence(["act"], ["dve"])
            a = qa[:, 0:K]
        ve.tensor_scalar(qc1[:, 0:K], a, 0.25, None, aop.is_gt)
        ve.scalar_tensor_tensor(qc2[:, 0:K], a, 0.75, qc1[:, 0:K], aop.is_gt, aop.add)
        ve.scalar_tensor_tensor(qc1[:, 0:K], a, 1.25, qc2[:, 0:K], aop.is_gt, aop.add)
        ve.scalar_tensor_tensor(qc2[:, 0:K], a, 1.75, qc1[:, 0:K],
                                 aop.is_gt, aop.add)
        ve.tensor_scalar(qc1[:, 0:K], qc2[:, 0:K], 0.5, None, aop.mult)
        ve.scalar_tensor_tensor(qc2[:, 0:K], a, 2.5, qc1[:, 0:K], aop.is_gt, aop.add)
        ve.scalar_tensor_tensor(qc1[:, 0:K], a, 3.5, qc2[:, 0:K], aop.is_gt, aop.add)
        ve.tensor_scalar(qc2[:, 0:K], a, 5.0, 2.0, aop.is_gt, aop.mult)
        ve.tensor_tensor(qc1[:, 0:K], qc1[:, 0:K], qc2[:, 0:K], aop.add)
        ve.tensor_tensor(qc2[:, 0:K], qc1[:, 0:K], qs[:, 0:K], aop.mult)
        if final3 is not None:
            dst, srcv, scb = final3
            return R("dve", ve.tensor_tensor(dst, srcv, scb, aop.mult))
        scb = bass.AP(scb_t, 0, [[KI // 16, 128], [1, K // 16], [0, 16]])
        dst = out_flat if out_flat is not None else out_t.rearrange(
            "p (a b) -> p a b", b=16)
        return R("dve", ve.tensor_tensor(
            dst, qc2[:, 0:K].rearrange("p (a b) -> p a b", b=16),
            scb, aop.mult))

    def wquant(src_dram, dst_dram, nrows, kcols):
        nrt = nrows // 128
        kct = kcols // 128
        dst_rowlen = nrows
        for nr in range(nrt):
            sp.wait_ge(done["act"], cnt["act"])
            sp.wait_ge(done["dve"], cnt["dve"])
            dma("sp", "din", f32a[:, 0:kcols], src_dram[nr * 128:(nr + 1) * 128, :])
            dma_wait("sp", "din")
            sp_mark()
            ve.wait_ge(done["sp"], cnt["sp"])
            ve.wait_ge(done["pool"], cnt["pool"])
            nb = kcols // 16
            ve.tensor_reduce(qbm[:, 0:nb],
                             f32a[:, 0:kcols].rearrange("p (a b) -> p a b", b=16),
                             AX.X, aop.max, apply_absolute_value=True)
            ve.tensor_scalar(qsc[:, 0:nb], qbm[:, 0:nb], 1.0 / 6.0, 1e-12,
                             aop.mult, aop.max)
            ve.reciprocal(qrs[:, 0:nb], qsc[:, 0:nb])
            ve.tensor_copy(qscb[:, 0:nb], qsc[:, 0:nb])
            rsb = bass.AP(qrs, 0, [[KI // 16, 128], [1, nb], [0, 16]])
            R("dve", ve.tensor_tensor(
                f32a[:, 0:kcols].rearrange("p (a b) -> p a b", b=16),
                f32a[:, 0:kcols].rearrange("p (a b) -> p a b", b=16),
                rsb, aop.mult))
            fence(["dve"], ["act"])
            stair(kcols, qrs, qscb, f32a[:, 0:kcols], qc2[:, 0:kcols], afp32=True)
            fence(["dve"], ["pe"])
            for k0 in range(0, kct, 8):
                kn = min(8, kct - k0)
                for kk in range(kn):
                    R("pe", te.transpose(ps_t[:, kk * 128:(kk + 1) * 128],
                                         qc2[:, (k0 + kk) * 128:(k0 + kk + 1) * 128],
                                         ident[:, :]))
                fence(["pe"], ["dve"])
                R("dve", ve.tensor_copy(qa[:, 0:kn * 128],
                                        bass.AP(ps_t, 0, [[1024, 128], [1, kn * 128]])))
                fence(["dve"], ["pool", "pe"])
                dst = bass.AP(dst_dram.tensor, (k0 * 128) * dst_rowlen + nr * 128,
                              [[dst_rowlen, 128], [128 * dst_rowlen, kn], [1, 128]])
                dma("pool", "dw", dst,
                    bass.AP(qa, 0, [[KI, 128], [128, kn], [1, 128]]))
                dma_wait("pool", "dw")
                pool_mark()
                ve.wait_ge(done["pool"], cnt["pool"])
            last["pe"] = None

    wquant(wg_in, wgq_d, KI, D)
    wquant(wu_in, wuq_d, KI, D)
    wquant(wd_in, wdq_d, D, KI)
    po.sem_inc(done["pool"], 1); cnt["pool"] += 1
    for e in ("pe", "act", "dve", "sp"):
        ENG[e].wait_ge(done["pool"], cnt["pool"])

    for bb in range(B_CORE):
        for mt in range(MPB):
            t0 = mt * T
            w0 = mt * WPM
            sg0 = w0 * SPW
            nsg = WPM * SPW
            lt0, lr0 = divmod(sg0, 128)

            sp.wait_ge(done["dve"], cnt["dve"])
            sp.wait_ge(done["pe"], cnt["pe"])
            dma("sp", "dm", costab[:, :], costab_d[:, t0:t0 + T])
            dma("sp", "dm", sintab[:, :], sintab_d[:, t0:t0 + T])
            dma_wait("sp", "dm")
            sp_mark()
            for e in ("dve", "pe", "act"):
                ENG[e].wait_ge(done["sp"], cnt["sp"])
            if "cost" in tap_d:
                tap_wait()
                dma("pool", "dmp", tap_d["cost"][bb, mt], costab[:, :])

            po.wait_ge(done["dve"], cnt["dve"])
            po.wait_ge(done["act"], cnt["act"])
            xv = []
            for tcc in range(TC):
                dma("pool", "dinp", x_tm[:, tcc, :],
                    x_in[bb, t0 + tcc * 128:t0 + (tcc + 1) * 128, :])
                xv.append(dsem["dinp"][1])
            for tcc in range(TC):
                te.wait_ge(dsem["dinp"][0], xv[tcc])
                if tcc == 0:
                    te.wait_ge(done["dve"], cnt["dve"])
                for cc in range(DCH):
                    R("pe", te.transpose(ps_t[:, cc * 128:(cc + 1) * 128],
                                         x_tm[:, tcc, cc * 128:(cc + 1) * 128],
                                         ident[:, :]))
                fence(["pe"], ["dve"])
                R("dve", ve.tensor_copy(
                    bass.AP(xfw, tcc * 128, [[DCH * T, 128], [T, DCH], [1, 128]]),
                    bass.AP(ps_t, 0, [[1024, 128], [128, DCH], [1, 128]])))
                fence(["dve"], ["pe"])

            if "xtm" in tap_d:
                tap_wait()
                dma("pool", "dmp", tap_d["xtm"][bb, mt], x_tm[:, :, :])
            if "xf" in tap_d:
                tap_wait()
                dma("pool", "dmp", tap_d["xf"][bb, mt], xfw[:, :, :])
            ctile = bass.AP(costab, 0, [[T, 128], [0, 2], [1, T]])
            stile = bass.AP(sintab, 0, [[T, 128], [0, 2], [1, T]])
            qk_ev = [0] * (2 * DCH)
            rot_dve = {}
            wq_guard = {}
            for fc in range(2 * DCH):
                db = fc % 2
                if db in wq_guard:
                    waiton("pool", "pe", wq_guard[db])
                dma("pool", "dw", wq_sb[:, db, :],
                    bass.AP(wqkv_d.tensor, fc * 128,
                            [[2 * D, 128], [128 * 2 * D, DCH], [1, 128]]))
                te.wait_ge(dsem["dw"][0], dsem["dw"][1])
                if fc >= 2:
                    waiton("pe", "act", qk_ev[fc - 2])
                for cc in range(DCH):
                    R("pe", te.matmul(ps_big[:, db * 512:db * 512 + T],
                                      wq_sb[:, db, cc * 128:(cc + 1) * 128],
                                      xfw[:, cc, :], start=(cc == 0), stop=(cc == DCH - 1)))
                qmm = mark("pe")
                wq_guard[db] = qmm
                waiton("act", "pe", qmm)
                with nodrain():
                    if fc < DCH:
                        R("act", ac.activation(q_rot[:, fc, :], ps_big[:, db * 512:db * 512 + T],
                                               AF.Copy, bias=0.0, scale=0.125))
                    else:
                        R("act", ac.activation(kvro[:, fc - DCH, :],
                                               ps_big[:, db * 512:db * 512 + T],
                                               AF.Copy, bias=0.0, scale=1.0))
                qk_ev[fc] = mark("act")
                if fc % 2 == 1:
                    rp = fc // 2
                    buf = q_rot if fc < DCH else kvro
                    cc0 = (fc if fc < DCH else fc - DCH) - 1
                    rps = ps_big[:, 1024:2048] if rp % 2 == 0 else ps_s[:, 0:1024]
                    waiton("pe", "act", qk_ev[fc])
                    if rp >= 2:
                        waiton("pe", "dve", rot_dve[rp - 2])
                    te.matmul(rps[:, 0:T], p32[:, :], buf[:, cc0, :],
                              start=True, stop=True)
                    R("pe", te.matmul(rps[:, T:2 * T], p32[:, :], buf[:, cc0 + 1, :],
                                      start=True, stop=True))
                    rmm = mark("pe")
                    waiton("dve", "pe", rmm)
                    bb2 = buf[:, cc0:cc0 + 2, :]
                    ve.tensor_tensor(qa[:, 0:2 * T].rearrange("p (a t) -> p a t", t=T),
                                     bb2, ctile, aop.mult)
                    ve.tensor_tensor(qa[:, 2 * T:4 * T].rearrange("p (a t) -> p a t", t=T),
                                     rps[:, 0:2 * T].rearrange("p (a t) -> p a t", t=T),
                                     stile, aop.mult)
                    R("dve", ve.tensor_tensor(
                        bb2, qa[:, 0:2 * T].rearrange("p (a t) -> p a t", t=T),
                        qa[:, 2 * T:4 * T].rearrange("p (a t) -> p a t", t=T),
                        aop.add))
                    rot_dve[rp] = mark("dve")
            waiton("pe", "dve", rot_dve[DCH - 1])
            waiton("act", "dve", rot_dve[DCH - 1])

            if "qpre" in tap_d:
                tap_wait()
                dma("pool", "dmp", tap_d["qpre"][bb, mt], q_rot[:, :, :])
                dma_wait("pool", "dmp")
                pool_mark()
                for e in ("pe", "dve"):
                    ENG[e].wait_ge(done["pool"], cnt["pool"])
            if "qrot" in tap_d:
                tap_wait()
                dma("pool", "dmp", tap_d["qrot"][bb, mt], q_rot[:, :, :])

            for cc in range(DCH):
                R("pe", te.matmul(ps_p[0:H, 0:T], wpsh[:, 16 - 2 * cc:32 - 2 * cc],
                                  kvro[:, cc, :], start=(cc == 0),
                                  stop=(cc == DCH - 1)))
            fence(["pe"], ["act"])
            R("act", ac.activation(pscore[:, :], ps_p[0:H, 0:T], AF.Copy))
            fence(["act"], ["pe", "act"])
            if "pscore" in tap_d:
                tap_wait()
                dma("pool", "dmp", tap_d["pscore"][bb, mt], pscore[:, :])
                dma_wait("pool", "dmp")
                pool_mark()
                ac.wait_ge(done["pool"], cnt["pool"])
            R("act", ac.activation(pscore[:, :], pscore[:, :], AF.Exp))
            fence(["act"], ["dve"])
            ve.tensor_reduce(qbm[0:H, 0:T // 16],
                             pscore[:, :].rearrange("p (a b) -> p a b", b=16),
                             AX.X, aop.add)
            ve.reciprocal(qrs[0:H, 0:T // 16], qbm[0:H, 0:T // 16])
            rsb = bass.AP(qrs, 0, [[KI // 16, H], [1, T // 16], [0, 16]])
            R("dve", ve.tensor_tensor(
                pkv[:, :].rearrange("p (a b) -> p a b", b=16),
                pscore[:, :].rearrange("p (a b) -> p a b", b=16), rsb, aop.mult))
            fence(["dve"], ["pe"])
            for tcc in range(TC):
                R("pe", te.transpose(ps_t[:, tcc * H:(tcc + 1) * H],
                                     pkv[:, tcc * 128:(tcc + 1) * 128],
                                     ident[0:H, 0:H]))
            fence(["pe"], ["dve"])
            R("dve", ve.tensor_copy(
                bass.AP(pkv_km, 0, [[TC * H, 128], [H, TC], [1, H]]),
                bass.AP(ps_t, 0, [[1024, 128], [H, TC], [1, H]])))
            fence(["dve"], ["pe"])

            for w in range(WPM):
                te.wait_ge(done["dve"], cnt["dve"])
                for cc in range(DCH):
                    R("pe", te.transpose(ps_t[:, cc * 128:(cc + 1) * 128],
                                         kvro[:, cc, w * 128:(w + 1) * 128],
                                         ident[:, :]))
                fence(["pe"], ["dve"])
                R("dve", ve.tensor_copy(kv_km[:, w, :],
                                        bass.AP(ps_t, 0, [[1024, 128], [1, H * DH]])))
                fence(["dve"], ["pe"])
            if "kvkm" in tap_d:
                tap_wait()
                dma("pool", "dmp", tap_d["kvkm"][bb, mt], kv_km[:, :, :])

            for w in range(WPM):
                pk = bass.AP(pkv_km, w * H, [[TC * H, 128], [1, H], [0, DH]])
                R("dve", ve.tensor_tensor(
                    bass.AP(xfw, w * H * DH, [[DCH * T, 128], [DH, H], [1, DH]]),
                    kv_km[:, w, :].rearrange("p (h d) -> p h d", d=DH),
                    pk, aop.mult))
            fence(["dve"], ["pe"])
            for w in range(WPM):
                lr = w * SPW
                st = segf[:, 128 - lr:256 - lr]
                te.matmul(ps_p[:, 0:512], st,
                          bass.AP(xfw, w * H * DH, [[DCH * T, 128], [1, 512]]),
                          start=(w == 0), stop=(w == WPM - 1))
                R("pe", te.matmul(ps_big[:, 1536:2048], st,
                                  bass.AP(xfw, w * H * DH + 512,
                                          [[DCH * T, 128], [1, 512]]),
                                  start=(w == 0), stop=(w == WPM - 1)))
            fence(["pe"], ["act"])
            with nodrain():
                ac.activation(gkm[lr0:lr0 + nsg, lt0, 0:512], ps_p[0:nsg, 0:512],
                              AF.Copy)
                R("act", ac.activation(gkm[lr0:lr0 + nsg, lt0, 512:1024],
                                       ps_big[0:nsg, 1536:2048], AF.Copy))
            fence(["act"], ["pe", "dve"])
            if "gkm" in tap_d:
                tap_wait()
                dma("pool", "dmp", tap_d["gkm"][bb, mt], gkm[:, :, :])

            def ln64(src_ap, dst_ap, r0, nr, goff):
                R("dve", ve.tensor_reduce(
                    stats[r0:r0 + nr, 0:H],
                    src_ap.rearrange("p (h d) -> p h d", d=DH), AX.X, aop.add))
                fence(["dve"], ["act"])
                R("act", ac.activation(f32a[r0:r0 + nr, 0:H * DH], src_ap, AF.Square))
                fence(["act"], ["dve"])
                ve.tensor_reduce(stats[r0:r0 + nr, H:2 * H],
                                 f32a[r0:r0 + nr, 0:H * DH].rearrange(
                                     "p (h d) -> p h d", d=DH), AX.X, aop.add)
                ve.tensor_scalar(stats[r0:r0 + nr, 2 * H:3 * H],
                                 stats[r0:r0 + nr, 0:H], 1.0 / DH, None, aop.mult)
                ve.tensor_scalar(stats[r0:r0 + nr, 3 * H:4 * H],
                                 stats[r0:r0 + nr, H:2 * H], 1.0 / DH, None, aop.mult)
                ve.tensor_tensor(stats[r0:r0 + nr, 4 * H:5 * H],
                                 stats[r0:r0 + nr, 2 * H:3 * H],
                                 stats[r0:r0 + nr, 2 * H:3 * H], aop.mult)
                R("dve", ve.tensor_tensor(stats[r0:r0 + nr, 3 * H:4 * H],
                                          stats[r0:r0 + nr, 3 * H:4 * H],
                                          stats[r0:r0 + nr, 4 * H:5 * H], aop.subtract))
                fence(["dve"], ["act"])
                R("act", ac.activation(stats[r0:r0 + nr, 4 * H:5 * H],
                                       stats[r0:r0 + nr, 3 * H:4 * H], AF.Ln,
                                       bias=1e-5, scale=1.0))
                R("act", ac.activation(stats[r0:r0 + nr, 5 * H:6 * H],
                                       stats[r0:r0 + nr, 4 * H:5 * H], AF.Exp,
                                       bias=0.0, scale=-0.5))
                fence(["act"], ["dve"])
                srow = 8 * 4 * H
                mu_b = bass.AP(stats, r0 * srow + 2 * H, [[srow, nr], [1, H], [0, DH]])
                rs_b = bass.AP(stats, r0 * srow + 5 * H, [[srow, nr], [1, H], [0, DH]])
                g_b = bass.AP(lngt, r0 * 4 * DH + goff, [[4 * DH, nr], [0, H], [1, DH]])
                b_b = bass.AP(lngt, r0 * 4 * DH + goff + DH, [[4 * DH, nr], [0, H], [1, DH]])
                sq = f32a[r0:r0 + nr, 0:H * DH].rearrange("p (h d) -> p h d", d=DH)
                ve.tensor_tensor(sq, src_ap.rearrange("p (h d) -> p h d", d=DH),
                                 mu_b, aop.subtract)
                ve.tensor_tensor(sq, sq, rs_b, aop.mult)
                ve.tensor_tensor(sq, sq, g_b, aop.mult)
                R("dve", ve.tensor_tensor(dst_ap.rearrange("p (h d) -> p h d", d=DH),
                                          sq, b_b, aop.add))

            WH = WPM * H
            W2 = WPM // 2
            GH = W2 * H
            kvv = kv_km[:, :, :].rearrange("p w (h d) -> p (w h) d", d=DH)
            R("dve", ve.tensor_reduce(stats[:, 0:WH], kvv, AX.X, aop.add))
            fence(["dve"], ["act"])
            for hw in range(2):
                R("act", ac.activation(
                    f32a[:, 0:GH * DH],
                    kv_km[:, hw * W2:(hw + 1) * W2, :].rearrange(
                        "p w hd -> p (w hd)"), AF.Square))
                fence(["act"], ["dve"])
                R("dve", ve.tensor_reduce(
                    stats[:, WH + hw * GH:WH + (hw + 1) * GH],
                    f32a[:, 0:GH * DH].rearrange("p (g d) -> p g d", d=DH),
                    AX.X, aop.add))
                if hw == 0:
                    fence(["dve"], ["act"])
            ve.tensor_scalar(stats[:, 2 * WH:3 * WH], stats[:, 0:WH],
                             1.0 / DH, None, aop.mult)
            ve.tensor_scalar(stats[:, 3 * WH:4 * WH], stats[:, WH:2 * WH],
                             1.0 / DH, None, aop.mult)
            ve.tensor_tensor(stats[:, 4 * WH:5 * WH], stats[:, 2 * WH:3 * WH],
                             stats[:, 2 * WH:3 * WH], aop.mult)
            R("dve", ve.tensor_tensor(stats[:, 3 * WH:4 * WH],
                                      stats[:, 3 * WH:4 * WH],
                                      stats[:, 4 * WH:5 * WH], aop.subtract))
            fence(["dve"], ["act"])
            ac.activation(stats[:, 4 * WH:5 * WH], stats[:, 3 * WH:4 * WH],
                          AF.Ln, bias=1e-5, scale=1.0)
            R("act", ac.activation(stats[:, 5 * WH:6 * WH], stats[:, 4 * WH:5 * WH],
                                   AF.Exp, bias=0.0, scale=-0.5))
            fence(["act"], ["dve"])
            for hw in range(2):
                kvh = kv_km[:, hw * W2:(hw + 1) * W2, :].rearrange(
                    "p w (h d) -> p (w h) d", d=DH)
                mu_b = bass.AP(stats, 2 * WH + hw * GH,
                               [[8 * 4 * H, 128], [1, GH], [0, DH]])
                rs_b = bass.AP(stats, 5 * WH + hw * GH,
                               [[8 * 4 * H, 128], [1, GH], [0, DH]])
                g_b = bass.AP(lngt, 0, [[4 * DH, 128], [0, GH], [1, DH]])
                b_b = bass.AP(lngt, DH, [[4 * DH, 128], [0, GH], [1, DH]])
                sqv = f32a[:, 0:GH * DH].rearrange("p (g d) -> p g d", d=DH)
                ve.tensor_tensor(sqv, kvh, mu_b, aop.subtract)
                ve.tensor_tensor(sqv, sqv, rs_b, aop.mult)
                ve.tensor_tensor(sqv, sqv, g_b, aop.mult)
                R("dve", ve.tensor_tensor(
                    vh_km[:, 1 + hw * W2:1 + (hw + 1) * W2, :].rearrange(
                        "p w (h d) -> p (w h) d", d=DH),
                    sqv, b_b, aop.add))
            fence(["dve"], ["pe", "act"])
            if "vhkm" in tap_d:
                tap_wait()
                dma("pool", "dmp", tap_d["vhkm"][bb, mt], vh_km[:, 1:WPM + 1, :])
            ln64(gkm[lr0:lr0 + nsg, lt0, :], vg_km[lr0:lr0 + nsg, lt0, :],
                 lr0, nsg, 2 * DH)
            fence(["dve"], ["pe", "act"])
            if "vgkm" in tap_d:
                tap_wait()
                dma("pool", "dmp", tap_d["vgkm"][bb, mt], vg_km[:, :, :])

            for w in range(WPM):
                for cc in range(DCH):
                    R("pe", te.transpose(ps_t[:, cc * 128:(cc + 1) * 128],
                                         vh_km[:, w + 1, cc * 128:(cc + 1) * 128],
                                         ident[:, :]))
                fence(["pe"], ["dve"])
                R("dve", ve.tensor_copy(
                    bass.AP(vh_f, (w + 1) * WIN,
                            [[DCH * (WPM + 1) * WIN, 128], [(WPM + 1) * WIN, DCH],
                             [1, WIN]]),
                    bass.AP(ps_t, 0, [[1024, 128], [128, DCH], [1, 128]])))
                fence(["dve"], ["pe"])
            for cc in range(DCH):
                R("pe", te.transpose(ps_t[:, cc * 128:(cc + 1) * 128],
                                     vg_km[:, lt0, cc * 128:(cc + 1) * 128],
                                     ident[:, :]))
            fence(["pe"], ["dve"])
            R("dve", ve.tensor_copy(
                bass.AP(vg_f, lt0 * 128 + lr0,
                        [[DCH * LMT * 128, 128], [LMT * 128, DCH], [1, nsg]]),
                bass.AP(ps_t, lr0, [[1024, 128], [128, DCH], [1, nsg]])))
            fence(["dve"], ["pe"])

            av_pe = [0] * DCH
            av_ev = [0] * DCH
            NW = 2 * DCH * WPM
            pe_sg = [0] * NW
            act_eg = [0] * NW
            dve_ng = [0] * NW
            pe_tg = [0] * NW
            dve_vg = [0] * NW

            def geom(w):
                wg = w0 + w
                has_prev = wg > 0
                loc0 = 128 if has_prev else 0
                gn_old = wg * SPW
                gn = gn_old + SPW
                return wg, has_prev, loc0, gn_old, gn, loc0 + 128

            for g in range(NW + 1):
                if g < NW:
                    hidx = g // WPM
                    w = g % WPM
                    cc = hidx // 2
                    ro = (hidx % 2) * 64
                    wg, has_prev, loc0, gn_old, gn, off2 = geom(w)
                    par = g % 2
                    pso = par * 512
                    qs_ap = q_rot[ro:ro + 64, cc, w * 128:(w + 1) * 128]
                    if g >= 2:
                        waiton("pe", "act", act_eg[g - 2])
                    kst = (w if has_prev else w + 1) * WIN
                    te.matmul(ps_s[:, pso:pso + loc0 + 128], qs_ap,
                              vh_f[ro:ro + 64, cc, kst:kst + loc0 + 128],
                              start=True, stop=False)
                    te.matmul(ps_s[:, pso + off2:pso + off2 + gn], qs_ap,
                              vg_f[ro:ro + 64, cc, 0:gn], start=False,
                              stop=False)
                    te.matmul(ps_s[:, pso + loc0:pso + loc0 + 128],
                              negi[:, :], ucur[:, :], start=False, stop=False)
                    R("pe", te.matmul(
                        ps_s[:, pso + off2 + gn_old:pso + off2 + gn],
                        negi[:, :], useg[:, :], start=False, stop=True))
                    pe_sg[g] = mark("pe")
                if g >= 1:
                    gq = g - 1
                    hidx = gq // WPM
                    w = gq % WPM
                    cc = hidx // 2
                    hp = hidx % 2
                    hh = hidx
                    ro = hp * 64
                    wg, has_prev, loc0, gn_old, gn, off2 = geom(w)
                    par = gq % 2
                    pso = par * 512
                    ncols = off2 + gn
                    waiton("act", "pe", pe_sg[gq])
                    if gq >= 2:
                        waiton("act", "pe", pe_tg[gq - 2])
                    with nodrain():
                        R("act", ac.activation(
                            p_sb[:, pso:pso + ncols], ps_s[:, pso:pso + ncols],
                            AF.Exp, bias=-C_SOFT, scale=1.0,
                            accum_out=denom[:, par:par + 1]))
                    act_eg[gq] = mark("act")
                    waiton("dve", "act", act_eg[gq])
                    ve.reciprocal(denom[:, 2 + par:3 + par],
                                  denom[:, par:par + 1])
                    with nodrain():
                        R("dve", ve.tensor_scalar(
                            p_sb[:, pso:pso + ncols], p_sb[:, pso:pso + ncols],
                            denom[:, 2 + par:3 + par], None, aop.mult))
                    dve_ng[gq] = mark("dve")
                    waiton("pe", "dve", dve_ng[gq])
                    if gq >= 1:
                        waiton("pe", "dve", dve_vg[gq - 1])
                    nt = 0
                    if has_prev:
                        te.transpose(ps_t[:, 0:128],
                                     p_sb[:, pso:pso + 128], ident[:, :])
                        nt += 1
                    te.transpose(ps_t[:, nt * 128:nt * 128 + 128],
                                 p_sb[:, pso + loc0:pso + loc0 + 128],
                                 ident[:, :])
                    nt += 1
                    gdone = 0
                    while gdone < gn:
                        take = min(128, gn - gdone)
                        R("pe", te.transpose(
                            ps_t[0:take, nt * 128:nt * 128 + 128],
                            p_sb[:, pso + off2 + gdone:pso + off2 + gdone + take],
                            ident[:, :]))
                        nt += 1
                        gdone += take
                    pe_tg[gq] = mark("pe")
                    waiton("dve", "pe", pe_tg[gq])
                    ve.memset(pt_g[:, :, w * 128:(w + 1) * 128], 0.0)
                    ve.drain()
                    with nodrain():
                        if has_prev:
                            ve.tensor_copy(
                                bass.AP(pt_loc, w * 256 + 128,
                                        [[(WPM + 1) * 256, 128], [1, 256]]),
                                ps_t[:, 0:256])
                            nt = 2
                        else:
                            ve.tensor_copy(pt_loc[:, w + 1, 0:128],
                                           ps_t[:, 0:128])
                            nt = 1
                        gdone = 0
                        while gdone < gn:
                            take = min(128, gn - gdone)
                            ve.tensor_copy(pt_g[0:take, gdone // 128,
                                                w * 128:(w + 1) * 128],
                                           ps_t[0:take, nt * 128:nt * 128 + 128])
                            nt += 1
                            gdone += take
                        R("dve", ve.tensor_copy(denom[:, 4 + par:5 + par],
                                                denom[:, 2 + par:3 + par]))
                    dve_vg[gq] = mark("dve")
                    if w == WPM - 1:
                        avo = (cc % 2) * 1024
                        waiton("pe", "dve", dve_vg[gq])
                        if cc >= 2 and hp == 0:
                            waiton("pe", "dve", av_ev[cc - 2])
                        for aw in range(WPM):
                            awg = w0 + aw
                            te.matmul(ps_big[ro:ro + 64, avo + aw * 128:avo + (aw + 1) * 128],
                                      vh_km[:, aw + 1, hh * DH:(hh + 1) * DH],
                                      pt_loc[:, aw + 1, 0:128], start=(aw == 0), stop=False)
                            if awg > 0:
                                te.matmul(ps_big[ro:ro + 64, avo + aw * 128:avo + (aw + 1) * 128],
                                          vh_km[:, aw, hh * DH:(hh + 1) * DH],
                                          pt_loc[:, aw, 128:256], start=False, stop=False)
                        ngl = (w0 + WPM) * SPW
                        gdone = 0
                        while gdone < ngl:
                            take = min(128, ngl - gdone)
                            R("pe", te.matmul(ps_big[ro:ro + 64, avo:avo + T],
                                              vg_km[0:take, gdone // 128,
                                                    hh * DH:(hh + 1) * DH],
                                              pt_g[0:take, gdone // 128, 0:T],
                                              start=False, stop=(gdone + take >= ngl)))
                            gdone += take
                        av_pe[cc] = mark("pe")
                        if hp == 1:
                            waiton("dve", "pe", av_pe[cc])
                            with nodrain():
                                R("dve", ve.tensor_copy(kvro[:, cc, :],
                                                        ps_big[:, avo:avo + T]))
                            av_ev[cc] = mark("dve")
            waiton("pe", "dve", av_ev[DCH - 1])
            waiton("act", "dve", av_ev[DCH - 1])
            if "attnf" in tap_d:
                tap_wait()
                dma("pool", "dmp", tap_d["attnf"][bb, mt], kvro[:, :, :])

            wo_guard = {}
            stt_m = [0] * TC
            xq_dve = [0] * TC
            for tcc in range(TC):
                half = (tcc % 2) * 1024
                sub = tcc % 2
                lo = sub * 6
                for cc in range(DCH):
                    db = cc % 2
                    if db in wo_guard:
                        waiton("pool", "pe", wo_guard[db])
                    dma("pool", "dw", wo_sb[:, db, :],
                        bass.AP(woutb_d.tensor, cc * 128 * D, [[D, 128], [1, D]]))
                    te.wait_ge(dsem["dw"][0], dsem["dw"][1])
                    if cc == 0:
                        if tcc == 0:
                            te.wait_ge(done["dve"], cnt["dve"])
                            te.wait_ge(done["act"], cnt["act"])
                        elif tcc >= 2:
                            waiton("pe", "dve", stt_m[tcc - 2])
                    for nh in range(2):
                        R("pe", te.matmul(
                            ps_big[:, half + nh * 512:half + (nh + 1) * 512],
                            kvro[:, cc, tcc * 128:(tcc + 1) * 128],
                            wo_sb[:, db, nh * 512:(nh + 1) * 512],
                            start=(cc == 0), stop=False))
                    last["pe"].then_inc(done["pe"], 1)
                    cnt["pe"] += 1
                    last["pe"] = None
                    wo_guard[db] = cnt["pe"]
                te.matmul(ps_big[:, half:half + 512], onesr[0:1, :],
                          boutr[0:1, 0:512], start=False, stop=True)
                R("pe", te.matmul(ps_big[:, half + 512:half + 1024],
                                  onesr[0:1, :], boutr[0:1, 512:1024],
                                  start=False, stop=True))
                wmm = mark("pe")
                waiton("dve", "pe", wmm)
                R("dve", ve.scalar_tensor_tensor(
                    x1[:, tcc, :], ps_big[:, half:half + D], 0.0,
                    x_tm[:, tcc, :], aop.add, aop.add,
                    accum_out=rowsum[:, sub * 2:sub * 2 + 1]))
                stt_m[tcc] = mark("dve")
                waiton("act", "dve", stt_m[tcc])
                R("act", ac.activation(f32a[:, 2 * D:3 * D], x1[:, tcc, :], AF.Square,
                                       accum_out=rowsum[:, sub * 2 + 1:sub * 2 + 2]))
                fence(["act"], ["dve"])
                ve.tensor_scalar(lnsc[:, lo:lo + 1], rowsum[:, sub * 2:sub * 2 + 1],
                                 1.0 / D, None, aop.mult)
                ve.tensor_scalar(lnsc[:, lo + 1:lo + 2],
                                 rowsum[:, sub * 2 + 1:sub * 2 + 2],
                                 1.0 / D, None, aop.mult)
                ve.tensor_tensor(lnsc[:, lo + 2:lo + 3], lnsc[:, lo:lo + 1],
                                 lnsc[:, lo:lo + 1], aop.mult)
                R("dve", ve.tensor_tensor(lnsc[:, lo + 1:lo + 2], lnsc[:, lo + 1:lo + 2],
                                          lnsc[:, lo + 2:lo + 3], aop.subtract))
                fence(["dve"], ["act"])
                ac.activation(lnsc[:, lo + 3:lo + 4], lnsc[:, lo + 1:lo + 2], AF.Ln,
                              bias=1e-5, scale=1.0)
                R("act", ac.activation(lnsc[:, lo + 4:lo + 5], lnsc[:, lo + 3:lo + 4],
                                       AF.Exp, bias=0.0, scale=-0.5))
                fence(["act"], ["dve"])
                ve.tensor_tensor(lnsc[:, lo + 5:lo + 6], lnsc[:, lo:lo + 1],
                                 lnsc[:, lo + 4:lo + 5], aop.mult)
                R("dve", ve.tensor_scalar(lnsc[:, lo + 5:lo + 6], lnsc[:, lo + 5:lo + 6],
                                          -1.0, None, aop.mult))
                fence(["dve"], ["act"])
                R("act", ac.activation(x1[:, tcc, :], x1[:, tcc, :], AF.Identity,
                                       bias=lnsc[:, lo + 5:lo + 6],
                                       scale=lnsc[:, lo + 4:lo + 5]))
                apm = mark("act")
                yst = sub * D
                nb = D // 16
                waiton("dve", "act", apm)
                ve.tensor_reduce(qbm[:, 0:nb],
                                 x1[:, tcc, :].rearrange("p (a b) -> p a b", b=16),
                                 AX.X, aop.max, apply_absolute_value=True)
                ve.tensor_scalar(qsc[:, 0:nb], qbm[:, 0:nb], 1.0 / 6.0, 1e-12,
                                 aop.mult, aop.max)
                ve.reciprocal(qrs[:, 0:nb], qsc[:, 0:nb])
                rsb = bass.AP(qrs, 0, [[KI // 16, 128], [1, nb], [0, 16]])
                R("dve", ve.tensor_tensor(
                    f32a[:, yst:yst + D].rearrange("p (a b) -> p a b", b=16),
                    x1[:, tcc, :].rearrange("p (a b) -> p a b", b=16),
                    rsb, aop.mult))
                fence(["dve"], ["act"])
                stair(D, qrs, qsc, f32a[:, yst:yst + D], None, afp32=True,
                      out_flat=bass.AP(xfw, tcc * D,
                                       [[DCH * T, 128], [16, D // 16], [1, 16]]))
                xq_dve[tcc] = mark("dve")
            if "x1" in tap_d:
                tap_wait()
                dma("pool", "dmp", tap_d["x1"][bb, mt], x1[:, :, :])

            for tcc in range(TC):
                waiton("pe", "dve", xq_dve[tcc])
                for cc in range(DCH):
                    R("pe", te.transpose(
                        ps_t[:, cc * 128:(cc + 1) * 128],
                        bass.AP(xfw, tcc * D + cc * 128, [[DCH * T, 128], [1, 128]]),
                        ident[:, :]))
                fence(["pe"], ["dve"])
                R("dve", ve.tensor_copy(
                    bass.AP(q_rot, tcc * 128, [[DCH * T, 128], [T, DCH], [1, 128]]),
                    bass.AP(ps_t, 0, [[1024, 128], [128, DCH], [1, 128]])))
                fence(["dve"], ["pe"])
            if "x1q" in tap_d:
                for tcc in range(TC):
                    dma("pool", "dmp", tap_d["x1q"][bb, mt, tcc],
                        bass.AP(q_rot, tcc * 128, [[DCH * T, 128], [T, DCH], [1, 128]]))

            pe_g = [0] * (NKB + 2)
            pe_u = [0] * (NKB + 2)
            act_s = [0] * (NKB + 2)
            dve_m = [0] * (NKB + 2)
            pe_h = [0] * (NKB + 2)
            dve_e = [0] * (NKB + 2)
            pool_w = [0] * (NKB + 2)
            for ncb in range(NKB + 2):
                par = ncb % 2
                if ncb < NKB:
                    if ncb >= 2:
                        waiton("pool", "pe", pe_g[ncb - 2])
                        waiton("pool", "pe", pe_u[ncb - 2])
                    dma("pool", "dw", wg_sb[:, par, :],
                        bass.AP(wgq_d.tensor, ncb * 128,
                                [[KI, 128], [128 * KI, DCH], [1, 128]]))
                    dma("pool", "dw", wu_sb[:, par, :],
                        bass.AP(wuq_d.tensor, ncb * 128,
                                [[KI, 128], [128 * KI, DCH], [1, 128]]))
                    pool_w[ncb] = dsem["dw"][1]
                    te.wait_ge(dsem["dw"][0], pool_w[ncb])
                    if ncb >= 2:
                        waiton("pe", "act", act_s[ncb - 2])
                    for cc in range(DCH):
                        R("pe", te.matmul(ps_big[:, par * 512:par * 512 + 512],
                                          wg_sb[:, par, cc * 128:(cc + 1) * 128],
                                          q_rot[:, cc, :],
                                          start=(cc == 0), stop=(cc == DCH - 1)))
                    pe_g[ncb] = mark("pe")
                    waiton("act", "pe", pe_g[ncb])
                    if ncb >= 2:
                        waiton("act", "dve", dve_m[ncb - 2])
                    with nodrain():
                        R("act", ac.activation(sg2[:, par, :],
                                               ps_big[:, par * 512:par * 512 + 512],
                                               AF.Silu))
                    act_s[ncb] = mark("act")
                    if ncb >= 2:
                        waiton("pe", "dve", dve_m[ncb - 2])
                    for cc in range(DCH):
                        R("pe", te.matmul(ps_big[:, 1024 + par * 512:1024 + par * 512 + 512],
                                          wu_sb[:, par, cc * 128:(cc + 1) * 128],
                                          q_rot[:, cc, :],
                                          start=(cc == 0), stop=(cc == DCH - 1)))
                    pe_u[ncb] = mark("pe")
                    waiton("dve", "act", act_s[ncb])
                    waiton("dve", "pe", pe_u[ncb])
                    if ncb >= 2:
                        waiton("dve", "pe", pe_h[ncb - 2])
                    with nodrain():
                        R("dve", ve.tensor_tensor(hf2[:, par, :],
                                                  ps_big[:, 1024 + par * 512:1024 + par * 512 + 512],
                                                  sg2[:, par, :], aop.mult))
                    dve_m[ncb] = mark("dve")
                if 0 <= ncb - 1 < NKB:
                    nch = ncb - 1
                    parh = nch % 2
                    waiton("pe", "dve", dve_m[nch])
                    if nch >= 2:
                        waiton("pe", "dve", dve_e[nch - 2])
                    pto = parh * 512
                    for tcc in range(TC):
                        R("pe", te.transpose(ps_t[:, pto + tcc * 128:pto + (tcc + 1) * 128],
                                             hf2[:, parh, tcc * 128:(tcc + 1) * 128],
                                             ident[:, :]))
                    pe_h[nch] = mark("pe")
                    waiton("dve", "pe", pe_h[nch])
                    with nodrain():
                        R("dve", ve.tensor_copy(
                            bass.AP(h_tm, nch * 128, [[TC * KI, 128], [KI, TC], [1, 128]]),
                            bass.AP(ps_t, pto, [[1024, 128], [128, TC], [1, 128]])))
                    dve_e[nch] = mark("dve")
                    if QFOLD and nch % 2 == 1:
                        ng = 2
                        c0 = (nch - ng + 1) * 128
                        nbk = ng * 8
                        kw = ng * 128
                        hview = bass.AP(h_tm, c0,
                                        [[TC * KI, 128], [KI, TC], [16, nbk], [1, 16]])
                        R("dve", ve.tensor_reduce(qbm[:, 0:TC * nbk], hview, AX.X,
                                                  aop.max, apply_absolute_value=True))
                        ve.tensor_scalar(qsc[:, 0:TC * nbk], qbm[:, 0:TC * nbk],
                                         1.0 / 6.0, 1e-12, aop.mult, aop.max)
                        R("dve", ve.reciprocal(qrs[:, 0:TC * nbk], qsc[:, 0:TC * nbk]))
                        rsb4 = bass.AP(qrs, 0, [[KI // 16, 128], [nbk, TC], [1, nbk], [0, 16]])
                        R("dve", ve.tensor_tensor(
                            bass.AP(qc1, 0, [[KI, 128], [kw, TC], [16, nbk], [1, 16]]),
                            hview, rsb4, aop.mult))
                        fence(["dve"], ["act"])
                        stair(TC * kw, None, None, qc1[:, 0:TC * kw], None, final3=(
                            bass.AP(h_tm, c0, [[TC * KI, 128], [KI, TC], [16, nbk], [1, 16]]),
                            bass.AP(qc2, 0, [[KI, 128], [kw, TC], [16, nbk], [1, 16]]),
                            bass.AP(qsc, 0, [[KI // 16, 128], [nbk, TC], [1, nbk], [0, 16]])))
            if "htm" in tap_d:
                tap_wait()
                dma("pool", "dmp", tap_d["htm"][bb, mt], h_tm[:, :, :])

            dve_stt = 0
            wd_guard = {}
            for tp in range(TC // 2):
                if tp == 0:
                    if last.get("dve") is not None:
                        fence(["dve"], ["pe"])
                    else:
                        waiton("pe", "dve", cnt["dve"])
                else:
                    waiton("pe", "dve", dve_stt)
                if TPAR:
                    evq = []
                    for sub in range(2):
                        tcc = tp * 2 + sub
                        for k0 in range(0, NKB, 4):
                            kn = min(4, NKB - k0)
                            half = len(evq) % 2
                            if len(evq) >= TDEPTH:
                                waiton("pe", "dve", evq[-TDEPTH])
                            for kk in range(kn):
                                R("pe", te.transpose(
                                    ps_t[:, half * 512 + kk * 128:half * 512 + (kk + 1) * 128],
                                    h_tm[:, tcc, (k0 + kk) * 128:(k0 + kk + 1) * 128],
                                    ident[:, :]))
                            ptb = mark("pe")
                            waiton("dve", "pe", ptb)
                            with nodrain():
                                R("dve", ve.tensor_copy(
                                    bass.AP(hq_f, k0 * 256 + sub * 128,
                                            [[NKB * 256, 128], [256, kn], [1, 128]]),
                                    bass.AP(ps_t, half * 512, [[1024, 128], [128, kn], [1, 128]])))
                            evq.append(mark("dve"))
                    waiton("pe", "dve", evq[-1])
                else:
                    for sub in range(2):
                        tcc = tp * 2 + sub
                        for k0 in range(0, NKB, 8):
                            kn = min(8, NKB - k0)
                            for kk in range(kn):
                                R("pe", te.transpose(ps_t[:, kk * 128:(kk + 1) * 128],
                                                     h_tm[:, tcc, (k0 + kk) * 128:(k0 + kk + 1) * 128],
                                                     ident[:, :]))
                            fence(["pe"], ["dve"])
                            R("dve", ve.tensor_copy(
                                bass.AP(hq_f, k0 * 256 + sub * 128,
                                        [[NKB * 256, 128], [256, kn], [1, 128]]),
                                bass.AP(ps_t, 0, [[1024, 128], [128, kn], [1, 128]])))
                            fence(["dve"], ["pe"])
                for kk in range(NKB):
                    db = kk % 3
                    if db in wd_guard:
                        waiton("pool", "pe", wd_guard[db])
                    dma("pool", "dw", wd_sb[:, db, :],
                        bass.AP(wdq_d.tensor, kk * 128 * D, [[D, 128], [1, D]]))
                    if PREFETCH:
                        te.wait_ge(dsem["dw"][0], dsem["dw"][1])
                    else:
                        dma_wait("pool", "dw")
                        pool_mark()
                        te.wait_ge(done["pool"], cnt["pool"])
                    for sub in range(2):
                        for nh in range(2):
                            R("pe", te.matmul(
                                ps_big[:, sub * 1024 + nh * 512:sub * 1024 + (nh + 1) * 512],
                                hq_f[:, kk, sub * 128:(sub + 1) * 128],
                                wd_sb[:, db, nh * 512:(nh + 1) * 512],
                                start=(kk == 0), stop=(kk == NKB - 1)))
                    last["pe"].then_inc(done["pe"], 1)
                    cnt["pe"] += 1
                    last["pe"] = None
                    wd_guard[db] = cnt["pe"]
                mm_done = cnt["pe"]
                waiton("dve", "pe", mm_done)
                for sub in range(2):
                    tcc = tp * 2 + sub
                    R("dve", ve.scalar_tensor_tensor(
                        x1[:, tcc, :], ps_big[:, sub * 1024:sub * 1024 + D], 0.0,
                        x1[:, tcc, :], aop.add, aop.add,
                        accum_out=rowsum[:, sub * 2:sub * 2 + 1]))
                dve_stt = mark("dve")
                for sub in range(2):
                    tcc = tp * 2 + sub
                    lo = sub * 6
                    if sub == 0:
                        waiton("act", "dve", dve_stt)
                    R("act", ac.activation(f32a[:, sub * D:(sub + 1) * D],
                                           x1[:, tcc, :], AF.Square,
                                           accum_out=rowsum[:, sub * 2 + 1:sub * 2 + 2]))
                    fence(["act"], ["dve"])
                    ve.tensor_scalar(lnsc[:, lo:lo + 1], rowsum[:, sub * 2:sub * 2 + 1],
                                     1.0 / D, None, aop.mult)
                    ve.tensor_scalar(lnsc[:, lo + 1:lo + 2],
                                     rowsum[:, sub * 2 + 1:sub * 2 + 2],
                                     1.0 / D, None, aop.mult)
                    ve.tensor_tensor(lnsc[:, lo + 2:lo + 3], lnsc[:, lo:lo + 1],
                                     lnsc[:, lo:lo + 1], aop.mult)
                    R("dve", ve.tensor_tensor(lnsc[:, lo + 1:lo + 2], lnsc[:, lo + 1:lo + 2],
                                              lnsc[:, lo + 2:lo + 3], aop.subtract))
                    fence(["dve"], ["act"])
                    ac.activation(lnsc[:, lo + 3:lo + 4], lnsc[:, lo + 1:lo + 2], AF.Ln,
                                  bias=1e-5, scale=1.0)
                    R("act", ac.activation(lnsc[:, lo + 4:lo + 5], lnsc[:, lo + 3:lo + 4],
                                           AF.Exp, bias=0.0, scale=-0.5))
                    fence(["act"], ["dve"])
                    ve.tensor_tensor(lnsc[:, lo + 5:lo + 6], lnsc[:, lo:lo + 1],
                                     lnsc[:, lo + 4:lo + 5], aop.mult)
                    R("dve", ve.tensor_scalar(lnsc[:, lo + 5:lo + 6], lnsc[:, lo + 5:lo + 6],
                                              -1.0, None, aop.mult))
                    fence(["dve"], ["act"])
                    R("act", ac.activation(x1[:, tcc, :], x1[:, tcc, :], AF.Identity,
                                           bias=lnsc[:, lo + 5:lo + 6],
                                           scale=lnsc[:, lo + 4:lo + 5]))
                    apm = mark("act")
                    if SPSTORE:
                        waiton("sp", "act", apm)
                        dma("sp", "dout",
                            out_d[bb, t0 + tcc * 128:t0 + (tcc + 1) * 128, :],
                            x1[:, tcc, :])
                    else:
                        waiton("pool", "act", apm)
                        dma("pool", "dmp",
                            out_d[bb, t0 + tcc * 128:t0 + (tcc + 1) * 128, :],
                            x1[:, tcc, :])
            if SPSTORE:
                dma_wait("sp", "dout")
                sp_mark()
                waiton("act", "sp", cnt["sp"])
                waiton("dve", "sp", cnt["sp"])
            else:
                dma_wait("pool", "dmp")
                pool_mark()
                waiton("act", "pool", cnt["pool"])
                waiton("dve", "pool", cnt["pool"])

            ve.wait_ge(done["pe"], cnt["pe"])
            ve.tensor_copy(vh_km[:, 0, :], vh_km[:, WPM, :])
            R("dve", ve.tensor_copy(
                bass.AP(vh_f, 0, [[DCH * (WPM + 1) * WIN, 128],
                                  [(WPM + 1) * WIN, DCH], [1, WIN]]),
                bass.AP(vh_f, WPM * WIN, [[DCH * (WPM + 1) * WIN, 128],
                                          [(WPM + 1) * WIN, DCH], [1, WIN]])))
            fence(["dve"], ["pe", "act", "pool", "sp"])

    if SPSTORE:
        dma_wait("sp", "dout")
    dma_wait("pool", "dmp")
    po.sem_inc(done["pool"], 1)

    for cm in reversed(ctxs):
        cm.__exit__(None, None, None)
    return nc


def kernel(L0_attn_x, L0_attn_Wq, L0_attn_Wkv, L0_attn_Wproj, L0_attn_Wout,
           L0_attn_bout, L0_attn_ln_local_g, L0_attn_ln_local_b,
           L0_attn_ln_global_g, L0_attn_ln_global_b,
           L0_ffn_w_gate, L0_ffn_w_up, L0_ffn_w_down):
    x = np.ascontiguousarray(np.asarray(L0_attn_x, dtype=np.float32))
    lng = np.ascontiguousarray(np.stack(
        [np.asarray(L0_attn_ln_local_g, np.float32),
         np.asarray(L0_attn_ln_local_b, np.float32),
         np.asarray(L0_attn_ln_global_g, np.float32),
         np.asarray(L0_attn_ln_global_b, np.float32)], axis=0))
    common = {
        "wq": np.ascontiguousarray(np.asarray(L0_attn_Wq, np.float32)),
        "wkv": np.ascontiguousarray(np.asarray(L0_attn_Wkv, np.float32)),
        "wproj": np.ascontiguousarray(np.asarray(L0_attn_Wproj, np.float32)[:, 0]),
        "wout": np.ascontiguousarray(np.asarray(L0_attn_Wout, np.float32)),
        "bout": np.ascontiguousarray(np.asarray(L0_attn_bout, np.float32)),
        "lng": lng,
        "wgate": np.ascontiguousarray(np.asarray(L0_ffn_w_gate, np.float32)),
        "wup": np.ascontiguousarray(np.asarray(L0_ffn_w_up, np.float32)),
        "wdown": np.ascontiguousarray(np.asarray(L0_ffn_w_down, np.float32)),
    }
    n_cores = 8
    bpc = x.shape[0] // n_cores
    nc = build_program(B_CORE=bpc, N=x.shape[1])
    in_maps = [dict(common, x=np.ascontiguousarray(x[i * bpc:(i + 1) * bpc]))
               for i in range(n_cores)]
    res = run_bass_kernel_spmd(nc, in_maps, core_ids=list(range(n_cores)))
    return np.concatenate([np.asarray(r["out"]) for r in res.results], axis=0)


# version 26 (final)
# speedup vs baseline: 1.0695x; 1.0041x over previous
import math
import numpy as np

import concourse.bass as bass
import concourse.mybir as mybir
from concourse.alu_op_type import AluOpType as aop
from concourse.bass_utils import run_bass_kernel_spmd

F32 = mybir.dt.float32
BF16 = mybir.dt.bfloat16
AF = mybir.ActivationFunctionType
AX = mybir.AxisListType

D = 1024
H = 16
DH = 64
DCH = D // 128
KI = 2816
NKB = KI // 128
WIN = 128
SEGL = 16
SPW = WIN // SEGL
C_SOFT = 12.0





QFOLD = True
SPSTORE = True
PREFETCH = True
TPAR = False
TDEPTH = 1


def build_program(B_CORE=2, N=4096, T=512, taps=()):
    NW = N // WIN
    WPM = T // WIN
    MPB = NW // WPM
    NSEG = N // SEGL
    LMT = (NSEG + 127) // 128
    TC = T // 128
    taps = set(taps)

    nc = bass.Bass("TRN2", target_bir_lowering=False, detect_race_conditions=False)

    last = {}
    drain_on = [True]

    class nodrain:
        def __enter__(self):
            drain_on.append(False)

        def __exit__(self, *a):
            drain_on.pop()

    class EngProxy:
        _SKIP = {"wait_ge", "sem_inc", "wait_op", "nop", "engine_nop", "drain",
                 "sem_clear"}
        _NODRAIN = {"dma_start", "dma_start_transpose"}

        def __init__(self, eng, name):
            self._eng = eng
            self._name = name

        def __getattr__(self, m):
            f = getattr(self._eng, m)
            if not callable(f) or m in self._SKIP:
                return f
            name = self._name
            eng = self._eng
            drainable = name in ("dve", "act") and m not in self._NODRAIN

            def wrap(*a, **k):
                r = f(*a, **k)
                if hasattr(r, "then_inc"):
                    last[name] = r
                    if drainable and a and isinstance(a[0], bass.AP):
                        out = a[0]
                        try:
                            free = 1
                            for s in out.shape[1:]:
                                free *= s
                            nbytes = free * mybir.dt.size(out.dtype)
                        except Exception:
                            nbytes = 0
                        if nbytes < 2048 and drain_on[-1]:
                            eng.drain()
                return r
            return wrap

    te = EngProxy(nc.tensor, "pe")
    ac = EngProxy(nc.scalar, "act")
    ve = EngProxy(nc.vector, "dve")
    po = EngProxy(nc.gpsimd, "pool")
    sp = EngProxy(nc.sync, "sp")

    x_in = nc.dram_tensor("x", [B_CORE, N, D], F32, kind="ExternalInput").ap()
    wq_in = nc.dram_tensor("wq", [D, D], F32, kind="ExternalInput").ap()
    wkv_in = nc.dram_tensor("wkv", [D, D], F32, kind="ExternalInput").ap()
    wproj_in = nc.dram_tensor("wproj", [DH], F32, kind="ExternalInput").ap()
    wout_in = nc.dram_tensor("wout", [D, D], F32, kind="ExternalInput").ap()
    bout_in = nc.dram_tensor("bout", [D], F32, kind="ExternalInput").ap()
    lng_in = nc.dram_tensor("lng", [4, DH], F32, kind="ExternalInput").ap()
    wg_in = nc.dram_tensor("wgate", [KI, D], F32, kind="ExternalInput").ap()
    wu_in = nc.dram_tensor("wup", [KI, D], F32, kind="ExternalInput").ap()
    wd_in = nc.dram_tensor("wdown", [D, KI], F32, kind="ExternalInput").ap()
    out_d = nc.dram_tensor("out", [B_CORE, N, D], F32, kind="ExternalOutput").ap()
    wqkv_d = nc.dram_tensor("wqkv_bf", [D, 2 * D], BF16).ap()
    woutb_d = nc.dram_tensor("wout_bf", [D, D], BF16).ap()
    wgq_d = nc.dram_tensor("wgq_t", [D, KI], BF16).ap()
    costab_d = nc.dram_tensor("costab_d", [128, N], BF16).ap()
    sintab_d = nc.dram_tensor("sintab_d", [128, N], BF16).ap()
    wuq_d = nc.dram_tensor("wuq_t", [D, KI], BF16).ap()
    wdq_d = nc.dram_tensor("wdq_t", [KI, D], BF16).ap()

    tap_d = {}
    def def_tap(name, shape, dt=F32):
        if name in taps:
            tap_d[name] = nc.dram_tensor("tap_" + name, list(shape), dt,
                                         kind="ExternalOutput").ap()
    def_tap("xtm", [B_CORE, MPB, 128, TC, D], dt=BF16)
    def_tap("xf", [B_CORE, MPB, 128, DCH, T], dt=BF16)
    def_tap("qpre", [B_CORE, MPB, 128, DCH, T], dt=BF16)
    def_tap("qrot", [B_CORE, MPB, 128, DCH, T], dt=BF16)
    def_tap("kvkm", [B_CORE, MPB, 128, WPM, H * DH], dt=BF16)
    def_tap("vhkm", [B_CORE, MPB, 128, WPM, H * DH], dt=BF16)
    def_tap("gkm", [B_CORE, MPB, 128, LMT, H * DH], dt=BF16)
    def_tap("vgkm", [B_CORE, MPB, 128, LMT, H * DH], dt=BF16)
    def_tap("pscore", [B_CORE, MPB, H, T], dt=F32)
    def_tap("attnf", [B_CORE, MPB, 128, DCH, T], dt=BF16)
    def_tap("x1", [B_CORE, MPB, 128, TC, D], dt=F32)
    def_tap("x1q", [B_CORE, MPB, TC, 128, D], dt=BF16)
    def_tap("htm", [B_CORE, MPB, 128, TC, KI], dt=BF16)
    def_tap("cost", [B_CORE, MPB, 128, T], dt=BF16)

    ctxs = []
    def mk(cm):
        h = cm.__enter__()
        ctxs.append(cm)
        return h

    done = {e: mk(nc.semaphore("dn_" + e)) for e in ("pe", "act", "dve", "pool", "sp")}
    cnt = {e: 0 for e in done}
    dsem = {s: [mk(nc.semaphore("dm_" + s)), 0]
            for s in ("din", "dw", "dout", "dm", "dmp", "dinp")}

    def R(eng, instr):
        return instr

    ENG = {"pe": te, "act": ac, "dve": ve, "pool": po, "sp": sp}

    def fence(producers, consumers):
        for pe_ in producers:
            ins = last.get(pe_)
            assert ins is not None, f"fence: nothing emitted on {pe_}"
            ins.then_inc(done[pe_], 1)
            cnt[pe_] += 1
            last[pe_] = None
        for ce in consumers:
            for pe_ in producers:
                ENG[ce].wait_ge(done[pe_], cnt[pe_])

    def mark(eng):
        ins = last.get(eng)
        assert ins is not None, f"mark: nothing emitted on {eng}"
        ins.then_inc(done[eng], 1)
        cnt[eng] += 1
        last[eng] = None
        return cnt[eng]

    def waiton(consumer, prod, count):
        if count > 0:
            ENG[consumer].wait_ge(done[prod], count)

    def dma(eng, s, out, in_):
        ENG[eng].dma_start(out=out, in_=in_).then_inc(dsem[s][0], 16)
        dsem[s][1] += 16

    def dma_wait(eng, s):
        ENG[eng].wait_ge(dsem[s][0], dsem[s][1])

    def tap_wait():
        for e in ("dve", "act", "pe", "sp"):
            po.wait_ge(done[e], cnt[e])

    def sp_mark():
        sp.sem_inc(done["sp"], 1)
        cnt["sp"] += 1

    def pool_mark():
        po.sem_inc(done["pool"], 1)
        cnt["pool"] += 1

    CONSTS = [0.0, 1.0, 0.125, -float(C_SOFT), 1e-5, -0.5]
    cpool = mk(nc.sbuf_tensor("cpool", [128, len(CONSTS)], F32))
    for ci, cv in enumerate(CONSTS):
        nc.const_aps.register(F32, float(cv), cpool[:, ci:ci + 1])

    ident = mk(nc.sbuf_tensor("ident", [128, 128], BF16))
    p32 = mk(nc.sbuf_tensor("p32", [128, 128], BF16))
    negi = mk(nc.sbuf_tensor("negi", [128, 128], BF16))
    ucur = mk(nc.sbuf_tensor("ucur", [128, 128], BF16))
    useg = mk(nc.sbuf_tensor("useg", [128, SPW], BF16))
    segf = mk(nc.sbuf_tensor("segf", [128, 256], BF16))
    onesr = mk(nc.sbuf_tensor("onesr", [1, 128], BF16))
    onesf = mk(nc.sbuf_tensor("onesf", [1, 128], F32))
    boutr = mk(nc.sbuf_tensor("boutr", [1, D], BF16))
    wpsh = mk(nc.sbuf_tensor("wpsh", [128, 32], BF16))
    lngt = mk(nc.sbuf_tensor("lngt", [128, 4 * DH], BF16))
    iotap = mk(nc.sbuf_tensor("iotap", [128, 1], F32))
    iotaf = mk(nc.sbuf_tensor("iotaf", [128, 128], F32))
    invf = mk(nc.sbuf_tensor("invf", [1, 32], F32))
    invfs = mk(nc.sbuf_tensor("invfs", [1, 32], F32))
    tabsc = mk(nc.sbuf_tensor("tabsc", [128, 128], F32))
    costab = mk(nc.sbuf_tensor("costab", [128, T], BF16))
    sintab = mk(nc.sbuf_tensor("sintab", [128, T], BF16))

    wq_sb = mk(nc.sbuf_tensor("wq_sb", [128, 2, DCH * 128], BF16))
    wo_sb = mk(nc.sbuf_tensor("wo_sb", [128, 2, D], BF16))
    wg_sb = mk(nc.sbuf_tensor("wg_sb", [128, 2, DCH * 128], BF16))
    wu_sb = mk(nc.sbuf_tensor("wu_sb", [128, 2, DCH * 128], BF16))
    sg2 = mk(nc.sbuf_tensor("sg2", [128, 2, 512], BF16))
    hf2 = mk(nc.sbuf_tensor("hf2", [128, 2, 512], BF16))
    wd_sb = mk(nc.sbuf_tensor("wd_sb", [128, 3, D], BF16))

    x_tm = mk(nc.sbuf_tensor("x_tm", [128, TC, D], BF16))
    xfw = mk(nc.sbuf_tensor("xfw", [128, DCH, T], BF16))
    q_rot = mk(nc.sbuf_tensor("q_rot", [128, DCH, T], BF16))
    kvro = mk(nc.sbuf_tensor("kvro", [128, DCH, T], BF16))
    kv_km = mk(nc.sbuf_tensor("kv_km", [128, WPM, H * DH], BF16))
    vh_km = mk(nc.sbuf_tensor("vh_km", [128, WPM + 1, H * DH], BF16))
    vh_f = mk(nc.sbuf_tensor("vh_f", [128, DCH, (WPM + 1) * WIN], BF16))
    gkm = mk(nc.sbuf_tensor("gkm", [128, LMT, H * DH], BF16))
    vg_km = mk(nc.sbuf_tensor("vg_km", [128, LMT, H * DH], BF16))
    vg_f = mk(nc.sbuf_tensor("vg_f", [128, DCH, LMT * 128], BF16))
    pscore = mk(nc.sbuf_tensor("pscore", [H, T], F32))
    pkv = mk(nc.sbuf_tensor("pkv", [H, T], BF16))
    pkv_km = mk(nc.sbuf_tensor("pkv_km", [128, TC, H], BF16))
    pt_loc = mk(nc.sbuf_tensor("pt_loc", [128, WPM + 1, 256], BF16))
    pt_g = mk(nc.sbuf_tensor("pt_g", [128, LMT, T], BF16))
    p_sb = mk(nc.sbuf_tensor("p_sb", [128, 1024], BF16))
    denom = mk(nc.sbuf_tensor("denom", [128, 6], F32))

    x1 = mk(nc.sbuf_tensor("x1", [128, TC, D], F32))
    f32a = mk(nc.sbuf_tensor("f32a", [128, 3072], F32))
    qa = mk(nc.sbuf_tensor("qa", [128, KI], BF16))
    qs = mk(nc.sbuf_tensor("qs", [128, KI], BF16))
    qc1 = mk(nc.sbuf_tensor("qc1", [128, KI], BF16))
    qc2 = mk(nc.sbuf_tensor("qc2", [128, KI], BF16))
    qbm = mk(nc.sbuf_tensor("qbm", [128, KI // 16], F32))
    qsc = mk(nc.sbuf_tensor("qsc", [128, KI // 16], F32))
    qrs = mk(nc.sbuf_tensor("qrs", [128, KI // 16], F32))
    qscb = mk(nc.sbuf_tensor("qscb", [128, KI // 16], BF16))
    qrsb = mk(nc.sbuf_tensor("qrsb", [128, KI // 16], BF16))
    h_tm = mk(nc.sbuf_tensor("h_tm", [128, TC, KI], BF16))
    hq_tm = mk(nc.sbuf_tensor("hq_tm", [128, 1024], BF16))
    hq_f = mk(nc.sbuf_tensor("hq_f", [128, NKB, 256], BF16))
    stats = mk(nc.sbuf_tensor("stats", [128, 8 * 4 * H], F32))
    lnsc = mk(nc.sbuf_tensor("lnsc", [128, 12], F32))
    rowsum = mk(nc.sbuf_tensor("rowsum", [128, 4], F32))

    ps_big = mk(nc.psum_tensor("ps_big", [128, 2048], F32))
    ps_s = mk(nc.psum_tensor("ps_s", [128, 1024], F32))
    ps_t = mk(nc.psum_tensor("ps_t", [128, 1024], BF16))
    ps_p = mk(nc.psum_tensor("ps_p", [128, 512], F32))

    for ci, cv in enumerate(CONSTS):
        ve.memset(cpool[:, ci:ci + 1], float(cv))
    ve.memset(gkm[:, :, :], 0.0)
    ve.memset(vg_km[:, :, :], 0.0)
    ve.memset(pt_g[:, :, :], 0.0)
    ve.memset(onesr[0:1, :], 1.0)
    R("dve", ve.memset(onesf[0:1, :], 1.0))
    fence(["dve"], ["pool", "sp", "act", "pe"])

    po.iota(iotaf[:, :], [[1, 128]], channel_multiplier=0,
            allow_small_or_imprecise_dtypes=True)
    R("pool", po.iota(iotap[:, :], [[0, 1]], channel_multiplier=1,
                      allow_small_or_imprecise_dtypes=True))
    fence(["pool"], ["dve"])

    ipb = bass.AP(iotap, 0, [[1, 128], [0, 128]])
    ve.tensor_tensor(ident[:, :], iotaf[:, :], ipb, aop.is_equal)
    ve.tensor_scalar(negi[:, :], ident[:, :], -30000.0, None, aop.mult)
    ve.tensor_scalar(tabsc[:, 0:1], iotap[:, :], 63.5, 64.0, aop.is_gt, aop.mult)
    ve.tensor_tensor(tabsc[:, 1:2], iotap[:, :], tabsc[:, 0:1], aop.subtract)
    ve.tensor_scalar(tabsc[:, 2:3], tabsc[:, 1:2], 31.5, 64.0, aop.is_gt, aop.mult)
    ve.tensor_tensor(tabsc[:, 3:4], iotap[:, :], tabsc[:, 2:3], aop.subtract)
    ve.tensor_scalar(tabsc[:, 3:4], tabsc[:, 3:4], 32.0, None, aop.add)
    ve.tensor_tensor(p32[:, :], iotaf[:, :],
                     bass.AP(tabsc, 3, [[128, 128], [0, 128]]), aop.is_equal)
    ve.tensor_tensor(ucur[:, :], iotaf[:, :], ipb, aop.is_gt)
    ve.tensor_scalar(tabsc[:, 8:8 + SPW], iotaf[:, 0:SPW], 16.0, 15.0, aop.mult, aop.add)
    ve.tensor_tensor(useg[:, :], tabsc[:, 8:8 + SPW],
                     bass.AP(iotap, 0, [[1, 128], [0, SPW]]), aop.is_ge)
    ve.tensor_scalar(tabsc[:, 4:5], iotap[:, :], 15.5, None, aop.is_gt)
    for k in range(2, 8):
        ve.scalar_tensor_tensor(tabsc[:, 4:5], iotap[:, :], 16.0 * k - 0.5,
                                tabsc[:, 4:5], aop.is_gt, aop.add)
    ve.tensor_scalar(tabsc[:, 5:6], tabsc[:, 4:5], 128.0, None, aop.add)
    fence(["dve"], ["pool"])
    po.iota(f32a[:, 0:256], [[1, 256]], channel_multiplier=0,
            allow_small_or_imprecise_dtypes=True).then_inc(done["pool"], 1)
    cnt["pool"] += 1
    ve.wait_ge(done["pool"], cnt["pool"])
    R("dve", ve.tensor_tensor(segf[:, :], f32a[:, 0:256],
                              bass.AP(tabsc, 5, [[128, 128], [0, 256]]), aop.is_equal))
    fence(["dve"], ["act", "pe", "sp", "pool"])

    po.iota(invf[0:1, :], [[1, 32]], channel_multiplier=0,
            allow_small_or_imprecise_dtypes=True).then_inc(done["pool"], 1)
    cnt["pool"] += 1
    ac.wait_ge(done["pool"], cnt["pool"])
    R("act", ac.activation(invf[0:1, :], invf[0:1, :], AF.Exp,
                           bias=0.0, scale=-math.log(10000.0) / 32.0))
    fence(["act"], ["dve", "pe"])

    dma("sp", "dm", f32a[0:1, 0:D], bout_in[None, :])
    dma("sp", "dm", f32a[0:64, 1024:1025], wproj_in[:, None])
    dma("sp", "dm", f32a[64:128, 1025:1026], wproj_in[:, None])
    dma("sp", "dm", f32a[:, 1026:1282],
        bass.AP(lng_in.tensor, 0, [[0, 128], [1, 4 * DH]]))
    dma_wait("sp", "dm")
    sp_mark()
    ve.wait_ge(done["sp"], cnt["sp"])
    ve.tensor_copy(boutr[0:1, :], f32a[0:1, 0:D])
    R("dve", ve.tensor_copy(lngt[:, :], f32a[:, 1026:1282]))
    ve.tensor_scalar(tabsc[:, 6:7], iotap[:, :], 63.5, 16.0, aop.is_gt, aop.add)
    ve.tensor_tensor(tabsc[:, 7:8], iotaf[:, 0:1], iotaf[:, 0:1], aop.subtract)
    ve.tensor_tensor(qa[:, 0:32], iotaf[:, 0:32],
                     bass.AP(tabsc, 6, [[128, 128], [0, 32]]), aop.is_equal)
    ve.tensor_copy(tabsc[0:64, 7:8], f32a[0:64, 1024:1025])
    ve.tensor_copy(tabsc[64:128, 7:8], f32a[64:128, 1025:1026])
    R("dve", ve.tensor_tensor(wpsh[:, :], qa[:, 0:32],
                              bass.AP(tabsc, 7, [[128, 128], [0, 32]]), aop.mult))
    fence(["dve"], ["pe", "sp"])

    for src, dst, cols in ((wq_in, wqkv_d, 0), (wkv_in, wqkv_d, D), (wout_in, woutb_d, 0)):
        for cc in range(DCH):
            dma("sp", "din", f32a[:, 0:D], src[cc * 128:(cc + 1) * 128, :])
            dma_wait("sp", "din")
            sp_mark()
            ve.wait_ge(done["sp"], cnt["sp"])
            R("dve", ve.tensor_copy(qa[:, 0:D], f32a[:, 0:D]))
            fence(["dve"], ["pool"])
            if dst is wqkv_d:
                dma("pool", "dw", bass.AP(dst.tensor, cc * 128 * 2 * D + cols,
                                          [[2 * D, 128], [1, D]]), qa[:, 0:D])
            else:
                dma("pool", "dw", bass.AP(dst.tensor, cc * 128 * D,
                                          [[D, 128], [1, D]]), qa[:, 0:D])
            dma_wait("pool", "dw")
            pool_mark()
            sp.wait_ge(done["pool"], cnt["pool"])

    for wg0 in range(0, N // WIN, 8):
        for wi in range(8):
            ve.tensor_scalar(f32a[0:1, 1024 + wi * 32:1024 + (wi + 1) * 32],
                             invf[0:1, :], float((wg0 + wi) * WIN), None, aop.mult)
        for wi in range(8):
            ve.tensor_copy(f32a[0:1, 1280 + wi * 32:1280 + (wi + 1) * 32],
                           invf[0:1, :])
        R("dve", ve.tensor_copy(f32a[0:1, 1280:1281], f32a[0:1, 1280:1281]))
        fence(["dve"], ["pe"])
        te.matmul(ps_big[:, 0:256], iotaf[0:1, :], f32a[0:1, 1280:1536],
                  start=True, stop=False)
        R("pe", te.matmul(ps_big[:, 0:256], onesf[0:1, :], f32a[0:1, 1024:1280],
                          start=False, stop=True))
        fence(["pe"], ["dve"])
        ve.tensor_copy(f32a[:, 0:256], ps_big[:, 0:256])
        for j in range(9, -1, -1):
            c = float(2.0 ** j * 2.0 * math.pi)
            ve.tensor_scalar(f32a[:, 256:512], f32a[:, 0:256], c, c,
                             aop.is_ge, aop.mult)
            ve.tensor_tensor(f32a[:, 0:256], f32a[:, 0:256], f32a[:, 256:512],
                             aop.subtract)
        ve.tensor_scalar(f32a[:, 256:512], f32a[:, 0:256], float(math.pi),
                         float(2 * math.pi), aop.is_gt, aop.mult)
        ve.tensor_tensor(f32a[:, 512:768], f32a[:, 0:256], f32a[:, 256:512],
                         aop.subtract)
        ve.tensor_scalar(f32a[:, 256:512], f32a[:, 0:256], float(math.pi / 2),
                         None, aop.add)
        ve.tensor_scalar(f32a[:, 768:1024], f32a[:, 256:512], float(math.pi),
                         float(2 * math.pi), aop.is_gt, aop.mult)
        R("dve", ve.tensor_tensor(f32a[:, 768:1024], f32a[:, 256:512],
                                  f32a[:, 768:1024], aop.subtract))
        fence(["dve"], ["act"])
        ac.activation(f32a[:, 0:256], f32a[:, 512:768], AF.Sin)
        R("act", ac.activation(f32a[:, 256:512], f32a[:, 768:1024], AF.Sin))
        fence(["act"], ["dve"])
        R("dve", ve.tensor_copy(qa[:, 0:512], f32a[:, 0:512]))
        fence(["dve"], ["pe"])
        for tt in range(4):
            R("pe", te.transpose(ps_t[:, tt * 128:(tt + 1) * 128],
                                 qa[:, tt * 128:(tt + 1) * 128], ident[:, :]))
        fence(["pe"], ["dve"])
        with nodrain():
            for wi in range(8):
                blk, rowo = divmod(wi * 32, 128)
                wc = (wg0 + wi) * WIN
                ve.tensor_scalar(sintab[0:32, 0:WIN] if False else qc1[0:32, wc % 512:wc % 512 + WIN], ps_t[0:1, 0:1], 0.0, None, aop.mult) if False else None
                ve.tensor_scalar(qc1[0:32, 0:WIN], ps_t[rowo:rowo + 32, blk * 128:(blk + 1) * 128],
                                 -1.0, None, aop.mult)
                ve.tensor_copy(qc1[32:64, 0:WIN], ps_t[rowo:rowo + 32, blk * 128:(blk + 1) * 128])
                ve.tensor_copy(qc1[64:96, 0:WIN], qc1[0:32, 0:WIN]) if False else None
                ve.tensor_copy(qc2[0:32, 0:WIN], ps_t[rowo:rowo + 32, (2 + blk) * 128:(3 + blk) * 128])
                R("dve", ve.tensor_copy(qc2[32:64, 0:WIN], ps_t[rowo:rowo + 32, (2 + blk) * 128:(3 + blk) * 128]))
                fence(["dve"], ["pool"])
                dma("pool", "dw", bass.AP(sintab_d.tensor, wc, [[N, 64], [1, WIN]]),
                    qc1[0:64, 0:WIN])
                dma("pool", "dw", bass.AP(sintab_d.tensor, 64 * N + wc, [[N, 64], [1, WIN]]),
                    qc1[0:64, 0:WIN])
                dma("pool", "dw", bass.AP(costab_d.tensor, wc, [[N, 64], [1, WIN]]),
                    qc2[0:64, 0:WIN])
                dma("pool", "dw", bass.AP(costab_d.tensor, 64 * N + wc, [[N, 64], [1, WIN]]),
                    qc2[0:64, 0:WIN])
                dma_wait("pool", "dw")
                pool_mark()
                ve.wait_ge(done["pool"], cnt["pool"])

    def stair(K, rsb_t, scb_t, ysrc, out_t, afp32=False, out_flat=None, final3=None):
        if afp32:
            ac.activation(qs[:, 0:K], ysrc, AF.Sign)
            R("act", ac.activation(ysrc, ysrc, AF.Abs))
            fence(["act"], ["dve"])
            a = ysrc
        else:
            ac.activation(qa[:, 0:K], ysrc, AF.Abs)
            R("act", ac.activation(qs[:, 0:K], ysrc, AF.Sign))
            fence(["act"], ["dve"])
            a = qa[:, 0:K]
        ve.tensor_scalar(qc1[:, 0:K], a, 0.25, None, aop.is_gt)
        ve.scalar_tensor_tensor(qc2[:, 0:K], a, 0.75, qc1[:, 0:K], aop.is_gt, aop.add)
        ve.scalar_tensor_tensor(qc1[:, 0:K], a, 1.25, qc2[:, 0:K], aop.is_gt, aop.add)
        ve.scalar_tensor_tensor(qc2[:, 0:K], a, 1.75, qc1[:, 0:K],
                                 aop.is_gt, aop.add)
        ve.tensor_scalar(qc1[:, 0:K], qc2[:, 0:K], 0.5, None, aop.mult)
        ve.scalar_tensor_tensor(qc2[:, 0:K], a, 2.5, qc1[:, 0:K], aop.is_gt, aop.add)
        ve.scalar_tensor_tensor(qc1[:, 0:K], a, 3.5, qc2[:, 0:K], aop.is_gt, aop.add)
        ve.tensor_scalar(qc2[:, 0:K], a, 5.0, 2.0, aop.is_gt, aop.mult)
        ve.tensor_tensor(qc1[:, 0:K], qc1[:, 0:K], qc2[:, 0:K], aop.add)
        ve.tensor_tensor(qc2[:, 0:K], qc1[:, 0:K], qs[:, 0:K], aop.mult)
        if final3 is not None:
            dst, srcv, scb = final3
            return R("dve", ve.tensor_tensor(dst, srcv, scb, aop.mult))
        scb = bass.AP(scb_t, 0, [[KI // 16, 128], [1, K // 16], [0, 16]])
        dst = out_flat if out_flat is not None else out_t.rearrange(
            "p (a b) -> p a b", b=16)
        return R("dve", ve.tensor_tensor(
            dst, qc2[:, 0:K].rearrange("p (a b) -> p a b", b=16),
            scb, aop.mult))

    def wquant(src_dram, dst_dram, nrows, kcols):
        nrt = nrows // 128
        kct = kcols // 128
        dst_rowlen = nrows
        for nr in range(nrt):
            sp.wait_ge(done["act"], cnt["act"])
            sp.wait_ge(done["dve"], cnt["dve"])
            dma("sp", "din", f32a[:, 0:kcols], src_dram[nr * 128:(nr + 1) * 128, :])
            dma_wait("sp", "din")
            sp_mark()
            ve.wait_ge(done["sp"], cnt["sp"])
            ve.wait_ge(done["pool"], cnt["pool"])
            nb = kcols // 16
            ve.tensor_reduce(qbm[:, 0:nb],
                             f32a[:, 0:kcols].rearrange("p (a b) -> p a b", b=16),
                             AX.X, aop.max, apply_absolute_value=True)
            ve.tensor_scalar(qsc[:, 0:nb], qbm[:, 0:nb], 1.0 / 6.0, 1e-12,
                             aop.mult, aop.max)
            ve.reciprocal(qrs[:, 0:nb], qsc[:, 0:nb])
            ve.tensor_copy(qscb[:, 0:nb], qsc[:, 0:nb])
            rsb = bass.AP(qrs, 0, [[KI // 16, 128], [1, nb], [0, 16]])
            R("dve", ve.tensor_tensor(
                f32a[:, 0:kcols].rearrange("p (a b) -> p a b", b=16),
                f32a[:, 0:kcols].rearrange("p (a b) -> p a b", b=16),
                rsb, aop.mult))
            fence(["dve"], ["act"])
            stair(kcols, qrs, qscb, f32a[:, 0:kcols], qc2[:, 0:kcols], afp32=True)
            fence(["dve"], ["pe"])
            for k0 in range(0, kct, 8):
                kn = min(8, kct - k0)
                for kk in range(kn):
                    R("pe", te.transpose(ps_t[:, kk * 128:(kk + 1) * 128],
                                         qc2[:, (k0 + kk) * 128:(k0 + kk + 1) * 128],
                                         ident[:, :]))
                fence(["pe"], ["dve"])
                R("dve", ve.tensor_copy(qa[:, 0:kn * 128],
                                        bass.AP(ps_t, 0, [[1024, 128], [1, kn * 128]])))
                fence(["dve"], ["pool", "pe"])
                dst = bass.AP(dst_dram.tensor, (k0 * 128) * dst_rowlen + nr * 128,
                              [[dst_rowlen, 128], [128 * dst_rowlen, kn], [1, 128]])
                dma("pool", "dw", dst,
                    bass.AP(qa, 0, [[KI, 128], [128, kn], [1, 128]]))
                dma_wait("pool", "dw")
                pool_mark()
                ve.wait_ge(done["pool"], cnt["pool"])
            last["pe"] = None

    wquant(wg_in, wgq_d, KI, D)
    wquant(wu_in, wuq_d, KI, D)
    wquant(wd_in, wdq_d, D, KI)
    po.sem_inc(done["pool"], 1); cnt["pool"] += 1
    for e in ("pe", "act", "dve", "sp"):
        ENG[e].wait_ge(done["pool"], cnt["pool"])

    pf = {}

    def emit_prefetch(bb2, mt2, xguard, tabguard):
        t02 = mt2 * T
        waiton("sp", "dve", tabguard)
        dma("sp", "dm", costab[:, :], costab_d[:, t02:t02 + T])
        dma("sp", "dm", sintab[:, :], sintab_d[:, t02:t02 + T])
        tabv = dsem["dm"][1]
        waiton("pool", "dve", xguard)
        xv = []
        for tcc in range(TC):
            dma("pool", "dinp", x_tm[:, tcc, :],
                x_in[bb2, t02 + tcc * 128:t02 + (tcc + 1) * 128, :])
            xv.append(dsem["dinp"][1])
        pf[(bb2, mt2)] = (tabv, xv)

    emit_prefetch(0, 0, cnt["dve"], cnt["dve"])

    for bb in range(B_CORE):
        for mt in range(MPB):
            t0 = mt * T
            w0 = mt * WPM
            sg0 = w0 * SPW
            nsg = WPM * SPW
            lt0, lr0 = divmod(sg0, 128)

            tabv, xv = pf.pop((bb, mt))
            ve.wait_ge(dsem["dm"][0], tabv)
            if "cost" in tap_d:
                tap_wait()
                po.wait_ge(dsem["dm"][0], tabv)
                dma("pool", "dmp", tap_d["cost"][bb, mt], costab[:, :])
            for tcc in range(TC):
                te.wait_ge(dsem["dinp"][0], xv[tcc])
                if tcc == 0:
                    te.wait_ge(done["dve"], cnt["dve"])
                for cc in range(DCH):
                    R("pe", te.transpose(ps_t[:, cc * 128:(cc + 1) * 128],
                                         x_tm[:, tcc, cc * 128:(cc + 1) * 128],
                                         ident[:, :]))
                fence(["pe"], ["dve"])
                R("dve", ve.tensor_copy(
                    bass.AP(xfw, tcc * 128, [[DCH * T, 128], [T, DCH], [1, 128]]),
                    bass.AP(ps_t, 0, [[1024, 128], [128, DCH], [1, 128]])))
                fence(["dve"], ["pe"])

            if "xtm" in tap_d:
                tap_wait()
                dma("pool", "dmp", tap_d["xtm"][bb, mt], x_tm[:, :, :])
            if "xf" in tap_d:
                tap_wait()
                dma("pool", "dmp", tap_d["xf"][bb, mt], xfw[:, :, :])
            ctile = bass.AP(costab, 0, [[T, 128], [0, 2], [1, T]])
            stile = bass.AP(sintab, 0, [[T, 128], [0, 2], [1, T]])
            qk_ev = [0] * (2 * DCH)
            rot_dve = {}
            wq_guard = {}
            for fc in range(2 * DCH):
                db = fc % 2
                if db in wq_guard:
                    waiton("pool", "pe", wq_guard[db])
                dma("pool", "dw", wq_sb[:, db, :],
                    bass.AP(wqkv_d.tensor, fc * 128,
                            [[2 * D, 128], [128 * 2 * D, DCH], [1, 128]]))
                te.wait_ge(dsem["dw"][0], dsem["dw"][1])
                if fc >= 2:
                    waiton("pe", "act", qk_ev[fc - 2])
                for cc in range(DCH):
                    R("pe", te.matmul(ps_big[:, db * 512:db * 512 + T],
                                      wq_sb[:, db, cc * 128:(cc + 1) * 128],
                                      xfw[:, cc, :], start=(cc == 0), stop=(cc == DCH - 1)))
                qmm = mark("pe")
                wq_guard[db] = qmm
                waiton("act", "pe", qmm)
                with nodrain():
                    if fc < DCH:
                        R("act", ac.activation(q_rot[:, fc, :], ps_big[:, db * 512:db * 512 + T],
                                               AF.Copy, bias=0.0, scale=0.125))
                    else:
                        R("act", ac.activation(kvro[:, fc - DCH, :],
                                               ps_big[:, db * 512:db * 512 + T],
                                               AF.Copy, bias=0.0, scale=1.0))
                qk_ev[fc] = mark("act")
                if fc % 2 == 1:
                    rp = fc // 2
                    buf = q_rot if fc < DCH else kvro
                    cc0 = (fc if fc < DCH else fc - DCH) - 1
                    rps = ps_big[:, 1024:2048] if rp % 2 == 0 else ps_s[:, 0:1024]
                    waiton("pe", "act", qk_ev[fc])
                    if rp >= 2:
                        waiton("pe", "dve", rot_dve[rp - 2])
                    te.matmul(rps[:, 0:T], p32[:, :], buf[:, cc0, :],
                              start=True, stop=True)
                    R("pe", te.matmul(rps[:, T:2 * T], p32[:, :], buf[:, cc0 + 1, :],
                                      start=True, stop=True))
                    rmm = mark("pe")
                    waiton("dve", "pe", rmm)
                    bb2 = buf[:, cc0:cc0 + 2, :]
                    ve.tensor_tensor(qa[:, 0:2 * T].rearrange("p (a t) -> p a t", t=T),
                                     bb2, ctile, aop.mult)
                    ve.tensor_tensor(qa[:, 2 * T:4 * T].rearrange("p (a t) -> p a t", t=T),
                                     rps[:, 0:2 * T].rearrange("p (a t) -> p a t", t=T),
                                     stile, aop.mult)
                    R("dve", ve.tensor_tensor(
                        bb2, qa[:, 0:2 * T].rearrange("p (a t) -> p a t", t=T),
                        qa[:, 2 * T:4 * T].rearrange("p (a t) -> p a t", t=T),
                        aop.add))
                    rot_dve[rp] = mark("dve")
            waiton("pe", "dve", rot_dve[DCH - 1])
            waiton("act", "dve", rot_dve[DCH - 1])

            if "qpre" in tap_d:
                tap_wait()
                dma("pool", "dmp", tap_d["qpre"][bb, mt], q_rot[:, :, :])
                dma_wait("pool", "dmp")
                pool_mark()
                for e in ("pe", "dve"):
                    ENG[e].wait_ge(done["pool"], cnt["pool"])
            if "qrot" in tap_d:
                tap_wait()
                dma("pool", "dmp", tap_d["qrot"][bb, mt], q_rot[:, :, :])

            for cc in range(DCH):
                R("pe", te.matmul(ps_p[0:H, 0:T], wpsh[:, 16 - 2 * cc:32 - 2 * cc],
                                  kvro[:, cc, :], start=(cc == 0),
                                  stop=(cc == DCH - 1)))
            fence(["pe"], ["act"])
            R("act", ac.activation(pscore[:, :], ps_p[0:H, 0:T], AF.Copy))
            fence(["act"], ["pe", "act"])
            if "pscore" in tap_d:
                tap_wait()
                dma("pool", "dmp", tap_d["pscore"][bb, mt], pscore[:, :])
                dma_wait("pool", "dmp")
                pool_mark()
                ac.wait_ge(done["pool"], cnt["pool"])
            R("act", ac.activation(pscore[:, :], pscore[:, :], AF.Exp))
            fence(["act"], ["dve"])
            ve.tensor_reduce(qbm[0:H, 0:T // 16],
                             pscore[:, :].rearrange("p (a b) -> p a b", b=16),
                             AX.X, aop.add)
            ve.reciprocal(qrs[0:H, 0:T // 16], qbm[0:H, 0:T // 16])
            rsb = bass.AP(qrs, 0, [[KI // 16, H], [1, T // 16], [0, 16]])
            R("dve", ve.tensor_tensor(
                pkv[:, :].rearrange("p (a b) -> p a b", b=16),
                pscore[:, :].rearrange("p (a b) -> p a b", b=16), rsb, aop.mult))
            fence(["dve"], ["pe"])
            for tcc in range(TC):
                R("pe", te.transpose(ps_t[:, tcc * H:(tcc + 1) * H],
                                     pkv[:, tcc * 128:(tcc + 1) * 128],
                                     ident[0:H, 0:H]))
            fence(["pe"], ["dve"])
            R("dve", ve.tensor_copy(
                bass.AP(pkv_km, 0, [[TC * H, 128], [H, TC], [1, H]]),
                bass.AP(ps_t, 0, [[1024, 128], [H, TC], [1, H]])))
            fence(["dve"], ["pe"])

            for w in range(WPM):
                te.wait_ge(done["dve"], cnt["dve"])
                for cc in range(DCH):
                    R("pe", te.transpose(ps_t[:, cc * 128:(cc + 1) * 128],
                                         kvro[:, cc, w * 128:(w + 1) * 128],
                                         ident[:, :]))
                fence(["pe"], ["dve"])
                R("dve", ve.tensor_copy(kv_km[:, w, :],
                                        bass.AP(ps_t, 0, [[1024, 128], [1, H * DH]])))
                fence(["dve"], ["pe"])
            if "kvkm" in tap_d:
                tap_wait()
                dma("pool", "dmp", tap_d["kvkm"][bb, mt], kv_km[:, :, :])

            for w in range(WPM):
                pk = bass.AP(pkv_km, w * H, [[TC * H, 128], [1, H], [0, DH]])
                R("dve", ve.tensor_tensor(
                    bass.AP(xfw, w * H * DH, [[DCH * T, 128], [DH, H], [1, DH]]),
                    kv_km[:, w, :].rearrange("p (h d) -> p h d", d=DH),
                    pk, aop.mult))
            fence(["dve"], ["pe"])
            for w in range(WPM):
                lr = w * SPW
                st = segf[:, 128 - lr:256 - lr]
                te.matmul(ps_p[:, 0:512], st,
                          bass.AP(xfw, w * H * DH, [[DCH * T, 128], [1, 512]]),
                          start=(w == 0), stop=(w == WPM - 1))
                R("pe", te.matmul(ps_big[:, 1536:2048], st,
                                  bass.AP(xfw, w * H * DH + 512,
                                          [[DCH * T, 128], [1, 512]]),
                                  start=(w == 0), stop=(w == WPM - 1)))
            fence(["pe"], ["act"])
            with nodrain():
                ac.activation(gkm[lr0:lr0 + nsg, lt0, 0:512], ps_p[0:nsg, 0:512],
                              AF.Copy)
                R("act", ac.activation(gkm[lr0:lr0 + nsg, lt0, 512:1024],
                                       ps_big[0:nsg, 1536:2048], AF.Copy))
            fence(["act"], ["pe", "dve"])
            if "gkm" in tap_d:
                tap_wait()
                dma("pool", "dmp", tap_d["gkm"][bb, mt], gkm[:, :, :])

            def ln64(src_ap, dst_ap, r0, nr, goff):
                R("dve", ve.tensor_reduce(
                    stats[r0:r0 + nr, 0:H],
                    src_ap.rearrange("p (h d) -> p h d", d=DH), AX.X, aop.add))
                fence(["dve"], ["act"])
                R("act", ac.activation(f32a[r0:r0 + nr, 0:H * DH], src_ap, AF.Square))
                fence(["act"], ["dve"])
                ve.tensor_reduce(stats[r0:r0 + nr, H:2 * H],
                                 f32a[r0:r0 + nr, 0:H * DH].rearrange(
                                     "p (h d) -> p h d", d=DH), AX.X, aop.add)
                ve.tensor_scalar(stats[r0:r0 + nr, 2 * H:3 * H],
                                 stats[r0:r0 + nr, 0:H], 1.0 / DH, None, aop.mult)
                ve.tensor_scalar(stats[r0:r0 + nr, 3 * H:4 * H],
                                 stats[r0:r0 + nr, H:2 * H], 1.0 / DH, None, aop.mult)
                ve.tensor_tensor(stats[r0:r0 + nr, 4 * H:5 * H],
                                 stats[r0:r0 + nr, 2 * H:3 * H],
                                 stats[r0:r0 + nr, 2 * H:3 * H], aop.mult)
                R("dve", ve.tensor_tensor(stats[r0:r0 + nr, 3 * H:4 * H],
                                          stats[r0:r0 + nr, 3 * H:4 * H],
                                          stats[r0:r0 + nr, 4 * H:5 * H], aop.subtract))
                fence(["dve"], ["act"])
                R("act", ac.activation(stats[r0:r0 + nr, 4 * H:5 * H],
                                       stats[r0:r0 + nr, 3 * H:4 * H], AF.Ln,
                                       bias=1e-5, scale=1.0))
                R("act", ac.activation(stats[r0:r0 + nr, 5 * H:6 * H],
                                       stats[r0:r0 + nr, 4 * H:5 * H], AF.Exp,
                                       bias=0.0, scale=-0.5))
                fence(["act"], ["dve"])
                srow = 8 * 4 * H
                mu_b = bass.AP(stats, r0 * srow + 2 * H, [[srow, nr], [1, H], [0, DH]])
                rs_b = bass.AP(stats, r0 * srow + 5 * H, [[srow, nr], [1, H], [0, DH]])
                g_b = bass.AP(lngt, r0 * 4 * DH + goff, [[4 * DH, nr], [0, H], [1, DH]])
                b_b = bass.AP(lngt, r0 * 4 * DH + goff + DH, [[4 * DH, nr], [0, H], [1, DH]])
                sq = f32a[r0:r0 + nr, 0:H * DH].rearrange("p (h d) -> p h d", d=DH)
                ve.tensor_tensor(sq, src_ap.rearrange("p (h d) -> p h d", d=DH),
                                 mu_b, aop.subtract)
                ve.tensor_tensor(sq, sq, rs_b, aop.mult)
                ve.tensor_tensor(sq, sq, g_b, aop.mult)
                R("dve", ve.tensor_tensor(dst_ap.rearrange("p (h d) -> p h d", d=DH),
                                          sq, b_b, aop.add))

            WH = WPM * H
            W2 = WPM // 2
            GH = W2 * H
            kvv = kv_km[:, :, :].rearrange("p w (h d) -> p (w h) d", d=DH)
            R("dve", ve.tensor_reduce(stats[:, 0:WH], kvv, AX.X, aop.add))
            fence(["dve"], ["act"])
            for hw in range(2):
                R("act", ac.activation(
                    f32a[:, 0:GH * DH],
                    kv_km[:, hw * W2:(hw + 1) * W2, :].rearrange(
                        "p w hd -> p (w hd)"), AF.Square))
                fence(["act"], ["dve"])
                R("dve", ve.tensor_reduce(
                    stats[:, WH + hw * GH:WH + (hw + 1) * GH],
                    f32a[:, 0:GH * DH].rearrange("p (g d) -> p g d", d=DH),
                    AX.X, aop.add))
                if hw == 0:
                    fence(["dve"], ["act"])
            ve.tensor_scalar(stats[:, 2 * WH:3 * WH], stats[:, 0:WH],
                             1.0 / DH, None, aop.mult)
            ve.tensor_scalar(stats[:, 3 * WH:4 * WH], stats[:, WH:2 * WH],
                             1.0 / DH, None, aop.mult)
            ve.tensor_tensor(stats[:, 4 * WH:5 * WH], stats[:, 2 * WH:3 * WH],
                             stats[:, 2 * WH:3 * WH], aop.mult)
            R("dve", ve.tensor_tensor(stats[:, 3 * WH:4 * WH],
                                      stats[:, 3 * WH:4 * WH],
                                      stats[:, 4 * WH:5 * WH], aop.subtract))
            fence(["dve"], ["act"])
            ac.activation(stats[:, 4 * WH:5 * WH], stats[:, 3 * WH:4 * WH],
                          AF.Ln, bias=1e-5, scale=1.0)
            R("act", ac.activation(stats[:, 5 * WH:6 * WH], stats[:, 4 * WH:5 * WH],
                                   AF.Exp, bias=0.0, scale=-0.5))
            fence(["act"], ["dve"])
            for hw in range(2):
                kvh = kv_km[:, hw * W2:(hw + 1) * W2, :].rearrange(
                    "p w (h d) -> p (w h) d", d=DH)
                mu_b = bass.AP(stats, 2 * WH + hw * GH,
                               [[8 * 4 * H, 128], [1, GH], [0, DH]])
                rs_b = bass.AP(stats, 5 * WH + hw * GH,
                               [[8 * 4 * H, 128], [1, GH], [0, DH]])
                g_b = bass.AP(lngt, 0, [[4 * DH, 128], [0, GH], [1, DH]])
                b_b = bass.AP(lngt, DH, [[4 * DH, 128], [0, GH], [1, DH]])
                sqv = f32a[:, 0:GH * DH].rearrange("p (g d) -> p g d", d=DH)
                ve.tensor_tensor(sqv, kvh, mu_b, aop.subtract)
                ve.tensor_tensor(sqv, sqv, rs_b, aop.mult)
                ve.tensor_tensor(sqv, sqv, g_b, aop.mult)
                R("dve", ve.tensor_tensor(
                    vh_km[:, 1 + hw * W2:1 + (hw + 1) * W2, :].rearrange(
                        "p w (h d) -> p (w h) d", d=DH),
                    sqv, b_b, aop.add))
            fence(["dve"], ["pe", "act"])
            if "vhkm" in tap_d:
                tap_wait()
                dma("pool", "dmp", tap_d["vhkm"][bb, mt], vh_km[:, 1:WPM + 1, :])
            ln64(gkm[lr0:lr0 + nsg, lt0, :], vg_km[lr0:lr0 + nsg, lt0, :],
                 lr0, nsg, 2 * DH)
            fence(["dve"], ["pe", "act"])
            if "vgkm" in tap_d:
                tap_wait()
                dma("pool", "dmp", tap_d["vgkm"][bb, mt], vg_km[:, :, :])

            for w in range(WPM):
                for cc in range(DCH):
                    R("pe", te.transpose(ps_t[:, cc * 128:(cc + 1) * 128],
                                         vh_km[:, w + 1, cc * 128:(cc + 1) * 128],
                                         ident[:, :]))
                fence(["pe"], ["dve"])
                R("dve", ve.tensor_copy(
                    bass.AP(vh_f, (w + 1) * WIN,
                            [[DCH * (WPM + 1) * WIN, 128], [(WPM + 1) * WIN, DCH],
                             [1, WIN]]),
                    bass.AP(ps_t, 0, [[1024, 128], [128, DCH], [1, 128]])))
                fence(["dve"], ["pe"])
            for cc in range(DCH):
                R("pe", te.transpose(ps_t[:, cc * 128:(cc + 1) * 128],
                                     vg_km[:, lt0, cc * 128:(cc + 1) * 128],
                                     ident[:, :]))
            fence(["pe"], ["dve"])
            R("dve", ve.tensor_copy(
                bass.AP(vg_f, lt0 * 128 + lr0,
                        [[DCH * LMT * 128, 128], [LMT * 128, DCH], [1, nsg]]),
                bass.AP(ps_t, lr0, [[1024, 128], [128, DCH], [1, nsg]])))
            fence(["dve"], ["pe"])

            av_pe = [0] * DCH
            av_ev = [0] * DCH
            NW = 2 * DCH * WPM
            pe_sg = [0] * NW
            act_eg = [0] * NW
            dve_ng = [0] * NW
            pe_tg = [0] * NW
            dve_vg = [0] * NW

            def geom(w):
                wg = w0 + w
                has_prev = wg > 0
                loc0 = 128 if has_prev else 0
                gn_old = wg * SPW
                gn = gn_old + SPW
                return wg, has_prev, loc0, gn_old, gn, loc0 + 128

            for g in range(NW + 1):
                if g < NW:
                    hidx = g // WPM
                    w = g % WPM
                    cc = hidx // 2
                    ro = (hidx % 2) * 64
                    wg, has_prev, loc0, gn_old, gn, off2 = geom(w)
                    par = g % 2
                    pso = par * 512
                    qs_ap = q_rot[ro:ro + 64, cc, w * 128:(w + 1) * 128]
                    if g >= 2:
                        waiton("pe", "act", act_eg[g - 2])
                    kst = (w if has_prev else w + 1) * WIN
                    te.matmul(ps_s[:, pso:pso + loc0 + 128], qs_ap,
                              vh_f[ro:ro + 64, cc, kst:kst + loc0 + 128],
                              start=True, stop=False)
                    te.matmul(ps_s[:, pso + off2:pso + off2 + gn], qs_ap,
                              vg_f[ro:ro + 64, cc, 0:gn], start=False,
                              stop=False)
                    te.matmul(ps_s[:, pso + loc0:pso + loc0 + 128],
                              negi[:, :], ucur[:, :], start=False, stop=False)
                    R("pe", te.matmul(
                        ps_s[:, pso + off2 + gn_old:pso + off2 + gn],
                        negi[:, :], useg[:, :], start=False, stop=True))
                    pe_sg[g] = mark("pe")
                if g >= 1:
                    gq = g - 1
                    hidx = gq // WPM
                    w = gq % WPM
                    cc = hidx // 2
                    hp = hidx % 2
                    hh = hidx
                    ro = hp * 64
                    wg, has_prev, loc0, gn_old, gn, off2 = geom(w)
                    par = gq % 2
                    pso = par * 512
                    ncols = off2 + gn
                    waiton("act", "pe", pe_sg[gq])
                    if gq >= 2:
                        waiton("act", "pe", pe_tg[gq - 2])
                    with nodrain():
                        R("act", ac.activation(
                            p_sb[:, pso:pso + ncols], ps_s[:, pso:pso + ncols],
                            AF.Exp, bias=-C_SOFT, scale=1.0,
                            accum_out=denom[:, par:par + 1]))
                    act_eg[gq] = mark("act")
                    waiton("dve", "act", act_eg[gq])
                    ve.reciprocal(denom[:, 2 + par:3 + par],
                                  denom[:, par:par + 1])
                    with nodrain():
                        R("dve", ve.tensor_scalar(
                            p_sb[:, pso:pso + ncols], p_sb[:, pso:pso + ncols],
                            denom[:, 2 + par:3 + par], None, aop.mult))
                    dve_ng[gq] = mark("dve")
                    waiton("pe", "dve", dve_ng[gq])
                    if gq >= 1:
                        waiton("pe", "dve", dve_vg[gq - 1])
                    nt = 0
                    if has_prev:
                        te.transpose(ps_t[:, 0:128],
                                     p_sb[:, pso:pso + 128], ident[:, :])
                        nt += 1
                    te.transpose(ps_t[:, nt * 128:nt * 128 + 128],
                                 p_sb[:, pso + loc0:pso + loc0 + 128],
                                 ident[:, :])
                    nt += 1
                    gdone = 0
                    while gdone < gn:
                        take = min(128, gn - gdone)
                        R("pe", te.transpose(
                            ps_t[0:take, nt * 128:nt * 128 + 128],
                            p_sb[:, pso + off2 + gdone:pso + off2 + gdone + take],
                            ident[:, :]))
                        nt += 1
                        gdone += take
                    pe_tg[gq] = mark("pe")
                    waiton("dve", "pe", pe_tg[gq])
                    ve.memset(pt_g[:, :, w * 128:(w + 1) * 128], 0.0)
                    ve.drain()
                    with nodrain():
                        if has_prev:
                            ve.tensor_copy(
                                bass.AP(pt_loc, w * 256 + 128,
                                        [[(WPM + 1) * 256, 128], [1, 256]]),
                                ps_t[:, 0:256])
                            nt = 2
                        else:
                            ve.tensor_copy(pt_loc[:, w + 1, 0:128],
                                           ps_t[:, 0:128])
                            nt = 1
                        gdone = 0
                        while gdone < gn:
                            take = min(128, gn - gdone)
                            ve.tensor_copy(pt_g[0:take, gdone // 128,
                                                w * 128:(w + 1) * 128],
                                           ps_t[0:take, nt * 128:nt * 128 + 128])
                            nt += 1
                            gdone += take
                        R("dve", ve.tensor_copy(denom[:, 4 + par:5 + par],
                                                denom[:, 2 + par:3 + par]))
                    dve_vg[gq] = mark("dve")
                    if w == WPM - 1:
                        avo = (cc % 2) * 1024
                        waiton("pe", "dve", dve_vg[gq])
                        if cc >= 2 and hp == 0:
                            waiton("pe", "dve", av_ev[cc - 2])
                        for aw in range(WPM):
                            awg = w0 + aw
                            te.matmul(ps_big[ro:ro + 64, avo + aw * 128:avo + (aw + 1) * 128],
                                      vh_km[:, aw + 1, hh * DH:(hh + 1) * DH],
                                      pt_loc[:, aw + 1, 0:128], start=(aw == 0), stop=False)
                            if awg > 0:
                                te.matmul(ps_big[ro:ro + 64, avo + aw * 128:avo + (aw + 1) * 128],
                                          vh_km[:, aw, hh * DH:(hh + 1) * DH],
                                          pt_loc[:, aw, 128:256], start=False, stop=False)
                        ngl = (w0 + WPM) * SPW
                        gdone = 0
                        while gdone < ngl:
                            take = min(128, ngl - gdone)
                            R("pe", te.matmul(ps_big[ro:ro + 64, avo:avo + T],
                                              vg_km[0:take, gdone // 128,
                                                    hh * DH:(hh + 1) * DH],
                                              pt_g[0:take, gdone // 128, 0:T],
                                              start=False, stop=(gdone + take >= ngl)))
                            gdone += take
                        av_pe[cc] = mark("pe")
                        if hp == 1:
                            waiton("dve", "pe", av_pe[cc])
                            with nodrain():
                                R("dve", ve.tensor_copy(kvro[:, cc, :],
                                                        ps_big[:, avo:avo + T]))
                            av_ev[cc] = mark("dve")
            waiton("pe", "dve", av_ev[DCH - 1])
            waiton("act", "dve", av_ev[DCH - 1])
            if "attnf" in tap_d:
                tap_wait()
                dma("pool", "dmp", tap_d["attnf"][bb, mt], kvro[:, :, :])

            wo_guard = {}
            stt_m = [0] * TC
            xq_dve = [0] * TC
            for tcc in range(TC):
                half = (tcc % 2) * 1024
                sub = tcc % 2
                lo = sub * 6
                for cc in range(DCH):
                    db = cc % 2
                    if db in wo_guard:
                        waiton("pool", "pe", wo_guard[db])
                    dma("pool", "dw", wo_sb[:, db, :],
                        bass.AP(woutb_d.tensor, cc * 128 * D, [[D, 128], [1, D]]))
                    te.wait_ge(dsem["dw"][0], dsem["dw"][1])
                    if cc == 0:
                        if tcc == 0:
                            te.wait_ge(done["dve"], cnt["dve"])
                            te.wait_ge(done["act"], cnt["act"])
                        elif tcc >= 2:
                            waiton("pe", "dve", stt_m[tcc - 2])
                    for nh in range(2):
                        R("pe", te.matmul(
                            ps_big[:, half + nh * 512:half + (nh + 1) * 512],
                            kvro[:, cc, tcc * 128:(tcc + 1) * 128],
                            wo_sb[:, db, nh * 512:(nh + 1) * 512],
                            start=(cc == 0), stop=False))
                    last["pe"].then_inc(done["pe"], 1)
                    cnt["pe"] += 1
                    last["pe"] = None
                    wo_guard[db] = cnt["pe"]
                te.matmul(ps_big[:, half:half + 512], onesr[0:1, :],
                          boutr[0:1, 0:512], start=False, stop=True)
                R("pe", te.matmul(ps_big[:, half + 512:half + 1024],
                                  onesr[0:1, :], boutr[0:1, 512:1024],
                                  start=False, stop=True))
                wmm = mark("pe")
                waiton("dve", "pe", wmm)
                R("dve", ve.scalar_tensor_tensor(
                    x1[:, tcc, :], ps_big[:, half:half + D], 0.0,
                    x_tm[:, tcc, :], aop.add, aop.add,
                    accum_out=rowsum[:, sub * 2:sub * 2 + 1]))
                stt_m[tcc] = mark("dve")
                waiton("act", "dve", stt_m[tcc])
                R("act", ac.activation(f32a[:, 2 * D:3 * D], x1[:, tcc, :], AF.Square,
                                       accum_out=rowsum[:, sub * 2 + 1:sub * 2 + 2]))
                fence(["act"], ["dve"])
                ve.tensor_scalar(lnsc[:, lo:lo + 1], rowsum[:, sub * 2:sub * 2 + 1],
                                 1.0 / D, None, aop.mult)
                ve.tensor_scalar(lnsc[:, lo + 1:lo + 2],
                                 rowsum[:, sub * 2 + 1:sub * 2 + 2],
                                 1.0 / D, None, aop.mult)
                ve.tensor_tensor(lnsc[:, lo + 2:lo + 3], lnsc[:, lo:lo + 1],
                                 lnsc[:, lo:lo + 1], aop.mult)
                R("dve", ve.tensor_tensor(lnsc[:, lo + 1:lo + 2], lnsc[:, lo + 1:lo + 2],
                                          lnsc[:, lo + 2:lo + 3], aop.subtract))
                fence(["dve"], ["act"])
                ac.activation(lnsc[:, lo + 3:lo + 4], lnsc[:, lo + 1:lo + 2], AF.Ln,
                              bias=1e-5, scale=1.0)
                R("act", ac.activation(lnsc[:, lo + 4:lo + 5], lnsc[:, lo + 3:lo + 4],
                                       AF.Exp, bias=0.0, scale=-0.5))
                fence(["act"], ["dve"])
                ve.tensor_tensor(lnsc[:, lo + 5:lo + 6], lnsc[:, lo:lo + 1],
                                 lnsc[:, lo + 4:lo + 5], aop.mult)
                R("dve", ve.tensor_scalar(lnsc[:, lo + 5:lo + 6], lnsc[:, lo + 5:lo + 6],
                                          -1.0, None, aop.mult))
                fence(["dve"], ["act"])
                R("act", ac.activation(x1[:, tcc, :], x1[:, tcc, :], AF.Identity,
                                       bias=lnsc[:, lo + 5:lo + 6],
                                       scale=lnsc[:, lo + 4:lo + 5]))
                apm = mark("act")
                yst = sub * D
                nb = D // 16
                waiton("dve", "act", apm)
                ve.tensor_reduce(qbm[:, 0:nb],
                                 x1[:, tcc, :].rearrange("p (a b) -> p a b", b=16),
                                 AX.X, aop.max, apply_absolute_value=True)
                ve.tensor_scalar(qsc[:, 0:nb], qbm[:, 0:nb], 1.0 / 6.0, 1e-12,
                                 aop.mult, aop.max)
                ve.reciprocal(qrs[:, 0:nb], qsc[:, 0:nb])
                rsb = bass.AP(qrs, 0, [[KI // 16, 128], [1, nb], [0, 16]])
                R("dve", ve.tensor_tensor(
                    f32a[:, yst:yst + D].rearrange("p (a b) -> p a b", b=16),
                    x1[:, tcc, :].rearrange("p (a b) -> p a b", b=16),
                    rsb, aop.mult))
                fence(["dve"], ["act"])
                stair(D, qrs, qsc, f32a[:, yst:yst + D], None, afp32=True,
                      out_flat=bass.AP(xfw, tcc * D,
                                       [[DCH * T, 128], [16, D // 16], [1, 16]]))
                xq_dve[tcc] = mark("dve")
            if "x1" in tap_d:
                tap_wait()
                dma("pool", "dmp", tap_d["x1"][bb, mt], x1[:, :, :])

            for tcc in range(TC):
                waiton("pe", "dve", xq_dve[tcc])
                for cc in range(DCH):
                    R("pe", te.transpose(
                        ps_t[:, cc * 128:(cc + 1) * 128],
                        bass.AP(xfw, tcc * D + cc * 128, [[DCH * T, 128], [1, 128]]),
                        ident[:, :]))
                fence(["pe"], ["dve"])
                R("dve", ve.tensor_copy(
                    bass.AP(q_rot, tcc * 128, [[DCH * T, 128], [T, DCH], [1, 128]]),
                    bass.AP(ps_t, 0, [[1024, 128], [128, DCH], [1, 128]])))
                fence(["dve"], ["pe"])
            if "x1q" in tap_d:
                for tcc in range(TC):
                    dma("pool", "dmp", tap_d["x1q"][bb, mt, tcc],
                        bass.AP(q_rot, tcc * 128, [[DCH * T, 128], [T, DCH], [1, 128]]))

            pe_g = [0] * (NKB + 2)
            pe_u = [0] * (NKB + 2)
            act_s = [0] * (NKB + 2)
            dve_m = [0] * (NKB + 2)
            pe_h = [0] * (NKB + 2)
            dve_e = [0] * (NKB + 2)
            pool_w = [0] * (NKB + 2)
            for ncb in range(NKB + 2):
                par = ncb % 2
                if ncb < NKB:
                    if ncb >= 2:
                        waiton("pool", "pe", pe_g[ncb - 2])
                        waiton("pool", "pe", pe_u[ncb - 2])
                    dma("pool", "dw", wg_sb[:, par, :],
                        bass.AP(wgq_d.tensor, ncb * 128,
                                [[KI, 128], [128 * KI, DCH], [1, 128]]))
                    dma("pool", "dw", wu_sb[:, par, :],
                        bass.AP(wuq_d.tensor, ncb * 128,
                                [[KI, 128], [128 * KI, DCH], [1, 128]]))
                    pool_w[ncb] = dsem["dw"][1]
                    te.wait_ge(dsem["dw"][0], pool_w[ncb])
                    if ncb >= 2:
                        waiton("pe", "act", act_s[ncb - 2])
                    for cc in range(DCH):
                        R("pe", te.matmul(ps_big[:, par * 512:par * 512 + 512],
                                          wg_sb[:, par, cc * 128:(cc + 1) * 128],
                                          q_rot[:, cc, :],
                                          start=(cc == 0), stop=(cc == DCH - 1)))
                    pe_g[ncb] = mark("pe")
                    waiton("act", "pe", pe_g[ncb])
                    if ncb >= 2:
                        waiton("act", "dve", dve_m[ncb - 2])
                    with nodrain():
                        R("act", ac.activation(sg2[:, par, :],
                                               ps_big[:, par * 512:par * 512 + 512],
                                               AF.Silu))
                    act_s[ncb] = mark("act")
                    if ncb >= 2:
                        waiton("pe", "dve", dve_m[ncb - 2])
                    for cc in range(DCH):
                        R("pe", te.matmul(ps_big[:, 1024 + par * 512:1024 + par * 512 + 512],
                                          wu_sb[:, par, cc * 128:(cc + 1) * 128],
                                          q_rot[:, cc, :],
                                          start=(cc == 0), stop=(cc == DCH - 1)))
                    pe_u[ncb] = mark("pe")
                    waiton("dve", "act", act_s[ncb])
                    waiton("dve", "pe", pe_u[ncb])
                    if ncb >= 2:
                        waiton("dve", "pe", pe_h[ncb - 2])
                    with nodrain():
                        R("dve", ve.tensor_tensor(hf2[:, par, :],
                                                  ps_big[:, 1024 + par * 512:1024 + par * 512 + 512],
                                                  sg2[:, par, :], aop.mult))
                    dve_m[ncb] = mark("dve")
                if 0 <= ncb - 1 < NKB:
                    nch = ncb - 1
                    parh = nch % 2
                    waiton("pe", "dve", dve_m[nch])
                    if nch >= 2:
                        waiton("pe", "dve", dve_e[nch - 2])
                    pto = parh * 512
                    for tcc in range(TC):
                        R("pe", te.transpose(ps_t[:, pto + tcc * 128:pto + (tcc + 1) * 128],
                                             hf2[:, parh, tcc * 128:(tcc + 1) * 128],
                                             ident[:, :]))
                    pe_h[nch] = mark("pe")
                    waiton("dve", "pe", pe_h[nch])
                    with nodrain():
                        R("dve", ve.tensor_copy(
                            bass.AP(h_tm, nch * 128, [[TC * KI, 128], [KI, TC], [1, 128]]),
                            bass.AP(ps_t, pto, [[1024, 128], [128, TC], [1, 128]])))
                    dve_e[nch] = mark("dve")
                    if QFOLD and nch % 2 == 1:
                        ng = 2
                        c0 = (nch - ng + 1) * 128
                        nbk = ng * 8
                        kw = ng * 128
                        hview = bass.AP(h_tm, c0,
                                        [[TC * KI, 128], [KI, TC], [16, nbk], [1, 16]])
                        R("dve", ve.tensor_reduce(qbm[:, 0:TC * nbk], hview, AX.X,
                                                  aop.max, apply_absolute_value=True))
                        ve.tensor_scalar(qsc[:, 0:TC * nbk], qbm[:, 0:TC * nbk],
                                         1.0 / 6.0, 1e-12, aop.mult, aop.max)
                        R("dve", ve.reciprocal(qrs[:, 0:TC * nbk], qsc[:, 0:TC * nbk]))
                        rsb4 = bass.AP(qrs, 0, [[KI // 16, 128], [nbk, TC], [1, nbk], [0, 16]])
                        R("dve", ve.tensor_tensor(
                            bass.AP(qc1, 0, [[KI, 128], [kw, TC], [16, nbk], [1, 16]]),
                            hview, rsb4, aop.mult))
                        fence(["dve"], ["act"])
                        stair(TC * kw, None, None, qc1[:, 0:TC * kw], None, final3=(
                            bass.AP(h_tm, c0, [[TC * KI, 128], [KI, TC], [16, nbk], [1, 16]]),
                            bass.AP(qc2, 0, [[KI, 128], [kw, TC], [16, nbk], [1, 16]]),
                            bass.AP(qsc, 0, [[KI // 16, 128], [nbk, TC], [1, nbk], [0, 16]])))
            if "htm" in tap_d:
                tap_wait()
                dma("pool", "dmp", tap_d["htm"][bb, mt], h_tm[:, :, :])

            if mt + 1 < MPB:
                emit_prefetch(bb, mt + 1, stt_m[TC - 1], rot_dve[DCH - 1])
            elif bb + 1 < B_CORE:
                emit_prefetch(bb + 1, 0, stt_m[TC - 1], rot_dve[DCH - 1])

            dve_stt = 0
            wd_guard = {}
            for tp in range(TC // 2):
                if tp == 0:
                    if last.get("dve") is not None:
                        fence(["dve"], ["pe"])
                    else:
                        waiton("pe", "dve", cnt["dve"])
                else:
                    waiton("pe", "dve", dve_stt)
                if TPAR:
                    evq = []
                    for sub in range(2):
                        tcc = tp * 2 + sub
                        for k0 in range(0, NKB, 4):
                            kn = min(4, NKB - k0)
                            half = len(evq) % 2
                            if len(evq) >= TDEPTH:
                                waiton("pe", "dve", evq[-TDEPTH])
                            for kk in range(kn):
                                R("pe", te.transpose(
                                    ps_t[:, half * 512 + kk * 128:half * 512 + (kk + 1) * 128],
                                    h_tm[:, tcc, (k0 + kk) * 128:(k0 + kk + 1) * 128],
                                    ident[:, :]))
                            ptb = mark("pe")
                            waiton("dve", "pe", ptb)
                            with nodrain():
                                R("dve", ve.tensor_copy(
                                    bass.AP(hq_f, k0 * 256 + sub * 128,
                                            [[NKB * 256, 128], [256, kn], [1, 128]]),
                                    bass.AP(ps_t, half * 512, [[1024, 128], [128, kn], [1, 128]])))
                            evq.append(mark("dve"))
                    waiton("pe", "dve", evq[-1])
                else:
                    for sub in range(2):
                        tcc = tp * 2 + sub
                        for k0 in range(0, NKB, 8):
                            kn = min(8, NKB - k0)
                            for kk in range(kn):
                                R("pe", te.transpose(ps_t[:, kk * 128:(kk + 1) * 128],
                                                     h_tm[:, tcc, (k0 + kk) * 128:(k0 + kk + 1) * 128],
                                                     ident[:, :]))
                            fence(["pe"], ["dve"])
                            R("dve", ve.tensor_copy(
                                bass.AP(hq_f, k0 * 256 + sub * 128,
                                        [[NKB * 256, 128], [256, kn], [1, 128]]),
                                bass.AP(ps_t, 0, [[1024, 128], [128, kn], [1, 128]])))
                            fence(["dve"], ["pe"])
                for kk in range(NKB):
                    db = kk % 3
                    if db in wd_guard:
                        waiton("pool", "pe", wd_guard[db])
                    dma("pool", "dw", wd_sb[:, db, :],
                        bass.AP(wdq_d.tensor, kk * 128 * D, [[D, 128], [1, D]]))
                    if PREFETCH:
                        te.wait_ge(dsem["dw"][0], dsem["dw"][1])
                    else:
                        dma_wait("pool", "dw")
                        pool_mark()
                        te.wait_ge(done["pool"], cnt["pool"])
                    for sub in range(2):
                        for nh in range(2):
                            R("pe", te.matmul(
                                ps_big[:, sub * 1024 + nh * 512:sub * 1024 + (nh + 1) * 512],
                                hq_f[:, kk, sub * 128:(sub + 1) * 128],
                                wd_sb[:, db, nh * 512:(nh + 1) * 512],
                                start=(kk == 0), stop=(kk == NKB - 1)))
                    last["pe"].then_inc(done["pe"], 1)
                    cnt["pe"] += 1
                    last["pe"] = None
                    wd_guard[db] = cnt["pe"]
                mm_done = cnt["pe"]
                waiton("dve", "pe", mm_done)
                for sub in range(2):
                    tcc = tp * 2 + sub
                    R("dve", ve.scalar_tensor_tensor(
                        x1[:, tcc, :], ps_big[:, sub * 1024:sub * 1024 + D], 0.0,
                        x1[:, tcc, :], aop.add, aop.add,
                        accum_out=rowsum[:, sub * 2:sub * 2 + 1]))
                dve_stt = mark("dve")
                for sub in range(2):
                    tcc = tp * 2 + sub
                    lo = sub * 6
                    if sub == 0:
                        waiton("act", "dve", dve_stt)
                    R("act", ac.activation(f32a[:, sub * D:(sub + 1) * D],
                                           x1[:, tcc, :], AF.Square,
                                           accum_out=rowsum[:, sub * 2 + 1:sub * 2 + 2]))
                    fence(["act"], ["dve"])
                    ve.tensor_scalar(lnsc[:, lo:lo + 1], rowsum[:, sub * 2:sub * 2 + 1],
                                     1.0 / D, None, aop.mult)
                    ve.tensor_scalar(lnsc[:, lo + 1:lo + 2],
                                     rowsum[:, sub * 2 + 1:sub * 2 + 2],
                                     1.0 / D, None, aop.mult)
                    ve.tensor_tensor(lnsc[:, lo + 2:lo + 3], lnsc[:, lo:lo + 1],
                                     lnsc[:, lo:lo + 1], aop.mult)
                    R("dve", ve.tensor_tensor(lnsc[:, lo + 1:lo + 2], lnsc[:, lo + 1:lo + 2],
                                              lnsc[:, lo + 2:lo + 3], aop.subtract))
                    fence(["dve"], ["act"])
                    ac.activation(lnsc[:, lo + 3:lo + 4], lnsc[:, lo + 1:lo + 2], AF.Ln,
                                  bias=1e-5, scale=1.0)
                    R("act", ac.activation(lnsc[:, lo + 4:lo + 5], lnsc[:, lo + 3:lo + 4],
                                           AF.Exp, bias=0.0, scale=-0.5))
                    fence(["act"], ["dve"])
                    ve.tensor_tensor(lnsc[:, lo + 5:lo + 6], lnsc[:, lo:lo + 1],
                                     lnsc[:, lo + 4:lo + 5], aop.mult)
                    R("dve", ve.tensor_scalar(lnsc[:, lo + 5:lo + 6], lnsc[:, lo + 5:lo + 6],
                                              -1.0, None, aop.mult))
                    fence(["dve"], ["act"])
                    R("act", ac.activation(x1[:, tcc, :], x1[:, tcc, :], AF.Identity,
                                           bias=lnsc[:, lo + 5:lo + 6],
                                           scale=lnsc[:, lo + 4:lo + 5]))
                    apm = mark("act")
                    if SPSTORE:
                        waiton("sp", "act", apm)
                        dma("sp", "dout",
                            out_d[bb, t0 + tcc * 128:t0 + (tcc + 1) * 128, :],
                            x1[:, tcc, :])
                    else:
                        waiton("pool", "act", apm)
                        dma("pool", "dmp",
                            out_d[bb, t0 + tcc * 128:t0 + (tcc + 1) * 128, :],
                            x1[:, tcc, :])
            if SPSTORE:
                dma_wait("sp", "dout")
                sp_mark()
                waiton("act", "sp", cnt["sp"])
                waiton("dve", "sp", cnt["sp"])
            else:
                dma_wait("pool", "dmp")
                pool_mark()
                waiton("act", "pool", cnt["pool"])
                waiton("dve", "pool", cnt["pool"])

            ve.wait_ge(done["pe"], cnt["pe"])
            ve.tensor_copy(vh_km[:, 0, :], vh_km[:, WPM, :])
            R("dve", ve.tensor_copy(
                bass.AP(vh_f, 0, [[DCH * (WPM + 1) * WIN, 128],
                                  [(WPM + 1) * WIN, DCH], [1, WIN]]),
                bass.AP(vh_f, WPM * WIN, [[DCH * (WPM + 1) * WIN, 128],
                                          [(WPM + 1) * WIN, DCH], [1, WIN]])))
            fence(["dve"], ["pe", "act", "pool", "sp"])

    if SPSTORE:
        dma_wait("sp", "dout")
    dma_wait("pool", "dmp")
    po.sem_inc(done["pool"], 1)

    for cm in reversed(ctxs):
        cm.__exit__(None, None, None)
    return nc


def kernel(L0_attn_x, L0_attn_Wq, L0_attn_Wkv, L0_attn_Wproj, L0_attn_Wout,
           L0_attn_bout, L0_attn_ln_local_g, L0_attn_ln_local_b,
           L0_attn_ln_global_g, L0_attn_ln_global_b,
           L0_ffn_w_gate, L0_ffn_w_up, L0_ffn_w_down):
    x = np.ascontiguousarray(np.asarray(L0_attn_x, dtype=np.float32))
    lng = np.ascontiguousarray(np.stack(
        [np.asarray(L0_attn_ln_local_g, np.float32),
         np.asarray(L0_attn_ln_local_b, np.float32),
         np.asarray(L0_attn_ln_global_g, np.float32),
         np.asarray(L0_attn_ln_global_b, np.float32)], axis=0))
    common = {
        "wq": np.ascontiguousarray(np.asarray(L0_attn_Wq, np.float32)),
        "wkv": np.ascontiguousarray(np.asarray(L0_attn_Wkv, np.float32)),
        "wproj": np.ascontiguousarray(np.asarray(L0_attn_Wproj, np.float32)[:, 0]),
        "wout": np.ascontiguousarray(np.asarray(L0_attn_Wout, np.float32)),
        "bout": np.ascontiguousarray(np.asarray(L0_attn_bout, np.float32)),
        "lng": lng,
        "wgate": np.ascontiguousarray(np.asarray(L0_ffn_w_gate, np.float32)),
        "wup": np.ascontiguousarray(np.asarray(L0_ffn_w_up, np.float32)),
        "wdown": np.ascontiguousarray(np.asarray(L0_ffn_w_down, np.float32)),
    }
    n_cores = 8
    bpc = x.shape[0] // n_cores
    nc = build_program(B_CORE=bpc, N=x.shape[1])
    in_maps = [dict(common, x=np.ascontiguousarray(x[i * bpc:(i + 1) * bpc]))
               for i in range(n_cores)]
    res = run_bass_kernel_spmd(nc, in_maps, core_ids=list(range(n_cores)))
    return np.concatenate([np.asarray(r["out"]) for r in res.results], axis=0)
